# version 07 (final)
# speedup vs baseline: 1.1307x; 1.1018x over previous
import math
from contextlib import ExitStack

import numpy as np
import concourse.bass as bass
import concourse.mybir as mybir
from concourse.bass_utils import run_bass_kernel_spmd

F32 = mybir.dt.float32
BF16 = mybir.dt.bfloat16
I32 = mybir.dt.int32
AF = mybir.ActivationFunctionType
ALU = mybir.AluOpType
AX = mybir.AxisListType

ENGINES = ("sync", "scalar", "vector", "tensor", "gpsimd")

B, T, D = 16, 2048, 1024
NCORES = 8


class Prog:

    def __init__(self, nc):
        self.nc = nc
        self.stack = ExitStack()
        self.q = {e: [] for e in ENGINES}
        self.sem = {}
        self.cnt = {}
        for e in ENGINES:
            self.sem["e_" + e] = self.stack.enter_context(nc.semaphore("se_" + e))
            self.cnt["e_" + e] = 0
        for e in ("sync", "gpsimd", "scalar", "vector"):
            self.sem["d_" + e] = self.stack.enter_context(nc.semaphore("sd_" + e))
            self.cnt["d_" + e] = 0
        self.waited = {e: {} for e in ENGINES}
        self.tail = None
        self._uid = 0

    def sb(self, name, shape, dtype, scope=None):
        st = scope if scope is not None else self.stack
        return st.enter_context(
            self.nc.sbuf_tensor(f"{name}_{self.uid()}", list(shape), dtype))

    def ps(self, name, shape, dtype=F32, scope=None):
        st = scope if scope is not None else self.stack
        return st.enter_context(
            self.nc.psum_tensor(f"{name}_{self.uid()}", list(shape), dtype))

    def uid(self):
        self._uid += 1
        return self._uid

    def _emit_wait(self, eng, engine_name, ev, hard=False):
        kind, sname, val = ev
        if val <= 0:
            return
        if kind == "eng" and sname == "e_" + engine_name and not hard:
            return
        w = self.waited[engine_name]
        if w.get(sname, 0) >= val:
            return
        w[sname] = val
        eng.wait_ge(self.sem[sname], val)

    def ph(self, engine, body, waits=None, hard=False):
        if waits is None:
            waits = [self.tail] if self.tail is not None else []
        waits = [w for w in waits if w is not None]
        sname = "e_" + engine
        self.cnt[sname] += 1
        val = self.cnt[sname]
        sem = self.sem[sname]

        def run(eng, waits=tuple(waits), body=body, sem=sem, val=val,
                engine=engine, hard=hard):
            for ev in waits:
                self._emit_wait(eng, engine, ev, hard=hard)
            last = body(eng)
            assert last is not None, "phase body must return its last instruction"
            last.then_inc(sem, 1)

        self.q[engine].append(run)
        ev = ("eng", sname, val)
        self.tail = ev
        return ev

    def dma(self, engine, transfers, waits=None, chain=True, accum=False):
        if waits is None:
            waits = [self.tail] if (self.tail is not None and chain) else []
        waits = [w for w in waits if w is not None]
        sname = "d_" + engine
        n = len(transfers)
        assert n > 0
        self.cnt[sname] += 16 * n
        val = self.cnt[sname]
        sem = self.sem[sname]

        def run(eng, waits=tuple(waits), transfers=tuple(transfers), sem=sem,
                engine=engine, accum=accum):
            for ev in waits:
                self._emit_wait(eng, engine, ev)
            for dst, src in transfers:
                if accum:
                    eng.dma_start(out=dst, in_=src,
                                  accum_op=ALU.add).then_inc(sem, 16)
                else:
                    eng.dma_start(out=dst, in_=src).then_inc(sem, 16)

        self.q[engine].append(run)
        ev = ("dma", sname, val)
        if chain:
            self.tail = ev
        return ev

    def finalize(self):
        nc = self.nc
        with nc.Block() as block:
            @block.sync
            def _(eng):
                for f in self.q["sync"]:
                    f(eng)

            @block.scalar
            def _(eng):
                for f in self.q["scalar"]:
                    f(eng)

            @block.vector
            def _(eng):
                for f in self.q["vector"]:
                    f(eng)

            @block.tensor
            def _(eng):
                for f in self.q["tensor"]:
                    f(eng)

            @block.gpsimd
            def _(eng):
                for f in self.q["gpsimd"]:
                    f(eng)

    def close(self):
        self.stack.close()



class Consts:
    def __init__(self, p):
        nc = p.nc
        self.ones_col = p.sb("ones_col", [128, 1], BF16)
        self.ones_row = p.sb("ones_row", [1, 128], BF16)
        self.ident = p.sb("ident", [128, 128], BF16)
        self.m_ut = p.sb("m_ut", [128, 128], BF16)
        self.m_lt = p.sb("m_lt", [128, 128], BF16)
        self.m_utS = p.sb("m_utS", [128, 128], BF16)
        self.m_ltS = p.sb("m_ltS", [128, 128], BF16)

        def init(eng):
            eng.memset(self.ones_col[:], 1.0)
            eng.memset(self.ones_row[:], 1.0)
            eng.memset(self.ident[:], 1.0)
            eng.memset(self.m_ut[:], 1.0)
            eng.memset(self.m_lt[:], 1.0)
            eng.affine_select(self.ident[:], self.ident[:],
                              pattern=[[-1, 128]], base=0, channel_multiplier=1,
                              compare_op=ALU.is_equal, fill=0.0)
            eng.affine_select(self.m_ut[:], self.m_ut[:],
                              pattern=[[1, 128]], base=0, channel_multiplier=-1,
                              compare_op=ALU.is_ge, fill=0.0)
            eng.affine_select(self.m_lt[:], self.m_lt[:],
                              pattern=[[-1, 128]], base=0,
                              channel_multiplier=1,
                              compare_op=ALU.is_ge, fill=0.0)
            eng.memset(self.m_utS[:], 1.0)
            eng.memset(self.m_ltS[:], 1.0)
            eng.affine_select(self.m_utS[:], self.m_utS[:],
                              pattern=[[1, 128]], base=-1, channel_multiplier=-1,
                              compare_op=ALU.is_ge, fill=0.0)
            last = eng.affine_select(self.m_ltS[:], self.m_ltS[:],
                                     pattern=[[-1, 128]], base=-1,
                                     channel_multiplier=1,
                                     compare_op=ALU.is_ge, fill=0.0)
            return last
        p.ph("gpsimd", init)



def emit_ln(p, cc, x_src, xn_dst, bufs, W=512, eps=1e-5, w_row=None,
            src_bf16=False, dst_hbm=False, b_col=None):
    x32, xb, sq = bufs["x32"], bufs["xb"], bufs["sq"]
    tmp32 = bufs["tmp32"]
    vrow_bf = bufs["vrow_bf"]
    ps_a, ps_b = bufs["ps_a"], bufs["ps_b"]
    nt = T // W

    def load(c, t0):
        if src_bf16:
            p.dma("sync", [(xb[:, 0:W], x_src[:, c * T + t0:c * T + t0 + W])])
        else:
            p.dma("sync", [(x32[:, 0:W],
                            x_src[c * 128:(c + 1) * 128, t0:t0 + W])])
            p.ph("vector", lambda eng: eng.tensor_copy(xb[:, 0:W], x32[:, 0:W]))

    for tc in range(nt):
        t0 = tc * W
        for c in range(8):
            load(c, t0)
            p.ph("scalar", lambda eng: eng.activation(sq[:, 0:W], xb[:, 0:W],
                                                      AF.Square))

            def mm(eng, c=c):
                eng.matmul(ps_a[0:1, 0:W], cc.ones_col[:], xb[:, 0:W],
                           start=(c == 0), stop=(c == 7))
                return eng.matmul(ps_b[0:1, 0:W], cc.ones_col[:], sq[:, 0:W],
                                  start=(c == 0), stop=(c == 7))
            p.ph("tensor", mm)

        emit_ln_tail(p, cc, bufs, W, eps, w_row)

        for c in range(8):
            load(c, t0)
            if w_row is not None:
                p.ph("tensor", lambda eng, c=c: eng.matmul(
                    ps_b[:, 0:W], w_row[0:1, c * 128:(c + 1) * 128],
                    vrow_bf[0:1, 0:W], start=True, stop=True))
            if dst_hbm:
                def norm(eng, c=c, t0=t0):
                    eng.tensor_tensor(tmp32[:, 0:W], xb[:, 0:W], ps_a[:, 0:W],
                                      ALU.subtract)
                    if b_col is None:
                        return eng.tensor_tensor(xb[:, 0:W], tmp32[:, 0:W],
                                                 ps_b[:, 0:W], ALU.mult)
                    eng.tensor_tensor(tmp32[:, 0:W], tmp32[:, 0:W],
                                      ps_b[:, 0:W], ALU.mult)
                    return eng.tensor_scalar(xb[:, 0:W], tmp32[:, 0:W],
                                             b_col[:, c:c + 1], None, ALU.add)
                p.ph("vector", norm)
                p.dma("sync", [(xn_dst[:, c * T + t0:c * T + t0 + W],
                                xb[:, 0:W])])
            else:
                def norm(eng, c=c, t0=t0):
                    eng.tensor_tensor(tmp32[:, 0:W], xb[:, 0:W], ps_a[:, 0:W],
                                      ALU.subtract)
                    if b_col is None:
                        return eng.tensor_tensor(
                            xn_dst[:, c * T + t0:c * T + t0 + W],
                            tmp32[:, 0:W], ps_b[:, 0:W], ALU.mult)
                    eng.tensor_tensor(tmp32[:, 0:W], tmp32[:, 0:W],
                                      ps_b[:, 0:W], ALU.mult)
                    return eng.tensor_scalar(
                        xn_dst[:, c * T + t0:c * T + t0 + W],
                        tmp32[:, 0:W], b_col[:, c:c + 1], None, ALU.add)
                p.ph("vector", norm)


def emit_ln_tail(p, cc, bufs, W, eps, w_row):
    tmp32 = bufs["tmp32"]
    mrow, vrow = bufs["mrow"], bufs["vrow"]
    mrow_bf, vrow_bf = bufs["mrow_bf"], bufs["vrow_bf"]
    ps_a, ps_b = bufs["ps_a"], bufs["ps_b"]
    inv = 1.0 / D

    def stats(eng):
        eng.tensor_scalar(mrow[0:1, 0:W], ps_a[0:1, 0:W], inv, None, ALU.mult)
        eng.tensor_scalar(vrow[0:1, 0:W], ps_b[0:1, 0:W], inv, None, ALU.mult)
        eng.tensor_tensor(tmp32[0:1, 0:W], mrow[0:1, 0:W], mrow[0:1, 0:W],
                          ALU.mult)
        eng.tensor_tensor(vrow[0:1, 0:W], vrow[0:1, 0:W], tmp32[0:1, 0:W],
                          ALU.subtract)
        return eng.tensor_scalar(vrow[0:1, 0:W], vrow[0:1, 0:W], eps, None,
                                 ALU.add)
    p.ph("vector", stats)
    p.ph("scalar", lambda eng: eng.activation(vrow[0:1, 0:W], vrow[0:1, 0:W],
                                              AF.Sqrt))

    def recip(eng):
        eng.reciprocal(vrow[0:1, 0:W], vrow[0:1, 0:W])
        eng.tensor_copy(mrow_bf[0:1, 0:W], mrow[0:1, 0:W])
        return eng.tensor_copy(vrow_bf[0:1, 0:W], vrow[0:1, 0:W])
    p.ph("vector", recip)

    def bcast(eng):
        last = eng.matmul(ps_a[:, 0:W], cc.ones_row[0:1, :], mrow_bf[0:1, 0:W],
                          start=True, stop=True)
        if w_row is None:
            last = eng.matmul(ps_b[:, 0:W], cc.ones_row[0:1, :],
                              vrow_bf[0:1, 0:W], start=True, stop=True)
        return last
    p.ph("tensor", bcast)


def emit_ln_sbbf(p, cc, src_bf, xn_bf, bufs, W=512, eps=1e-5, w_row=None):
    sq = bufs["sq"]
    tmp32 = bufs["tmp32"]
    vrow_bf = bufs["vrow_bf"]
    ps_a, ps_b = bufs["ps_a"], bufs["ps_b"]
    nt = T // W
    for tc in range(nt):
        t0 = tc * W
        for c in range(8):
            p.ph("scalar", lambda eng, c=c, t0=t0: eng.activation(
                sq[:, 0:W], src_bf[:, c * T + t0:c * T + t0 + W], AF.Square))

            def mm(eng, c=c, t0=t0):
                eng.matmul(ps_a[0:1, 0:W], cc.ones_col[:],
                           src_bf[:, c * T + t0:c * T + t0 + W],
                           start=(c == 0), stop=(c == 7))
                return eng.matmul(ps_b[0:1, 0:W], cc.ones_col[:], sq[:, 0:W],
                                  start=(c == 0), stop=(c == 7))
            p.ph("tensor", mm)

        emit_ln_tail(p, cc, bufs, W, eps, w_row)

        for c in range(8):
            if w_row is not None:
                p.ph("tensor", lambda eng, c=c: eng.matmul(
                    ps_b[:, 0:W], w_row[0:1, c * 128:(c + 1) * 128],
                    vrow_bf[0:1, 0:W], start=True, stop=True))

            def norm(eng, c=c, t0=t0):
                eng.tensor_tensor(tmp32[:, 0:W],
                                  src_bf[:, c * T + t0:c * T + t0 + W],
                                  ps_a[:, 0:W], ALU.subtract)
                return eng.tensor_tensor(xn_bf[:, c * T + t0:c * T + t0 + W],
                                         tmp32[:, 0:W], ps_b[:, 0:W], ALU.mult)
            p.ph("vector", norm)



def emit_attn_core(p, cc, st, heads, causal, window, seqlen=T, qw=512):
    qfm, kfm, vtm = st["qfm"], st["kfm"], st["vtm"]
    av_fm, negb, PT = st["av_fm"], st["negb"], st["PT"]
    avtm, rcp_col = st["avtm"], st["rcp_col"]
    ps_score, ps_b, ps_tr = st["ps_score"], st["ps_b"], st["ps_tr"]
    nkt = seqlen // 128
    nqtc = seqlen // qw
    nsub = qw // 128
    nslot = PT.shape[1] // qw

    def kt_allowed_qt(kt, qt):
        r = (qt - kt) * 128
        if causal and r < 0:
            return False
        if window is not None and r > window:
            return False
        return True

    for h in range(heads):
        hb = (h % 2) * 64
        pcol = (h // 2) * seqlen
        st["emit_negb"](p, h)
        for qtc in range(nqtc):
            q0 = qtc * qw
            kts = [kt for kt in range(nkt)
                   if any(kt_allowed_qt(kt, (q0 // 128) + s) for s in range(nsub))]
            assert len(kts) <= nslot, (len(kts), nslot)
            slot_of = {kt: i for i, kt in enumerate(kts)}
            for base in range(0, len(kts), 4):
                batch = tuple(kts[base:base + 4])

                def mm_scores(eng, batch=batch, hb=hb, pcol=pcol, q0=q0):
                    last = None
                    for i, kt in enumerate(batch):
                        pslice = ps_score[:, i * 512:i * 512 + qw]
                        eng.matmul(pslice,
                                   kfm[hb:hb + 64, pcol + kt * 128:pcol + kt * 128 + 128],
                                   qfm[hb:hb + 64, pcol + q0:pcol + q0 + qw],
                                   start=True, stop=False)
                        last = eng.matmul(pslice, cc.ones_row[0:1, :],
                                          negb[0:1, q0:q0 + qw],
                                          start=False, stop=True)
                    return last
                p.ph("tensor", mm_scores)

                def expp(eng, batch=batch, slot_of=dict(slot_of)):
                    last = None
                    for i, kt in enumerate(batch):
                        sl = slot_of[kt]
                        last = eng.activation(PT[:, sl * qw:(sl + 1) * qw],
                                              ps_score[:, i * 512:i * 512 + qw],
                                              AF.Exp)
                    return last
                p.ph("scalar", expp)

            mask_jobs = []
            for kt in kts:
                for s4 in range(nsub):
                    qt = q0 // 128 + s4
                    r = (qt - kt) * 128
                    if causal and r == 0:
                        mask_jobs.append((kt, s4, cc.m_ut))
                    if window is not None and r == window:
                        mask_jobs.append((kt, s4, cc.m_lt))
            if mask_jobs:
                def masks(eng, jobs=tuple(mask_jobs), slot_of=dict(slot_of)):
                    last = None
                    for kt, s4, m in jobs:
                        sl = slot_of[kt]
                        ap = PT[:, sl * qw + s4 * 128: sl * qw + s4 * 128 + 128]
                        last = eng.tensor_tensor(ap, ap, m[:], ALU.mult)
                    return last
                p.ph("vector", masks)

            def av_mm(eng, kts=tuple(kts), q0=q0, h=h, slot_of=dict(slot_of)):
                last = None
                for s4 in range(nsub):
                    qt = q0 // 128 + s4
                    use = [kt for kt in kts if kt_allowed_qt(kt, qt)]
                    assert use, (kts, qt)
                    po = ps_b[:, s4 * 128:s4 * 128 + 65]
                    for i, kt in enumerate(use):
                        sl = slot_of[kt]
                        last = eng.matmul(
                            po,
                            PT[:, sl * qw + s4 * 128: sl * qw + s4 * 128 + 128],
                            vtm[:, h, kt, :],
                            start=(i == 0), stop=(i == len(use) - 1))
                return last
            p.ph("tensor", av_mm)

            def av_rcp(eng):
                last = None
                for s4 in range(nsub):
                    last = eng.reciprocal(rcp_col[:, s4:s4 + 1],
                                          ps_b[:, s4 * 128 + 64:s4 * 128 + 65])
                return last
            p.ph("vector", av_rcp)

            def av_norm(eng):
                last = None
                for s4 in range(nsub):
                    last = eng.tensor_scalar(avtm[:, s4 * 64:(s4 + 1) * 64],
                                             ps_b[:, s4 * 128:s4 * 128 + 64],
                                             rcp_col[:, s4:s4 + 1], None,
                                             ALU.mult)
                return last
            p.ph("vector", av_norm, hard=True)

            def av_tr(eng):
                last = None
                for s4 in range(nsub):
                    last = eng.transpose(ps_tr[0:64, s4 * 128:(s4 + 1) * 128],
                                         avtm[:, s4 * 64:(s4 + 1) * 64], cc.ident[:])
                return last
            p.ph("tensor", av_tr)

            def av_ev(eng, hb=hb, pcol=pcol, q0=q0):
                last = None
                for s4 in range(nsub):
                    dst = av_fm[hb:hb + 64,
                                pcol + q0 + s4 * 128:pcol + q0 + s4 * 128 + 128]
                    last = eng.tensor_copy(dst, ps_tr[0:64, s4 * 128:(s4 + 1) * 128])
                return last
            p.ph("vector", av_ev)


def emit_top2(p, prow, ptmp, mcol, wgt_dst, n, renorm=True, ind_dst=None):
    p.ph("vector", lambda eng: eng.reduce_max(mcol[:, 0:1], prow[:, 0:n],
                                              axis=AX.X), hard=True)

    p.ph("vector", lambda eng: eng.tensor_scalar(
        ptmp[:, 0:n], prow[:, 0:n], mcol[:, 0:1], None, ALU.is_equal),
        hard=True)
    p.ph("vector", lambda eng: eng.scalar_tensor_tensor(
        ptmp[:, 0:n], ptmp[:, 0:n], -1e30, prow[:, 0:n], ALU.mult, ALU.add),
        hard=True)
    p.ph("vector", lambda eng: eng.reduce_max(mcol[:, 1:2], ptmp[:, 0:n],
                                              axis=AX.X), hard=True)

    def s2(eng):
        eng.tensor_scalar(ptmp[:, 0:n], prow[:, 0:n], mcol[:, 1:2], None,
                          ALU.is_ge)
        return eng.tensor_tensor(mcol[:, 2:3], mcol[:, 0:1], mcol[:, 1:2],
                                 ALU.add)
    p.ph("vector", s2, hard=True)
    if ind_dst is not None:
        p.ph("vector", lambda eng: eng.tensor_copy(ind_dst, ptmp[:, 0:n]),
             hard=True)
    if renorm:
        p.ph("vector", lambda eng: eng.reciprocal(mcol[:, 3:4], mcol[:, 2:3]),
             hard=True)
        p.ph("vector", lambda eng: eng.tensor_tensor(
            ptmp[:, 0:n], ptmp[:, 0:n], prow[:, 0:n], ALU.mult), hard=True)
        p.ph("vector", lambda eng: eng.tensor_scalar(
            wgt_dst, ptmp[:, 0:n], mcol[:, 3:4], None, ALU.mult), hard=True)
    else:
        p.ph("vector", lambda eng: eng.tensor_tensor(
            wgt_dst, ptmp[:, 0:n], prow[:, 0:n], ALU.mult), hard=True)


def emit_negb_normbound(p, cc, st, h, seqlen=T, scale=1.0):
    qfm, kfm, negb = st["qfm"], st["kfm"], st["negb"]
    sq, mrow, m8 = st["sq"], st["mrow"], st["m8"]
    ps_misc = st["ps_misc"]
    hb = (h % 2) * 64
    pcol = (h // 2) * seqlen
    nck = seqlen // 512
    for c in range(nck):
        p.ph("scalar", lambda eng, c=c: eng.activation(
            sq[0:64, 0:512], kfm[hb:hb + 64, pcol + c * 512:pcol + (c + 1) * 512],
            AF.Square))
        p.ph("tensor", lambda eng: eng.matmul(
            ps_misc[0:1, 0:512], cc.ones_col[0:64, :], sq[0:64, 0:512],
            start=True, stop=True))

        if c == 0:
            p.ph("vector", lambda eng: eng.reduce_max(
                m8[0:1, 0:1], ps_misc[0:1, 0:512], axis=AX.X))
        else:
            p.ph("vector", lambda eng: eng.reduce_max(
                m8[0:1, 1:2], ps_misc[0:1, 0:512], axis=AX.X))
            p.ph("vector", lambda eng: eng.tensor_tensor(
                m8[0:1, 0:1], m8[0:1, 0:1], m8[0:1, 1:2], ALU.max), hard=True)

    def fin(eng):
        return eng.activation(m8[0:1, 0:1], m8[0:1, 0:1], AF.Sqrt)
    p.ph("scalar", fin)
    p.ph("vector", lambda eng: eng.tensor_scalar(m8[0:1, 0:1], m8[0:1, 0:1],
                                                 -scale, None, ALU.mult))
    for c in range(nck):
        p.ph("scalar", lambda eng, c=c: eng.activation(
            sq[0:64, 0:512], qfm[hb:hb + 64, pcol + c * 512:pcol + (c + 1) * 512],
            AF.Square))
        p.ph("tensor", lambda eng: eng.matmul(
            ps_misc[0:1, 0:512], cc.ones_col[0:64, :], sq[0:64, 0:512],
            start=True, stop=True))
        p.ph("scalar", lambda eng: eng.activation(mrow[0:1, 0:512],
                                                  ps_misc[0:1, 0:512], AF.Sqrt))
        p.ph("vector", lambda eng, c=c: eng.tensor_scalar(
            negb[0:1, c * 512:(c + 1) * 512], mrow[0:1, 0:512],
            m8[0:1, 0:1], None, ALU.mult))



def build_layer0(nc):
    t = {}
    t["x"] = nc.dram_tensor("x", [2, D, T], F32, kind="ExternalInput").ap()
    t["wqkv_t"] = nc.dram_tensor("wqkv_t", [D, 3 * D], F32, kind="ExternalInput").ap()
    t["wproj_t"] = nc.dram_tensor("wproj_t", [D, D], F32, kind="ExternalInput").ap()
    t["bproj_col"] = nc.dram_tensor("bproj_col", [128, 8], F32, kind="ExternalInput").ap()
    t["ropes"] = nc.dram_tensor("ropes", [4, T, 512], BF16, kind="ExternalInput").ap()
    t["wg_t"] = nc.dram_tensor("wg_t", [D, 3 * D], I32, kind="ExternalInput").ap()
    t["wu_t"] = nc.dram_tensor("wu_t", [D, 3 * D], I32, kind="ExternalInput").ap()
    t["wd_t"] = nc.dram_tensor("wd_t", [3 * D, D], I32, kind="ExternalInput").ap()
    t["xa"] = nc.dram_tensor("xa", [2, D, T], F32, kind="Internal").ap()
    t["y"] = nc.dram_tensor("y", [2, D, T], F32, kind="ExternalOutput").ap()

    p = Prog(nc)
    cc = Consts(p)
    W = 512
    bufs = {
        "x32": p.sb("x32", [128, W], F32),
        "xev": p.sb("xev", [128, W], F32),
        "xb": p.sb("xb", [128, W], BF16),
        "sq": p.sb("sq", [128, W], BF16),
        "tmp32": p.sb("tmp32", [128, W], F32),
        "tmpo": p.sb("tmpo", [128, 256], F32),
        "mrow": p.sb("mrow", [1, W], F32),
        "vrow": p.sb("vrow", [1, W], F32),
        "mrow_bf": p.sb("mrow_bf", [1, W], BF16),
        "vrow_bf": p.sb("vrow_bf", [1, W], BF16),
        "ps_a": p.ps("ps_a", [128, 512], F32),
        "ps_b": p.ps("ps_b", [128, 512], F32),
        "ps_misc": p.ps("ps_misc", [128, 512], F32),
    }
    ps_score = p.ps("ps_score", [128, 2048], F32)
    ps_tr = p.ps("ps_tr", [128, 512], BF16)

    xn_bf = p.sb("xn_bf", [128, 8 * T], BF16)
    bias_col = p.sb("bias_col", [128, 8], F32)
    p.dma("sync", [(bias_col[:], t["bproj_col"])])

    st = {
        "PT": p.sb("PT", [128, 8 * 512], BF16),
        "negb": p.sb("negb", [1, T], BF16),
        "avtm": p.sb("avtm", [128, 256], BF16),
        "rcp_col": p.sb("rcp_col", [128, 4], F32),
        "m8": p.sb("m8", [1, 2], F32),
        "ps_score": ps_score,
        "ps_b": bufs["ps_b"],
        "ps_tr": ps_tr,
        "ps_misc": bufs["ps_misc"],
        "sq": bufs["sq"],
        "mrow": bufs["mrow"],
    }
    st["emit_negb"] = lambda p_, h: emit_negb_normbound(p_, cc, st, h)

    for s in range(2):
        with ExitStack() as attn_scope:
            qfm = p.sb("qfm", [128, 8 * T], BF16, scope=attn_scope)
            kfm = p.sb("kfm", [128, 8 * T], BF16, scope=attn_scope)
            vtm = p.sb("vtm", [128, 16, 16, 65], BF16, scope=attn_scope)
            av_fm = p.sb("av_fm", [128, 8 * T], BF16, scope=attn_scope)
            wq_sb = p.sb("wq_sb", [128, 8 * 512], BF16, scope=attn_scope)
            ropes_sb = p.sb("ropes_sb", [128, 2 * 512], BF16, scope=attn_scope)
            qtm = p.sb("qtm", [128, 512], BF16, scope=attn_scope)
            st.update(qfm=qfm, kfm=kfm, vtm=vtm, av_fm=av_fm)

            emit_ln(p, cc, t["x"][s], xn_bf, bufs)

            p.ph("vector", lambda eng: eng.memset(vtm[:, :, :, 64:65], 1.0))

            for grp in range(6):
                p.dma("gpsimd",
                      [(wq_sb[:, c * 512:(c + 1) * 512],
                        t["wqkv_t"][c * 128:(c + 1) * 128,
                                    grp * 512:(grp + 1) * 512])
                       for c in range(8)])
                is_q = grp < 2
                is_k = 2 <= grp < 4
                for tt in range(16):
                    t0 = tt * 128

                    def mm_qkv(eng, grp=grp, t0=t0):
                        last = None
                        for c in range(8):
                            last = eng.matmul(
                                bufs["ps_a"][:, 0:512],
                                xn_bf[:, c * T + t0:c * T + t0 + 128],
                                wq_sb[:, c * 512:(c + 1) * 512],
                                start=(c == 0), stop=(c == 7))
                        return last
                    p.ph("tensor", mm_qkv)

                    if is_q or is_k:
                        ri = 2 if is_q else 0
                        p.dma("sync",
                              [(ropes_sb[:, 0:512],
                                t["ropes"][ri, t0:t0 + 128, :]),
                               (ropes_sb[:, 512:1024],
                                t["ropes"][ri + 1, t0:t0 + 128, :])])

                        def rope(eng):
                            ps = bufs["ps_a"]
                            pe = ps[:, 0:512].rearrange("p (x two) -> p x two", two=2)
                            ce = ropes_sb[:, 0:512].rearrange("p (x two) -> p x two", two=2)
                            se = ropes_sb[:, 512:1024].rearrange("p (x two) -> p x two", two=2)
                            qe = qtm[:, 0:512].rearrange("p (x two) -> p x two", two=2)
                            te = bufs["tmp32"][:, 0:256]
                            to = bufs["tmpo"][:, 0:256]
                            eng.tensor_tensor(te, pe[:, :, 0], ce[:, :, 0], ALU.mult)
                            eng.tensor_tensor(to, pe[:, :, 1], se[:, :, 0], ALU.mult)
                            eng.tensor_tensor(qe[:, :, 0], te, to, ALU.subtract)
                            eng.tensor_tensor(te, pe[:, :, 1], ce[:, :, 1], ALU.mult)
                            eng.tensor_tensor(to, pe[:, :, 0], se[:, :, 1], ALU.mult)
                            return eng.tensor_tensor(qe[:, :, 1], te, to, ALU.add)
                        p.ph("vector", rope)

                        def tr(eng):
                            last = None
                            for j in range(4):
                                last = eng.transpose(
                                    ps_tr[:, j * 128:(j + 1) * 128],
                                    qtm[:, j * 128:(j + 1) * 128], cc.ident[:])
                            return last
                        p.ph("tensor", tr)

                        dstt = qfm if is_q else kfm

                        def ev(eng, grp=grp, t0=t0, dstt=dstt):
                            last = None
                            for j in range(4):
                                pair = (grp % 2) * 4 + j
                                for half in range(2):
                                    last = eng.tensor_copy(
                                        dstt[half * 64:half * 64 + 64,
                                             pair * T + t0:pair * T + t0 + 128],
                                        ps_tr[half * 64:half * 64 + 64,
                                              j * 128:(j + 1) * 128])
                            return last
                        p.ph("vector", ev)
                    else:
                        def evv(eng, grp=grp, tt=tt):
                            last = None
                            for j in range(8):
                                h = (grp - 4) * 8 + j
                                last = eng.tensor_copy(
                                    vtm[:, h, tt, 0:64],
                                    bufs["ps_a"][:, j * 64:(j + 1) * 64])
                            return last
                        p.ph("vector", evv)

            emit_attn_core(p, cc, st, heads=16, causal=True, window=512)

            wproj_sb = p.sb("wproj_sb", [128, 8 * 128], BF16, scope=attn_scope)
            for f in range(8):
                p.dma("gpsimd",
                      [(wproj_sb[:].rearrange("p (c g) -> p c g", c=8),
                        t["wproj_t"][:, f * 128:(f + 1) * 128].rearrange(
                            "(c p) g -> p c g", p=128))])
                for tc in range(4):
                    t0 = tc * 512

                    def mm_o(eng, t0=t0):
                        last = None
                        for c in range(8):
                            last = eng.matmul(
                                bufs["ps_a"][:, 0:512],
                                wproj_sb[:, c * 128:(c + 1) * 128],
                                av_fm[:, c * T + t0:c * T + t0 + 512],
                                start=(c == 0), stop=(c == 7))
                        return last
                    p.ph("tensor", mm_o)
                    p.dma("sync", [(bufs["x32"][:, 0:512],
                                    t["x"][s, f * 128:(f + 1) * 128, t0:t0 + 512])])

                    def evac(eng, f=f):
                        return eng.scalar_tensor_tensor(
                            bufs["xev"][:, 0:512], bufs["ps_a"][:, 0:512],
                            bias_col[:, f:f + 1], bufs["x32"][:, 0:512],
                            ALU.add, ALU.add)
                    p.ph("vector", evac)
                    p.dma("sync", [(t["xa"][s, f * 128:(f + 1) * 128, t0:t0 + 512],
                                    bufs["xev"][:, 0:512])])

        with ExitStack() as ffn_scope:
            hbuf = p.sb("hbuf", [128, 24 * 512], BF16, scope=ffn_scope)
            ubuf = p.sb("ubuf", [128, 4 * 512], BF16, scope=ffn_scope)
            wg_sb = p.sb("wg_sb", [128, 8 * 512], BF16, scope=ffn_scope)
            wu_sb = p.sb("wu_sb", [128, 8 * 512], BF16, scope=ffn_scope)
            wd_sb = p.sb("wd_sb", [128, 24 * 128], BF16, scope=ffn_scope)
            sc = 1.0 / 127.0
            for tc in range(4):
                t0 = tc * 512
                for hs in range(6):
                    p.dma("gpsimd",
                          [(wg_sb[:, c * 512:(c + 1) * 512],
                            t["wg_t"][c * 128:(c + 1) * 128,
                                      hs * 512:(hs + 1) * 512])
                           for c in range(8)] +
                          [(wu_sb[:, c * 512:(c + 1) * 512],
                            t["wu_t"][c * 128:(c + 1) * 128,
                                      hs * 512:(hs + 1) * 512])
                           for c in range(8)])

                    def mm_g(eng, hs=hs, t0=t0):
                        last = None
                        for j in range(4):
                            for c in range(8):
                                last = eng.matmul(
                                    ps_score[:, j * 512:(j + 1) * 512],
                                    wg_sb[:, c * 512 + j * 128:c * 512 + j * 128 + 128],
                                    xn_bf[:, c * T + t0:c * T + t0 + 512],
                                    start=(c == 0), stop=(c == 7))
                        return last
                    p.ph("tensor", mm_g)

                    def silu(eng, hs=hs):
                        last = None
                        for j in range(4):
                            last = eng.activation(
                                hbuf[:, (hs * 4 + j) * 512:(hs * 4 + j + 1) * 512],
                                ps_score[:, j * 512:(j + 1) * 512], AF.Silu,
                                scale=sc)
                        return last
                    p.ph("scalar", silu)

                    def mm_u(eng, hs=hs, t0=t0):
                        last = None
                        for j in range(4):
                            for c in range(8):
                                last = eng.matmul(
                                    ps_score[:, j * 512:(j + 1) * 512],
                                    wu_sb[:, c * 512 + j * 128:c * 512 + j * 128 + 128],
                                    xn_bf[:, c * T + t0:c * T + t0 + 512],
                                    start=(c == 0), stop=(c == 7))
                        return last
                    p.ph("tensor", mm_u)

                    def ev_u(eng):
                        last = None
                        for j in range(4):
                            last = eng.tensor_copy(
                                ubuf[:, j * 512:(j + 1) * 512],
                                ps_score[:, j * 512:(j + 1) * 512])
                        return last
                    p.ph("vector", ev_u)

                    def mul_h(eng, hs=hs):
                        last = None
                        for j in range(4):
                            ap = hbuf[:, (hs * 4 + j) * 512:(hs * 4 + j + 1) * 512]
                            last = eng.tensor_tensor(
                                ap, ap, ubuf[:, j * 512:(j + 1) * 512], ALU.mult)
                        return last
                    p.ph("vector", mul_h)

                for f in range(8):
                    p.dma("gpsimd",
                          [(wd_sb[:].rearrange("p (c g) -> p c g", c=24),
                            t["wd_t"][:, f * 128:(f + 1) * 128].rearrange(
                                "(c p) g -> p c g", p=128))])

                    def mm_d(eng, t0=t0):
                        last = None
                        for cj in range(24):
                            last = eng.matmul(
                                bufs["ps_a"][:, 0:512],
                                wd_sb[:, cj * 128:(cj + 1) * 128],
                                hbuf[:, cj * 512:(cj + 1) * 512],
                                start=(cj == 0), stop=(cj == 23))
                        return last
                    p.ph("tensor", mm_d)
                    p.dma("sync", [(bufs["x32"][:, 0:512],
                                    t["xa"][s, f * 128:(f + 1) * 128, t0:t0 + 512])])

                    def evac(eng):
                        return eng.scalar_tensor_tensor(
                            bufs["xev"][:, 0:512], bufs["ps_a"][:, 0:512],
                            sc * sc, bufs["x32"][:, 0:512], ALU.mult, ALU.add)
                    p.ph("vector", evac)
                    ev = p.dma("sync",
                               [(t["y"][s, f * 128:(f + 1) * 128, t0:t0 + 512],
                                 bufs["xev"][:, 0:512])])

    p.ph("sync", lambda eng: eng.nop(), waits=[ev])
    p.finalize()
    p.close()
    return nc



def build_layer1_attn(nc):
    t = {}
    t["x"] = nc.dram_tensor("x", [2, D, T], F32, kind="ExternalInput").ap()
    t["wqkv_t"] = nc.dram_tensor("wqkv_t", [D, 3 * D], F32, kind="ExternalInput").ap()
    t["bqkv_row"] = nc.dram_tensor("bqkv_row", [1, 3 * D], F32, kind="ExternalInput").ap()
    t["wout_t"] = nc.dram_tensor("wout_t", [D, D], F32, kind="ExternalInput").ap()
    t["bout_col"] = nc.dram_tensor("bout_col", [128, 8], F32, kind="ExternalInput").ap()
    t["y"] = nc.dram_tensor("y", [2, D, T], F32, kind="ExternalOutput").ap()

    p = Prog(nc)
    cc = Consts(p)
    W = 512
    bufs = {
        "x32": p.sb("x32", [128, W], F32),
        "xev": p.sb("xev", [128, W], F32),
        "xb": p.sb("xb", [128, W], BF16),
        "sq": p.sb("sq", [128, W], BF16),
        "tmp32": p.sb("tmp32", [128, W], F32),
        "mrow": p.sb("mrow", [1, W], F32),
        "vrow": p.sb("vrow", [1, W], F32),
        "mrow_bf": p.sb("mrow_bf", [1, W], BF16),
        "vrow_bf": p.sb("vrow_bf", [1, W], BF16),
        "ps_a": p.ps("ps_a", [128, 512], F32),
        "ps_b": p.ps("ps_b", [128, 512], F32),
        "ps_misc": p.ps("ps_misc", [128, 512], F32),
    }
    ps_score = p.ps("ps_score", [128, 2048], F32)
    ps_tr = p.ps("ps_tr", [128, 512], BF16)

    xn_bf = p.sb("xn_bf", [128, 8 * T], BF16)
    bias_col = p.sb("bias_col", [128, 8], F32)
    bqkv = p.sb("bqkv", [1, 3 * D], BF16)
    p.dma("sync", [(bias_col[:], t["bout_col"])])
    p.dma("gpsimd", [(bqkv[:], t["bqkv_row"])])

    st = {
        "PT": p.sb("PT", [128, 16 * 256], BF16),
        "negb": p.sb("negb", [1, T], BF16),
        "avtm": p.sb("avtm", [128, 256], BF16),
        "rcp_col": p.sb("rcp_col", [128, 4], F32),
        "m8": p.sb("m8", [1, 2], F32),
        "ps_score": ps_score,
        "ps_b": bufs["ps_b"],
        "ps_tr": ps_tr,
        "ps_misc": bufs["ps_misc"],
        "sq": bufs["sq"],
        "mrow": bufs["mrow"],
    }
    st["emit_negb"] = lambda p_, h: emit_negb_normbound(p_, cc, st, h)

    for s in range(2):
        with ExitStack() as attn_scope:
            qfm = p.sb("qfm", [128, 8 * T], BF16, scope=attn_scope)
            kfm = p.sb("kfm", [128, 8 * T], BF16, scope=attn_scope)
            vtm = p.sb("vtm", [128, 16, 16, 65], BF16, scope=attn_scope)
            av_fm = p.sb("av_fm", [128, 8 * T], BF16, scope=attn_scope)
            wq_sb = p.sb("wq_sb", [128, 8 * 512], BF16, scope=attn_scope)
            qtm = p.sb("qtm", [128, 512], BF16, scope=attn_scope)
            st.update(qfm=qfm, kfm=kfm, vtm=vtm, av_fm=av_fm)

            emit_ln(p, cc, t["x"][s], xn_bf, bufs)
            p.ph("vector", lambda eng, vtm=vtm: eng.memset(vtm[:, :, :, 64:65], 1.0))

            for grp in range(6):
                p.dma("gpsimd",
                      [(wq_sb[:, c * 512:(c + 1) * 512],
                        t["wqkv_t"][c * 128:(c + 1) * 128,
                                    grp * 512:(grp + 1) * 512])
                       for c in range(8)])
                is_q = grp < 2
                is_k = 2 <= grp < 4
                for tt in range(16):
                    t0 = tt * 128

                    def mm_qkv(eng, grp=grp, t0=t0):
                        for c in range(8):
                            eng.matmul(
                                bufs["ps_a"][:, 0:512],
                                xn_bf[:, c * T + t0:c * T + t0 + 128],
                                wq_sb[:, c * 512:(c + 1) * 512],
                                start=(c == 0), stop=False)
                        return eng.matmul(
                            bufs["ps_a"][:, 0:512], cc.ones_row[0:1, 0:128],
                            bqkv[0:1, grp * 512:(grp + 1) * 512],
                            start=False, stop=True)
                    p.ph("tensor", mm_qkv)

                    if is_q or is_k:
                        scale = 0.125 if is_q else 1.0

                        def cvt(eng, scale=scale):
                            return eng.tensor_scalar(qtm[:, 0:512],
                                                     bufs["ps_a"][:, 0:512],
                                                     scale, None, ALU.mult)
                        p.ph("vector", cvt)

                        def tr(eng):
                            last = None
                            for j in range(4):
                                last = eng.transpose(
                                    ps_tr[:, j * 128:(j + 1) * 128],
                                    qtm[:, j * 128:(j + 1) * 128], cc.ident[:])
                            return last
                        p.ph("tensor", tr)

                        dstt = qfm if is_q else kfm

                        def ev(eng, grp=grp, t0=t0, dstt=dstt):
                            last = None
                            for j in range(4):
                                pair = (grp % 2) * 4 + j
                                for half in range(2):
                                    last = eng.tensor_copy(
                                        dstt[half * 64:half * 64 + 64,
                                             pair * T + t0:pair * T + t0 + 128],
                                        ps_tr[half * 64:half * 64 + 64,
                                              j * 128:(j + 1) * 128])
                            return last
                        p.ph("vector", ev)
                    else:
                        def evv(eng, grp=grp, tt=tt, vtm=vtm):
                            last = None
                            for j in range(8):
                                h = (grp - 4) * 8 + j
                                last = eng.tensor_copy(
                                    vtm[:, h, tt, 0:64],
                                    bufs["ps_a"][:, j * 64:(j + 1) * 64])
                            return last
                        p.ph("vector", evv)

            emit_attn_core(p, cc, st, heads=16, causal=False, window=None, qw=256)

            wproj_sb = p.sb("wproj_sb", [128, 8 * 128], BF16, scope=attn_scope)
            for f in range(8):
                p.dma("gpsimd",
                      [(wproj_sb[:].rearrange("p (c g) -> p c g", c=8),
                        t["wout_t"][:, f * 128:(f + 1) * 128].rearrange(
                            "(c p) g -> p c g", p=128))])
                for tc in range(4):
                    t0 = tc * 512

                    def mm_o(eng, t0=t0, av_fm=av_fm, wproj_sb=wproj_sb):
                        last = None
                        for c in range(8):
                            last = eng.matmul(
                                bufs["ps_a"][:, 0:512],
                                wproj_sb[:, c * 128:(c + 1) * 128],
                                av_fm[:, c * T + t0:c * T + t0 + 512],
                                start=(c == 0), stop=(c == 7))
                        return last
                    p.ph("tensor", mm_o)
                    p.dma("sync", [(bufs["x32"][:, 0:512],
                                    t["x"][s, f * 128:(f + 1) * 128, t0:t0 + 512])])

                    def evac(eng, f=f):
                        return eng.scalar_tensor_tensor(
                            bufs["xev"][:, 0:512], bufs["ps_a"][:, 0:512],
                            bias_col[:, f:f + 1], bufs["x32"][:, 0:512],
                            ALU.add, ALU.add)
                    p.ph("vector", evac)
                    ev = p.dma("sync",
                               [(t["y"][s, f * 128:(f + 1) * 128, t0:t0 + 512],
                                 bufs["xev"][:, 0:512])])

    p.ph("sync", lambda eng: eng.nop(), waits=[ev])
    p.finalize()
    p.close()
    return nc


def run_layer1_attn(x_fm8, w):
    nc = bass.Bass("TRN2", target_bir_lowering=False)
    build_layer1_attn(nc)
    wqkv_t = np.ascontiguousarray(np.asarray(w["L1_attn_qkv_w"]).T)
    bqkv_row = np.ascontiguousarray(np.asarray(w["L1_attn_qkv_b"]).reshape(1, -1))
    wout_t = np.ascontiguousarray(np.asarray(w["L1_attn_out_w"]).T)
    bout_col = np.ascontiguousarray(
        np.asarray(w["L1_attn_out_b"]).reshape(8, 128).T)
    in_maps = [{"x": x_fm8[c], "wqkv_t": wqkv_t, "bqkv_row": bqkv_row,
                "wout_t": wout_t, "bout_col": bout_col} for c in range(NCORES)]
    res = run_bass_kernel_spmd(nc, in_maps, core_ids=list(range(NCORES)))
    return [res.results[c]["y"] for c in range(NCORES)]



def build_layer1_ffn(nc):
    t = {}
    t["x"] = nc.dram_tensor("x", [2, D, T], F32, kind="ExternalInput").ap()
    t["xa"] = nc.dram_tensor("xa", [2, D, T], F32, kind="ExternalInput").ap()
    t["ln1_row"] = nc.dram_tensor("ln1_row", [1, D], F32, kind="ExternalInput").ap()
    t["ln2_row"] = nc.dram_tensor("ln2_row", [1, D], F32, kind="ExternalInput").ap()
    t["kvd_t"] = nc.dram_tensor("kvd_t", [D, 256], F32, kind="ExternalInput").ap()
    t["qd_t"] = nc.dram_tensor("qd_t", [D, 256], F32, kind="ExternalInput").ap()
    t["ku_t"] = nc.dram_tensor("ku_t", [256, D], F32, kind="ExternalInput").ap()
    t["qu_t"] = nc.dram_tensor("qu_t", [256, D], F32, kind="ExternalInput").ap()
    t["vu_t"] = nc.dram_tensor("vu_t", [256, D], F32, kind="ExternalInput").ap()
    t["rq_t"] = nc.dram_tensor("rq_t", [256, 512], F32, kind="ExternalInput").ap()
    t["rk_t"] = nc.dram_tensor("rk_t", [D, 512], F32, kind="ExternalInput").ap()
    t["ow_t"] = nc.dram_tensor("ow_t", [D, D], F32, kind="ExternalInput").ap()
    t["rw_t"] = nc.dram_tensor("rw_t", [D, 7], F32, kind="ExternalInput").ap()
    t["rb_row"] = nc.dram_tensor("rb_row", [1, 7], F32, kind="ExternalInput").ap()
    t["rg_all"] = nc.dram_tensor("rg_all", [8, D, 2048], F32, kind="ExternalInput").ap()
    t["ru_all"] = nc.dram_tensor("ru_all", [8, D, 2048], F32, kind="ExternalInput").ap()
    t["rd_all"] = nc.dram_tensor("rd_all", [8, 2048, D], F32, kind="ExternalInput").ap()
    t["rope2"] = nc.dram_tensor("rope2", [2, T, 512], BF16, kind="ExternalInput").ap()
    t["xnh"] = nc.dram_tensor("xnh", [2, 128, 8 * T], BF16, kind="Internal").ap()
    t["t1h"] = nc.dram_tensor("t1h", [2, D, T], F32, kind="Internal").ap()
    t["macc"] = nc.dram_tensor("macc", [2, T, D], F32, kind="Internal").ap()
    t["y"] = nc.dram_tensor("y", [2, D, T], F32, kind="ExternalOutput").ap()

    p = Prog(nc)
    cc = Consts(p)
    W = 512
    bufs = {
        "x32": p.sb("x32", [128, W], F32),
        "xev": p.sb("xev", [128, W], F32),
        "xb": p.sb("xb", [128, W], BF16),
        "sq": p.sb("sq", [128, W], BF16),
        "tmp32": p.sb("tmp32", [128, W], F32),
        "tmpo": p.sb("tmpo", [128, 256], F32),
        "mrow": p.sb("mrow", [1, W], F32),
        "vrow": p.sb("vrow", [1, W], F32),
        "mrow_bf": p.sb("mrow_bf", [1, W], BF16),
        "vrow_bf": p.sb("vrow_bf", [1, W], BF16),
        "ps_a": p.ps("ps_a", [128, 512], F32),
        "ps_b": p.ps("ps_b", [128, 512], F32),
        "ps_misc": p.ps("ps_misc", [128, 512], F32),
    }
    ps_score = p.ps("ps_score", [128, 2048], F32)
    ps_tr = p.ps("ps_tr", [128, 512], BF16)

    ln1_row = p.sb("ln1_row", [1, D], BF16)
    ln2_row = p.sb("ln2_row", [1, D], BF16)
    rb_bf = p.sb("rb_bf", [1, 7], BF16)
    p.dma("gpsimd", [(ln1_row[:], t["ln1_row"]), (ln2_row[:], t["ln2_row"]),
                     (rb_bf[:], t["rb_row"])])

    st = {
        "negb": p.sb("negb", [1, T], BF16),
        "avtm": p.sb("avtm", [128, 256], BF16),
        "rcp_col": p.sb("rcp_col", [128, 4], F32),
        "m8": p.sb("m8", [1, 2], F32),
        "ps_score": ps_score,
        "ps_b": bufs["ps_b"],
        "ps_tr": ps_tr,
        "ps_misc": bufs["ps_misc"],
        "sq": bufs["sq"],
        "mrow": bufs["mrow"],
    }
    st["emit_negb"] = lambda p_, h: emit_negb_normbound(p_, cc, st, h)

    for s in range(2):
        with ExitStack() as sc_attn:
            qfm = p.sb("qfm", [128, 8 * T], BF16, scope=sc_attn)
            kfm = p.sb("kfm", [128, 8 * T], BF16, scope=sc_attn)
            vtm = p.sb("vtm", [128, 16, 16, 65], BF16, scope=sc_attn)
            st.update(qfm=qfm, kfm=kfm, vtm=vtm)
            p.ph("vector", lambda eng, vtm=vtm: eng.memset(vtm[:, :, :, 64:65], 1.0))

            with ExitStack() as sc_proj:
                emit_ln(p, cc, t["x"][s], t["xnh"][s], bufs, dst_hbm=True)
                xn2 = p.sb("xn2", [128, 8 * T], BF16, scope=sc_proj)
                emit_ln(p, cc, t["xnh"][s], xn2, bufs, w_row=ln1_row,
                        src_bf16=True)

                kvlat = p.sb("kvlat", [128, 2 * T], BF16, scope=sc_proj)
                qlat = p.sb("qlat", [128, 2 * T], BF16, scope=sc_proj)
                with ExitStack() as sc_lat:
                    kvd_sb = p.sb("kvd_sb", [128, 8 * 256], BF16, scope=sc_lat)
                    qd_sb = p.sb("qd_sb", [128, 8 * 256], BF16, scope=sc_lat)
                    p.dma("gpsimd",
                          [(kvd_sb[:].rearrange("p (c g) -> p c g", c=8),
                            t["kvd_t"].rearrange("(c p) g -> p c g", p=128)),
                           (qd_sb[:].rearrange("p (c g) -> p c g", c=8),
                            t["qd_t"].rearrange("(c p) g -> p c g", p=128))])

                    for dst, wsb in ((kvlat, kvd_sb), (qlat, qd_sb)):
                        for lc in range(2):
                            for tc in range(4):
                                t0 = tc * 512

                                def mm_lat(eng, wsb=wsb, lc=lc, t0=t0):
                                    last = None
                                    for c in range(8):
                                        last = eng.matmul(
                                            bufs["ps_a"][:, 0:512],
                                            wsb[:, c * 256 + lc * 128:c * 256 + lc * 128 + 128],
                                            xn2[:, c * T + t0:c * T + t0 + 512],
                                            start=(c == 0), stop=(c == 7))
                                    return last
                                p.ph("tensor", mm_lat)
                                p.ph("vector", lambda eng, dst=dst, lc=lc, t0=t0:
                                     eng.tensor_copy(dst[:, lc * T + t0:lc * T + t0 + 512],
                                                     bufs["ps_a"][:, 0:512]))

                ku_sb = p.sb("ku_sb", [128, 2 * D], BF16, scope=sc_proj)
                qu_sb = p.sb("qu_sb", [128, 2 * D], BF16, scope=sc_proj)
                vu_sb = p.sb("vu_sb", [128, 2 * D], BF16, scope=sc_proj)
                rq_sb = p.sb("rq_sb", [128, 2 * 512], BF16, scope=sc_proj)
                rk_sb = p.sb("rk_sb", [128, 8 * 512], BF16, scope=sc_proj)
                rop_sb = p.sb("rop_sb", [128, 2 * 512], BF16, scope=sc_proj)
                rtm = p.sb("rtm", [128, 512], BF16, scope=sc_proj)
                qtm = p.sb("qtm", [128, 512], BF16, scope=sc_proj)
                p.dma("gpsimd",
                      [(ku_sb[:].rearrange("p (c g) -> p c g", c=2),
                        t["ku_t"].rearrange("(c p) g -> p c g", p=128)),
                       (qu_sb[:].rearrange("p (c g) -> p c g", c=2),
                        t["qu_t"].rearrange("(c p) g -> p c g", p=128)),
                       (vu_sb[:].rearrange("p (c g) -> p c g", c=2),
                        t["vu_t"].rearrange("(c p) g -> p c g", p=128)),
                       (rq_sb[:].rearrange("p (c g) -> p c g", c=2),
                        t["rq_t"].rearrange("(c p) g -> p c g", p=128)),
                       (rk_sb[:].rearrange("p (c g) -> p c g", c=8),
                        t["rk_t"].rearrange("(c p) g -> p c g", p=128))])

                def v3(ap512, half):
                    return ap512.rearrange("p (h two g) -> p h two g",
                                           two=2, g=16)[:, :, half, :]

                for tt in range(16):
                    t0 = tt * 128
                    p.dma("sync", [(rop_sb[:, 0:512], t["rope2"][0, t0:t0 + 128, :]),
                                   (rop_sb[:, 512:1024], t["rope2"][1, t0:t0 + 128, :])])
                    for which in ("k", "q"):
                        if which == "k":
                            def mm_rot(eng, t0=t0):
                                last = None
                                for c in range(8):
                                    last = eng.matmul(
                                        bufs["ps_b"][:, 0:512],
                                        xn2[:, c * T + t0:c * T + t0 + 128],
                                        rk_sb[:, c * 512:(c + 1) * 512],
                                        start=(c == 0), stop=(c == 7))
                                return last
                        else:
                            def mm_rot(eng, t0=t0):
                                last = None
                                for c in range(2):
                                    last = eng.matmul(
                                        bufs["ps_b"][:, 0:512],
                                        qlat[:, c * T + t0:c * T + t0 + 128],
                                        rq_sb[:, c * 512:(c + 1) * 512],
                                        start=(c == 0), stop=(c == 1))
                                return last
                        p.ph("tensor", mm_rot)

                        def rope(eng):
                            ps = bufs["ps_b"][:, 0:512]
                            te = bufs["tmp32"][:, 0:256].rearrange(
                                "p (h g) -> p h g", g=16)
                            to = bufs["tmpo"][:, 0:256].rearrange(
                                "p (h g) -> p h g", g=16)
                            c1 = v3(rop_sb[:, 0:512], 0)
                            c2 = v3(rop_sb[:, 0:512], 1)
                            s1 = v3(rop_sb[:, 512:1024], 0)
                            s2 = v3(rop_sb[:, 512:1024], 1)
                            x1 = v3(ps, 0)
                            x2 = v3(ps, 1)
                            eng.tensor_tensor(te, x1, c1, ALU.mult)
                            eng.tensor_tensor(to, x2, s1, ALU.mult)
                            eng.tensor_tensor(v3(rtm[:, 0:512], 0), te, to,
                                              ALU.subtract)
                            eng.tensor_tensor(te, x2, c2, ALU.mult)
                            eng.tensor_tensor(to, x1, s2, ALU.mult)
                            return eng.tensor_tensor(v3(rtm[:, 0:512], 1), te, to,
                                                     ALU.add)
                        p.ph("vector", rope)

                        for grp in range(2):
                            if which == "k":
                                def mm_main(eng, grp=grp, t0=t0):
                                    last = None
                                    for c in range(2):
                                        last = eng.matmul(
                                            bufs["ps_a"][:, 0:512],
                                            kvlat[:, c * T + t0:c * T + t0 + 128],
                                            ku_sb[:, c * D + grp * 512:c * D + grp * 512 + 512],
                                            start=(c == 0), stop=(c == 1))
                                    return last
                            else:
                                def mm_main(eng, grp=grp, t0=t0):
                                    last = None
                                    for c in range(2):
                                        last = eng.matmul(
                                            bufs["ps_a"][:, 0:512],
                                            qlat[:, c * T + t0:c * T + t0 + 128],
                                            qu_sb[:, c * D + grp * 512:c * D + grp * 512 + 512],
                                            start=(c == 0), stop=(c == 1))
                                    return last
                            p.ph("tensor", mm_main)

                            scale = 0.125 if which == "q" else 1.0

                            def asm(eng, grp=grp, scale=scale):
                                ps = bufs["ps_a"][:, 0:512]
                                pv = ps.rearrange("p (h d) -> p h d", d=64)
                                qv = qtm[:, 0:512].rearrange("p (h d) -> p h d", d=64)
                                rv = rtm[:, grp * 256:(grp + 1) * 256].rearrange(
                                    "p (h g) -> p h g", g=32)
                                eng.tensor_scalar(qv[:, :, 32:64], pv[:, :, 32:64],
                                                  scale, None, ALU.mult)
                                return eng.tensor_scalar(qv[:, :, 0:32], rv,
                                                         scale, None, ALU.mult)
                            p.ph("vector", asm)

                            def tr(eng):
                                last = None
                                for j in range(4):
                                    last = eng.transpose(
                                        ps_tr[:, j * 128:(j + 1) * 128],
                                        qtm[:, j * 128:(j + 1) * 128], cc.ident[:])
                                return last
                            p.ph("tensor", tr)

                            dstt = kfm if which == "k" else qfm

                            def ev(eng, grp=grp, t0=t0, dstt=dstt):
                                last = None
                                for j in range(4):
                                    pair = grp * 4 + j
                                    for half in range(2):
                                        last = eng.tensor_copy(
                                            dstt[half * 64:half * 64 + 64,
                                                 pair * T + t0:pair * T + t0 + 128],
                                            ps_tr[half * 64:half * 64 + 64,
                                                  j * 128:(j + 1) * 128])
                                return last
                            p.ph("vector", ev)

                    for grp in range(2):
                        def mm_v(eng, grp=grp, t0=t0):
                            last = None
                            for c in range(2):
                                last = eng.matmul(
                                    bufs["ps_a"][:, 0:512],
                                    kvlat[:, c * T + t0:c * T + t0 + 128],
                                    vu_sb[:, c * D + grp * 512:c * D + grp * 512 + 512],
                                    start=(c == 0), stop=(c == 1))
                            return last
                        p.ph("tensor", mm_v)

                        def evv(eng, grp=grp, tt=tt, vtm=vtm):
                            last = None
                            for j in range(8):
                                h = grp * 8 + j
                                last = eng.tensor_copy(
                                    vtm[:, h, tt, 0:64],
                                    bufs["ps_a"][:, j * 64:(j + 1) * 64])
                            return last
                        p.ph("vector", evv)

            av_fm = p.sb("av_fm", [128, 8 * T], BF16, scope=sc_attn)
            PT = p.sb("PT", [128, 16 * 256], BF16, scope=sc_attn)
            st.update(av_fm=av_fm, PT=PT)
            emit_attn_core(p, cc, st, heads=16, causal=True, window=None, qw=256)

            wproj_sb = p.sb("wproj_sb", [128, 8 * 128], BF16, scope=sc_attn)
            xnb_sb = p.sb("xnb_sb", [128, 512], BF16, scope=sc_attn)
            for f in range(8):
                p.dma("gpsimd",
                      [(wproj_sb[:].rearrange("p (c g) -> p c g", c=8),
                        t["ow_t"][:, f * 128:(f + 1) * 128].rearrange(
                            "(c p) g -> p c g", p=128))])
                for tc in range(4):
                    t0 = tc * 512

                    def mm_o(eng, t0=t0, av_fm=av_fm, wproj_sb=wproj_sb):
                        last = None
                        for c in range(8):
                            last = eng.matmul(
                                bufs["ps_a"][:, 0:512],
                                wproj_sb[:, c * 128:(c + 1) * 128],
                                av_fm[:, c * T + t0:c * T + t0 + 512],
                                start=(c == 0), stop=(c == 7))
                        return last
                    p.ph("tensor", mm_o)
                    p.dma("sync", [(xnb_sb[:],
                                    t["xnh"][s][:, f * T + t0:f * T + t0 + 512])])
                    p.ph("vector", lambda eng: eng.tensor_tensor(
                        bufs["xev"][:, 0:512], bufs["ps_a"][:, 0:512],
                        xnb_sb[:], ALU.add))
                    p.dma("sync", [(t["t1h"][s, f * 128:(f + 1) * 128, t0:t0 + 512],
                                    bufs["xev"][:, 0:512])])

        with ExitStack() as sc_moe:
            xn3 = p.sb("xn3", [128, 8 * T], BF16, scope=sc_moe)
            emit_ln(p, cc, t["t1h"][s], xn3, bufs, w_row=ln2_row)

            rw_sb = p.sb("rw_sb", [128, 8, 7], BF16, scope=sc_moe)
            wgt_sb = p.sb("wgt_sb", [128, 16, 8], F32, scope=sc_moe)
            prow = p.sb("prow", [128, 8], F32, scope=sc_moe)
            ptmp = p.sb("ptmp", [128, 8], F32, scope=sc_moe)
            mcol = p.sb("mcol", [128, 4], F32, scope=sc_moe)
            p.dma("gpsimd", [(rw_sb[:],
                              t["rw_t"].rearrange("(c p) e -> p c e", p=128))])
            p.ph("vector", lambda eng: eng.memset(wgt_sb[:, :, 0:1], 1.0))
            for tt in range(16):
                t0 = tt * 128

                def mm_r(eng, t0=t0):
                    for c in range(8):
                        eng.matmul(bufs["ps_misc"][:, 0:7],
                                   xn3[:, c * T + t0:c * T + t0 + 128],
                                   rw_sb[:, c, :], start=(c == 0), stop=False)
                    return eng.matmul(bufs["ps_misc"][:, 0:7],
                                      cc.ones_row[0:1, 0:128], rb_bf[0:1, 0:7],
                                      start=False, stop=True)
                p.ph("tensor", mm_r)
                p.ph("scalar", lambda eng: eng.activation(
                    prow[:, 0:7], bufs["ps_misc"][:, 0:7], AF.Sigmoid))

                emit_top2(p, prow, ptmp, mcol, wgt_sb[:, tt, 1:8], 7)

            wg_sb = p.sb("wg_sb", [128, 8 * 2048], BF16, scope=sc_moe)
            wu_sb = p.sb("wu_sb", [128, 8 * 2048], BF16, scope=sc_moe)
            wd_sb = p.sb("wd_sb", [128, 16 * D], BF16, scope=sc_moe)
            h_fm = p.sb("h_fm", [128, 16 * 512], BF16, scope=sc_moe)
            ubuf = p.sb("ubuf", [128, 4 * 512], BF16, scope=sc_moe)
            maccbs = [p.sb("maccb", [128, 512], F32, scope=sc_moe)
                      for _ in range(2)]
            macc_evs = [None, None]
            dnct = [0]

            for e in range(8):
                p.dma("gpsimd",
                      [(wg_sb[:, c * 2048:(c + 1) * 2048],
                        t["rg_all"][e, c * 128:(c + 1) * 128, :])
                       for c in range(8)] +
                      [(wu_sb[:, c * 2048:(c + 1) * 2048],
                        t["ru_all"][e, c * 128:(c + 1) * 128, :])
                       for c in range(8)] +
                      [(wd_sb[:, ic * D:(ic + 1) * D],
                        t["rd_all"][e, ic * 128:(ic + 1) * 128, :])
                       for ic in range(16)])
                for tc in range(4):
                    t0 = tc * 512
                    for ib in range(4):
                        def mm_g(eng, ib=ib, t0=t0):
                            last = None
                            for j in range(4):
                                ic = ib * 4 + j
                                for c in range(8):
                                    last = eng.matmul(
                                        ps_score[:, j * 512:(j + 1) * 512],
                                        wg_sb[:, c * 2048 + ic * 128:c * 2048 + ic * 128 + 128],
                                        xn3[:, c * T + t0:c * T + t0 + 512],
                                        start=(c == 0), stop=(c == 7))
                            return last
                        p.ph("tensor", mm_g)

                        def silu(eng, ib=ib):
                            last = None
                            for j in range(4):
                                ic = ib * 4 + j
                                last = eng.activation(
                                    h_fm[:, ic * 512:(ic + 1) * 512],
                                    ps_score[:, j * 512:(j + 1) * 512], AF.Silu)
                            return last
                        p.ph("scalar", silu)

                        def mm_u(eng, ib=ib, t0=t0):
                            last = None
                            for j in range(4):
                                ic = ib * 4 + j
                                for c in range(8):
                                    last = eng.matmul(
                                        ps_score[:, j * 512:(j + 1) * 512],
                                        wu_sb[:, c * 2048 + ic * 128:c * 2048 + ic * 128 + 128],
                                        xn3[:, c * T + t0:c * T + t0 + 512],
                                        start=(c == 0), stop=(c == 7))
                            return last
                        p.ph("tensor", mm_u)

                        def ev_u(eng):
                            last = None
                            for j in range(4):
                                last = eng.tensor_copy(
                                    ubuf[:, j * 512:(j + 1) * 512],
                                    ps_score[:, j * 512:(j + 1) * 512])
                            return last
                        p.ph("vector", ev_u)

                        def mul_h(eng, ib=ib):
                            last = None
                            for j in range(4):
                                ic = ib * 4 + j
                                ap = h_fm[:, ic * 512:(ic + 1) * 512]
                                last = eng.tensor_tensor(
                                    ap, ap, ubuf[:, j * 512:(j + 1) * 512],
                                    ALU.mult)
                            return last
                        p.ph("vector", mul_h)

                    for ts in range(4):
                        for fg in range(2):
                            def mm_d(eng, ts=ts, fg=fg):
                                last = None
                                for ic in range(16):
                                    last = eng.matmul(
                                        bufs["ps_a"][:, 0:512],
                                        h_fm[:, ic * 512 + ts * 128:ic * 512 + ts * 128 + 128],
                                        wd_sb[:, ic * D + fg * 512:ic * D + fg * 512 + 512],
                                        start=(ic == 0), stop=(ic == 15))
                                return last
                            p.ph("tensor", mm_d)
                            par = dnct[0] % 2
                            dnct[0] += 1
                            mb = maccbs[par]
                            p.ph("vector", lambda eng, tc=tc, ts=ts, e=e, mb=mb:
                                 eng.tensor_scalar(mb[:], bufs["ps_a"][:, 0:512],
                                                   wgt_sb[:, tc * 4 + ts, e:e + 1],
                                                   None, ALU.mult),
                                 waits=[p.tail, macc_evs[par]])
                            tglob = t0 + ts * 128
                            macc_evs[par] = ev = p.dma(
                                "gpsimd",
                                [(t["macc"][s, tglob:tglob + 128,
                                            fg * 512:(fg + 1) * 512], mb[:])],
                                accum=(e > 0), chain=False, waits=[p.tail])

            moefm = p.sb("moefm", [128, 8 * 512], BF16, scope=sc_moe)
            m32 = p.sb("m32", [128, D], F32, scope=sc_moe)
            mbf = p.sb("mbf", [128, D], BF16, scope=sc_moe)
            first_f = [True]
            for tc in range(4):
                t0 = tc * 512
                for tsl in range(4):
                    tg = t0 + tsl * 128
                    if first_f[0]:
                        first_f[0] = False
                        p.dma("sync", [(m32[:], t["macc"][s, tg:tg + 128, :])],
                              waits=[p.tail] + [e for e in macc_evs if e])
                    else:
                        p.dma("sync", [(m32[:], t["macc"][s, tg:tg + 128, :])])
                    p.ph("vector", lambda eng: eng.tensor_copy(mbf[:], m32[:]))
                    for half in range(2):
                        def trm(eng, half=half):
                            last = None
                            for j in range(4):
                                c = half * 4 + j
                                last = eng.transpose(
                                    ps_tr[:, j * 128:(j + 1) * 128],
                                    mbf[:, c * 128:(c + 1) * 128], cc.ident[:])
                            return last
                        p.ph("tensor", trm)

                        def evm(eng, half=half, tsl=tsl):
                            last = None
                            for j in range(4):
                                c = half * 4 + j
                                last = eng.tensor_copy(
                                    moefm[:, c * 512 + tsl * 128:c * 512 + tsl * 128 + 128],
                                    ps_tr[:, j * 128:(j + 1) * 128])
                            return last
                        p.ph("vector", evm)
                for f in range(8):
                    p.dma("sync",
                          [(bufs["x32"][:, 0:512],
                            t["t1h"][s, f * 128:(f + 1) * 128, t0:t0 + 512]),
                           (bufs["tmp32"][:, 0:512],
                            t["xa"][s, f * 128:(f + 1) * 128, t0:t0 + 512])])

                    def fin(eng, f=f):
                        eng.tensor_tensor(bufs["xev"][:, 0:512],
                                          moefm[:, f * 512:(f + 1) * 512],
                                          bufs["x32"][:, 0:512], ALU.add)
                        return eng.tensor_tensor(bufs["xev"][:, 0:512],
                                                 bufs["xev"][:, 0:512],
                                                 bufs["tmp32"][:, 0:512], ALU.add)
                    p.ph("vector", fin)
                    ev = p.dma("sync",
                               [(t["y"][s, f * 128:(f + 1) * 128, t0:t0 + 512],
                                 bufs["xev"][:, 0:512])])

    p.ph("sync", lambda eng: eng.nop(), waits=[ev])
    p.finalize()
    p.close()
    return nc


def _rope_tables_l1():
    RD = 32
    inv = (1.0 / 10000.0 ** (np.arange(0, RD, 2, dtype=np.float32) / RD)).astype(np.float32)
    tpos = np.arange(T, dtype=np.float32)
    freqs = tpos[:, None] * inv[None, :]
    emb = np.concatenate([freqs, freqs], axis=-1)
    cos = np.cos(emb).astype(np.float32)
    sin = np.sin(emb).astype(np.float32)
    import ml_dtypes
    out = np.stack([np.tile(cos, (1, 16)), np.tile(sin, (1, 16))])
    return np.ascontiguousarray(out.astype(ml_dtypes.bfloat16))


def run_layer1_ffn(x_fm8, xa_fm8, w):
    nc = bass.Bass("TRN2", target_bir_lowering=False)
    build_layer1_ffn(nc)
    sgT = np.asarray(w["L1_ffn_shared_gate"])[0].T[None]
    suT = np.asarray(w["L1_ffn_shared_up"])[0].T[None]
    sdT = np.asarray(w["L1_ffn_shared_down"])[0].T[None]
    rgT = np.asarray(w["L1_ffn_routed_gate"]).transpose(0, 2, 1)
    ruT = np.asarray(w["L1_ffn_routed_up"]).transpose(0, 2, 1)
    rdT = np.asarray(w["L1_ffn_routed_down"]).transpose(0, 2, 1)
    base = {
        "ln1_row": np.asarray(w["L1_ffn_ln1_w"]).reshape(1, -1).astype(np.float32),
        "ln2_row": np.asarray(w["L1_ffn_ln2_w"]).reshape(1, -1).astype(np.float32),
        "kvd_t": np.ascontiguousarray(np.asarray(w["L1_ffn_kv_d"]).T),
        "qd_t": np.ascontiguousarray(np.asarray(w["L1_ffn_q_d"]).T),
        "ku_t": np.ascontiguousarray(np.asarray(w["L1_ffn_k_u"]).T),
        "qu_t": np.ascontiguousarray(np.asarray(w["L1_ffn_q_u"]).T),
        "vu_t": np.ascontiguousarray(np.asarray(w["L1_ffn_v_u"]).T),
        "rq_t": np.ascontiguousarray(np.asarray(w["L1_ffn_rope_q_w"]).T),
        "rk_t": np.ascontiguousarray(np.asarray(w["L1_ffn_rope_k_w"]).T),
        "ow_t": np.ascontiguousarray(np.asarray(w["L1_ffn_o_w"]).T),
        "rw_t": np.ascontiguousarray(np.asarray(w["L1_ffn_router_w"]).T),
        "rb_row": np.asarray(w["L1_ffn_routing_bias"]).reshape(1, -1).astype(np.float32),
        "rg_all": np.ascontiguousarray(np.concatenate([sgT, rgT], axis=0)),
        "ru_all": np.ascontiguousarray(np.concatenate([suT, ruT], axis=0)),
        "rd_all": np.ascontiguousarray(np.concatenate([sdT, rdT], axis=0)),
        "rope2": _rope_tables_l1(),
    }
    in_maps = []
    for c in range(NCORES):
        m = dict(base)
        m["x"] = x_fm8[c]
        m["xa"] = xa_fm8[c]
        in_maps.append(m)
    res = run_bass_kernel_spmd(nc, in_maps, core_ids=list(range(NCORES)))
    return [res.results[c]["y"] for c in range(NCORES)]



def build_layer2(nc):
    t = {}
    t["x"] = nc.dram_tensor("x", [2, D, T], F32, kind="ExternalInput").ap()
    t["wv_t"] = nc.dram_tensor("wv_t", [D, D], F32, kind="ExternalInput").ap()
    t["bv_row"] = nc.dram_tensor("bv_row", [1, D], F32, kind="ExternalInput").ap()
    t["wo_t"] = nc.dram_tensor("wo_t", [D, D], F32, kind="ExternalInput").ap()
    t["bo_col"] = nc.dram_tensor("bo_col", [128, 8], F32, kind="ExternalInput").ap()
    t["wr_t"] = nc.dram_tensor("wr_t", [D, 128], F32, kind="ExternalInput").ap()
    t["cent"] = nc.dram_tensor("cent", [16, 128], F32, kind="ExternalInput").ap()
    t["temp"] = nc.dram_tensor("temp", [1, 1], F32, kind="ExternalInput").ap()
    t["ie"] = nc.dram_tensor("ie", [16, D, 512], F32, kind="ExternalInput").ap()
    t["oe"] = nc.dram_tensor("oe", [16, 512, 64], F32, kind="ExternalInput").ap()
    t["xa"] = nc.dram_tensor("xa", [2, D, T], F32, kind="Internal").ap()
    t["y"] = nc.dram_tensor("y", [2, D, T], F32, kind="ExternalOutput").ap()

    p = Prog(nc)
    cc = Consts(p)
    W = 512
    bufs = {
        "x32": p.sb("x32", [128, W], F32),
        "xev": p.sb("xev", [128, W], F32),
        "xb": p.sb("xb", [128, W], BF16),
        "sq": p.sb("sq", [128, W], BF16),
        "tmp32": p.sb("tmp32", [128, W], F32),
        "mrow": p.sb("mrow", [1, W], F32),
        "vrow": p.sb("vrow", [1, W], F32),
        "mrow_bf": p.sb("mrow_bf", [1, W], BF16),
        "vrow_bf": p.sb("vrow_bf", [1, W], BF16),
        "ps_a": p.ps("ps_a", [128, 512], F32),
        "ps_b": p.ps("ps_b", [128, 512], F32),
        "ps_misc": p.ps("ps_misc", [128, 512], F32),
    }
    ps_score = p.ps("ps_score", [128, 2048], F32)
    ps_tr = p.ps("ps_tr", [128, 512], BF16)

    xn_bf = p.sb("xn_bf", [128, 8 * T], BF16)
    bias_col = p.sb("bias_col", [128, 8], F32)
    bv_bf = p.sb("bv_bf", [1, D], BF16)
    p.dma("sync", [(bias_col[:], t["bo_col"])])
    p.dma("gpsimd", [(bv_bf[:], t["bv_row"])])

    st = {
        "negb": p.sb("negb", [1, T], BF16),
        "avtm": p.sb("avtm", [128, 256], BF16),
        "rcp_col": p.sb("rcp_col", [128, 4], F32),
        "m8": p.sb("m8", [1, 2], F32),
        "ps_score": ps_score,
        "ps_b": bufs["ps_b"],
        "ps_tr": ps_tr,
        "ps_misc": bufs["ps_misc"],
        "sq": bufs["sq"],
        "mrow": bufs["mrow"],
    }

    def emit_negb_diag(p_, h, seqlen=T):
        kfm, negb = st["kfm"], st["negb"]
        sq = st["sq"]
        ps_misc_ = st["ps_misc"]
        hb = (h % 2) * 64
        pcol = (h // 2) * seqlen
        for c in range(seqlen // 512):
            p_.ph("scalar", lambda eng, c=c: eng.activation(
                sq[0:64, 0:512],
                kfm[hb:hb + 64, pcol + c * 512:pcol + (c + 1) * 512], AF.Square))
            p_.ph("tensor", lambda eng: eng.matmul(
                ps_misc_[0:1, 0:512], cc.ones_col[0:64, :], sq[0:64, 0:512],
                start=True, stop=True))
            p_.ph("vector", lambda eng, c=c: eng.tensor_scalar(
                negb[0:1, c * 512:(c + 1) * 512], ps_misc_[0:1, 0:512],
                -0.125, None, ALU.mult))
    st["emit_negb"] = emit_negb_diag

    for s in range(2):
        with ExitStack() as sc_attn:
            qfm = p.sb("qfm", [128, 8 * T], BF16, scope=sc_attn)
            kfm = p.sb("kfm", [128, 8 * T], BF16, scope=sc_attn)
            vtm = p.sb("vtm", [128, 16, 16, 65], BF16, scope=sc_attn)
            av_fm = p.sb("av_fm", [128, 8 * T], BF16, scope=sc_attn)
            wq_sb = p.sb("wq_sb", [128, 8 * 512], BF16, scope=sc_attn)
            qtm = p.sb("qtm", [128, 512], BF16, scope=sc_attn)
            st.update(qfm=qfm, kfm=kfm, vtm=vtm, av_fm=av_fm)

            emit_ln(p, cc, t["x"][s], xn_bf, bufs)
            p.ph("vector", lambda eng, vtm=vtm: eng.memset(vtm[:, :, :, 64:65], 1.0))

            for grp in range(2):
                p.dma("gpsimd",
                      [(wq_sb[:, c * 512:(c + 1) * 512],
                        t["wv_t"][c * 128:(c + 1) * 128,
                                  grp * 512:(grp + 1) * 512])
                       for c in range(8)])
                for tt in range(16):
                    t0 = tt * 128

                    def mm_p(eng, grp=grp, t0=t0):
                        for c in range(8):
                            eng.matmul(
                                bufs["ps_a"][:, 0:512],
                                xn_bf[:, c * T + t0:c * T + t0 + 128],
                                wq_sb[:, c * 512:(c + 1) * 512],
                                start=(c == 0), stop=False)
                        return eng.matmul(
                            bufs["ps_a"][:, 0:512], cc.ones_row[0:1, 0:128],
                            bv_bf[0:1, grp * 512:(grp + 1) * 512],
                            start=False, stop=True)
                    p.ph("tensor", mm_p)

                    p.ph("vector", lambda eng: eng.tensor_copy(
                        qtm[:, 0:512], bufs["ps_a"][:, 0:512]))

                    def tr(eng):
                        last = None
                        for j in range(4):
                            last = eng.transpose(
                                ps_tr[:, j * 128:(j + 1) * 128],
                                qtm[:, j * 128:(j + 1) * 128], cc.ident[:])
                        return last
                    p.ph("tensor", tr)

                    def ev(eng, grp=grp, t0=t0, qfm=qfm, kfm=kfm):
                        last = None
                        for j in range(4):
                            pair = grp * 4 + j
                            for half in range(2):
                                srcp = ps_tr[half * 64:half * 64 + 64,
                                             j * 128:(j + 1) * 128]
                                eng.tensor_copy(
                                    kfm[half * 64:half * 64 + 64,
                                        pair * T + t0:pair * T + t0 + 128], srcp)
                                last = eng.tensor_scalar(
                                    qfm[half * 64:half * 64 + 64,
                                        pair * T + t0:pair * T + t0 + 128],
                                    srcp, 0.125, None, ALU.mult)
                        return last
                    p.ph("vector", ev)

                    def evv(eng, grp=grp, tt=tt, vtm=vtm):
                        last = None
                        for j in range(8):
                            h = grp * 8 + j
                            last = eng.tensor_copy(
                                vtm[:, h, tt, 0:64],
                                bufs["ps_a"][:, j * 64:(j + 1) * 64])
                        return last
                    p.ph("vector", evv)

            PT = p.sb("PT", [128, 16 * 256], BF16, scope=sc_attn)
            st.update(PT=PT)
            emit_attn_core(p, cc, st, heads=16, causal=False, window=None, qw=256)

            wproj_sb = p.sb("wproj_sb", [128, 8 * 128], BF16, scope=sc_attn)
            for f in range(8):
                p.dma("gpsimd",
                      [(wproj_sb[:].rearrange("p (c g) -> p c g", c=8),
                        t["wo_t"][:, f * 128:(f + 1) * 128].rearrange(
                            "(c p) g -> p c g", p=128))])
                for tc in range(4):
                    t0 = tc * 512

                    def mm_o(eng, t0=t0, av_fm=av_fm, wproj_sb=wproj_sb):
                        last = None
                        for c in range(8):
                            last = eng.matmul(
                                bufs["ps_a"][:, 0:512],
                                wproj_sb[:, c * 128:(c + 1) * 128],
                                av_fm[:, c * T + t0:c * T + t0 + 512],
                                start=(c == 0), stop=(c == 7))
                        return last
                    p.ph("tensor", mm_o)
                    p.dma("sync", [(bufs["x32"][:, 0:512],
                                    t["x"][s, f * 128:(f + 1) * 128, t0:t0 + 512])])

                    def evac(eng, f=f):
                        return eng.scalar_tensor_tensor(
                            bufs["xev"][:, 0:512], bufs["ps_a"][:, 0:512],
                            bias_col[:, f:f + 1], bufs["x32"][:, 0:512],
                            ALU.add, ALU.add)
                    p.ph("vector", evac)
                    p.dma("sync", [(t["xa"][s, f * 128:(f + 1) * 128, t0:t0 + 512],
                                    bufs["xev"][:, 0:512])])

        with ExitStack() as sc_ffn:
            wr_sb = p.sb("wr_sb", [128, 8 * 128], BF16, scope=sc_ffn)
            cn_sb = p.sb("cn_sb", [16, 128], F32, scope=sc_ffn)
            cnb = p.sb("cnb", [16, 128], BF16, scope=sc_ffn)
            cnT = p.sb("cnT", [128, 16], BF16, scope=sc_ffn)
            ncol = p.sb("ncol", [128, 4], F32, scope=sc_ffn)
            zn_tm = p.sb("zn_tm", [128, 128], BF16, scope=sc_ffn)
            znT = p.sb("znT", [128, 128], BF16, scope=sc_ffn)
            lgt = p.sb("lgt", [128, 16], F32, scope=sc_ffn)
            prow = p.sb("prow", [128, 16], F32, scope=sc_ffn)
            ptmp = p.sb("ptmp", [128, 16], F32, scope=sc_ffn)
            mcol = p.sb("mcol", [128, 4], F32, scope=sc_ffn)
            wgt_sb = p.sb("wgt_sb", [128, 16, 16], F32, scope=sc_ffn)
            tcol = p.sb("tcol", [128, 1], F32, scope=sc_ffn)
            trow = p.sb("trow", [1, 1], F32, scope=sc_ffn)
            junk = p.sb("junk", [128, 512], F32, scope=sc_ffn)

            p.dma("gpsimd", [(wr_sb[:].rearrange("p (c g) -> p c g", c=8),
                              t["wr_t"].rearrange("(c p) g -> p c g", p=128)),
                             (cn_sb[:], t["cent"]),
                             (trow[:], t["temp"])])
            p.ph("scalar", lambda eng: eng.activation(junk[0:16, 0:128],
                                                      cn_sb[:], AF.Square,
                                                      accum_out=ncol[0:16, 0:1]))
            p.ph("scalar", lambda eng: eng.activation(ncol[0:16, 1:2],
                                                      ncol[0:16, 0:1], AF.Sqrt))
            p.ph("vector", lambda eng: eng.tensor_scalar_max(
                ncol[0:16, 1:2], ncol[0:16, 1:2], 1e-12), hard=True)
            p.ph("vector", lambda eng: eng.reciprocal(
                ncol[0:16, 2:3], ncol[0:16, 1:2]), hard=True)
            p.ph("vector", lambda eng: eng.tensor_scalar(
                cnb[:], cn_sb[:], ncol[0:16, 2:3], None, ALU.mult), hard=True)
            p.ph("tensor", lambda eng: eng.transpose(
                ps_tr[:, 0:16], cnb[:], cc.ident[0:16, 0:16]))
            p.ph("vector", lambda eng: eng.tensor_copy(cnT[:], ps_tr[:, 0:16]))
            p.ph("scalar", lambda eng: eng.activation(trow[:], trow[:], AF.Exp))
            p.ph("vector", lambda eng: eng.tensor_scalar(
                trow[:], trow[:], 1.0 / math.sqrt(128.0), None, ALU.mult),
                hard=True)
            tr_bf = p.sb("tr_bf", [1, 1], BF16, scope=sc_ffn)
            p.ph("vector", lambda eng: eng.tensor_copy(tr_bf[:], trow[:]),
                 hard=True)
            p.ph("tensor", lambda eng: eng.matmul(
                bufs["ps_misc"][:, 0:1], cc.ones_row[0:1, :], tr_bf[:],
                start=True, stop=True))
            p.ph("vector", lambda eng: eng.tensor_copy(
                tcol[:], bufs["ps_misc"][:, 0:1]))

            for tt in range(16):
                t0 = tt * 128
                def mm_z(eng, t0=t0):
                    last = None
                    for c in range(8):
                        last = eng.matmul(
                            bufs["ps_a"][:, 0:128],
                            xn_bf[:, c * T + t0:c * T + t0 + 128],
                            wr_sb[:, c * 128:(c + 1) * 128],
                            start=(c == 0), stop=(c == 7))
                    return last
                p.ph("tensor", mm_z)
                p.ph("scalar", lambda eng: eng.activation(
                    junk[:, 0:128], bufs["ps_a"][:, 0:128], AF.Square,
                    accum_out=ncol[:, 0:1]))
                p.ph("scalar", lambda eng: eng.activation(
                    ncol[:, 1:2], ncol[:, 0:1], AF.Sqrt), hard=True)
                p.ph("vector", lambda eng: eng.tensor_scalar_max(
                    ncol[:, 1:2], ncol[:, 1:2], 1e-12))
                p.ph("vector", lambda eng: eng.reciprocal(
                    ncol[:, 2:3], ncol[:, 1:2]), hard=True)
                p.ph("vector", lambda eng: eng.tensor_scalar(
                    zn_tm[:], bufs["ps_a"][:, 0:128], ncol[:, 2:3], None,
                    ALU.mult), hard=True)
                p.ph("tensor", lambda eng: eng.transpose(
                    ps_tr[:, 0:128], zn_tm[:], cc.ident[:]))
                p.ph("vector", lambda eng: eng.tensor_copy(
                    znT[:], ps_tr[:, 0:128]))
                p.ph("tensor", lambda eng: eng.matmul(
                    bufs["ps_misc"][:, 0:16], znT[:], cnT[:],
                    start=True, stop=True))
                p.ph("vector", lambda eng: eng.tensor_scalar(
                    lgt[:], bufs["ps_misc"][:, 0:16], tcol[:, 0:1], None,
                    ALU.mult))
                p.ph("vector", lambda eng: eng.reduce_max(
                    mcol[:, 0:1], lgt[:], axis=AX.X))
                p.ph("vector", lambda eng: eng.tensor_scalar(
                    mcol[:, 0:1], mcol[:, 0:1], -1.0, None, ALU.mult),
                    hard=True)
                p.ph("scalar", lambda eng: eng.activation(
                    prow[:], lgt[:], AF.Exp, bias=mcol[:, 0:1],
                    accum_out=mcol[:, 1:2]))
                p.ph("vector", lambda eng: eng.reciprocal(
                    mcol[:, 2:3], mcol[:, 1:2]))
                p.ph("vector", lambda eng: eng.tensor_scalar(
                    prow[:], prow[:], mcol[:, 2:3], None, ALU.mult), hard=True)
                emit_top2(p, prow, ptmp, mcol, wgt_sb[:, tt, 0:16], 16,
                          renorm=False)

            ie_sb = p.sb("ie_sb", [128, 8 * 512], BF16, scope=sc_ffn)
            oe_sb = p.sb("oe_sb", [128, 4 * 64], BF16, scope=sc_ffn)
            hh = p.sb("hh", [128, 4 * 512], BF16, scope=sc_ffn)
            y_tm = p.sb("y_tm", [128, 4 * D], BF16, scope=sc_ffn)
            yfm = p.sb("yfm", [128, 8 * 512], BF16, scope=sc_ffn)
            for tc in range(4):
                t0 = tc * 512
                for h in range(16):
                    p.dma("gpsimd",
                          [(ie_sb[:, c * 512:(c + 1) * 512],
                            t["ie"][h, c * 128:(c + 1) * 128, :])
                           for c in range(8)] +
                          [(oe_sb[:].rearrange("p (c g) -> p c g", c=4),
                            t["oe"][h].rearrange("(c p) g -> p c g", p=128))])

                    def mm_h(eng, t0=t0):
                        last = None
                        for hc in range(4):
                            for c in range(8):
                                last = eng.matmul(
                                    ps_score[:, hc * 512:(hc + 1) * 512],
                                    ie_sb[:, c * 512 + hc * 128:c * 512 + hc * 128 + 128],
                                    xn_bf[:, c * T + t0:c * T + t0 + 512],
                                    start=(c == 0), stop=(c == 7))
                        return last
                    p.ph("tensor", mm_h)

                    def gelu(eng):
                        last = None
                        for hc in range(4):
                            last = eng.activation(
                                hh[:, hc * 512:(hc + 1) * 512],
                                ps_score[:, hc * 512:(hc + 1) * 512], AF.Gelu)
                        return last
                    p.ph("scalar", gelu)

                    def mm_oe(eng):
                        last = None
                        for ts in range(4):
                            po = bufs["ps_b"][:, ts * 64:(ts + 1) * 64]
                            for hc in range(4):
                                last = eng.matmul(
                                    po,
                                    hh[:, hc * 512 + ts * 128:hc * 512 + ts * 128 + 128],
                                    oe_sb[:, hc * 64:(hc + 1) * 64],
                                    start=(hc == 0), stop=(hc == 3))
                        return last
                    p.ph("tensor", mm_oe)

                    def evy(eng, h=h, tc=tc):
                        last = None
                        for ts in range(4):
                            last = eng.tensor_scalar(
                                y_tm[:, ts * D + h * 64:ts * D + h * 64 + 64],
                                bufs["ps_b"][:, ts * 64:(ts + 1) * 64],
                                wgt_sb[:, tc * 4 + ts, h:h + 1], None, ALU.mult)
                        return last
                    p.ph("vector", evy)

                for tsl in range(4):
                    for half in range(2):
                        def trm(eng, half=half, tsl=tsl):
                            last = None
                            for j in range(4):
                                c = half * 4 + j
                                last = eng.transpose(
                                    ps_tr[:, j * 128:(j + 1) * 128],
                                    y_tm[:, tsl * D + c * 128:tsl * D + (c + 1) * 128],
                                    cc.ident[:])
                            return last
                        p.ph("tensor", trm)

                        def evm(eng, half=half, tsl=tsl):
                            last = None
                            for j in range(4):
                                c = half * 4 + j
                                last = eng.tensor_copy(
                                    yfm[:, c * 512 + tsl * 128:c * 512 + tsl * 128 + 128],
                                    ps_tr[:, j * 128:(j + 1) * 128])
                            return last
                        p.ph("vector", evm)
                for f in range(8):
                    p.dma("sync",
                          [(bufs["x32"][:, 0:512],
                            t["xa"][s, f * 128:(f + 1) * 128, t0:t0 + 512])])

                    def fin(eng, f=f):
                        return eng.tensor_tensor(bufs["xev"][:, 0:512],
                                                 yfm[:, f * 512:(f + 1) * 512],
                                                 bufs["x32"][:, 0:512], ALU.add)
                    p.ph("vector", fin)
                    ev = p.dma("sync",
                               [(t["y"][s, f * 128:(f + 1) * 128, t0:t0 + 512],
                                 bufs["xev"][:, 0:512])])

    p.ph("sync", lambda eng: eng.nop(), waits=[ev])
    p.finalize()
    p.close()
    return nc


def run_layer2(x_fm8, w):
    nc = bass.Bass("TRN2", target_bir_lowering=False)
    build_layer2(nc)
    base = {
        "wv_t": np.ascontiguousarray(np.asarray(w["L2_attn_Wv"]).T),
        "bv_row": np.asarray(w["L2_attn_bv"]).reshape(1, -1).astype(np.float32),
        "wo_t": np.ascontiguousarray(np.asarray(w["L2_attn_Wo"]).T),
        "bo_col": np.ascontiguousarray(
            np.asarray(w["L2_attn_bo"]).reshape(8, 128).T),
        "wr_t": np.ascontiguousarray(np.asarray(w["L2_ffn_w_router"]).T),
        "cent": np.ascontiguousarray(np.asarray(w["L2_ffn_head_centroids"])),
        "temp": np.asarray(w["L2_ffn_temperature"]).reshape(1, 1).astype(np.float32),
        "ie": np.ascontiguousarray(np.asarray(w["L2_ffn_input_experts"])),
        "oe": np.ascontiguousarray(np.asarray(w["L2_ffn_output_experts"])),
    }
    in_maps = []
    for c in range(NCORES):
        m = dict(base)
        m["x"] = x_fm8[c]
        in_maps.append(m)
    res = run_bass_kernel_spmd(nc, in_maps, core_ids=list(range(NCORES)))
    return [res.results[c]["y"] for c in range(NCORES)]



def emit_l3_tiles(p, z, h, ch, par, chain=True):
    t0 = ch * 128
    return p.dma("sync",
                 [(z["qt" + par][:],
                   z["proj_h"][0, t0:t0 + 128, h * 64:h * 64 + 64]),
                  (z["kt" + par][:],
                   z["proj_h"][1, t0:t0 + 128, h * 64:h * 64 + 64]),
                  (z["vt" + par][:],
                   z["proj_h"][2, t0:t0 + 128, h * 64:h * 64 + 64]),
                  (z["at" + par][:],
                   z["proj_h"][3, t0:t0 + 128, h * 64:h * 64 + 64]),
                  (z["art" + par][:],
                   z["proj_h"][4, t0:t0 + 128, h * 64:h * 64 + 64]),
                  (z["bg_sb" + par][:], z["bgh"][t0:t0 + 128, :])],
                 chain=chain)


def emit_l3_chunk(p, cc, z, s, h, ch, par, ld_ev, nxt):
    t0 = ch * 128
    bufs = z["bufs"]
    ps_a, ps_b, ps_misc = bufs["ps_a"], bufs["ps_b"], bufs["ps_misc"]
    ps_score, ps_tr = z["ps_score"], z["ps_tr"]
    qt, kt, vt, at, art, bg_sb = (z["qt" + par], z["kt" + par], z["vt" + par],
                                  z["at" + par], z["art" + par],
                                  z["bg_sb" + par])
    z["qt"], z["kt"], z["vt"], z["at"], z["art"], z["bg_sb"] = (
        qt, kt, vt, at, art, bg_sb)
    opar = "B" if par == "A" else "A"
    nxt_ev = None
    if nxt is not None:
        nxt_ev = emit_l3_tiles(p, z, nxt[0], nxt[1], opar, chain=False)
    if p.tail is not None and ld_ev is not None:
        pass
    ncol = z["ncol"]
    p.ph("scalar", lambda eng: eng.activation(z["scr"][:], z["kt"][:], AF.Square,
                                              accum_out=ncol[:, 0:1]),
         waits=[p.tail, ld_ev])
    p.ph("scalar", lambda eng: eng.activation(ncol[:, 1:2], ncol[:, 0:1],
                                              AF.Sqrt), hard=True)
    p.ph("vector", lambda eng: eng.tensor_scalar_max(ncol[:, 1:2], ncol[:, 1:2],
                                                     1e-12))
    p.ph("vector", lambda eng: eng.reciprocal(ncol[:, 2:3], ncol[:, 1:2]),
         hard=True)
    p.ph("vector", lambda eng: eng.tensor_scalar(z["kn"][:], z["kt"][:],
                                                 ncol[:, 2:3], None, ALU.mult),
         hard=True)

    bcol = {"S": z["bg_sb"][:, h:h + 1], "R": z["bg_sb"][:, 16 + h:16 + h + 1]}
    gate = {"S": z["at"], "R": z["art"]}
    vsrc = {"S": z["vt"], "R": z["r_bf"]}
    state32 = {"S": z["S32"], "R": z["R32"]}
    statebf = {"S": z["S_bf"], "R": z["R_bf"]}
    usb = {"S": z["u_sb"], "R": z["u2_sb"]}
    pcc = {"S": 0, "R": 1}

    for side in ("S", "R"):
        p.ph("scalar", lambda eng, side=side: eng.activation(
            z["lna32"][:], gate[side][:], AF.Ln))
        p.ph("vector", lambda eng: eng.tensor_copy(z["lna_bf"][:], z["lna32"][:]))

        def mm_cum(eng):
            eng.matmul(ps_a[:, 0:64], cc.m_ut[:], z["lna_bf"][:],
                       start=True, stop=True)
            return eng.matmul(ps_b[0:64, 0:128], z["lna_bf"][:], cc.m_ut[:],
                              start=True, stop=True)
        p.ph("tensor", mm_cum)
        p.ph("scalar", lambda eng, side=side: eng.activation(
            z["e1"][:], ps_a[:, 0:64], AF.Exp))
        p.ph("scalar", lambda eng, side=side: eng.activation(
            z["pc_col"][0:64, pcc[side]:pcc[side] + 1], ps_b[0:64, 127:128],
            AF.Exp))
        p.ph("scalar", lambda eng: eng.activation(
            z["e3"][:], ps_a[:, 0:64], AF.Exp, scale=-1.0))
        p.ph("vector", lambda eng: eng.tensor_tensor(
            z["scr"][:], ps_a[:, 0:64], z["lna32"][:], ALU.subtract))
        p.ph("scalar", lambda eng: eng.activation(z["e2"][:], z["scr"][:],
                                                  AF.Exp))
        def prods1(eng, side=side):
            eng.tensor_tensor(z["qpb"][:], z["qt"][:], z["e1"][:], ALU.mult)
            return eng.tensor_tensor(z["scr"][:], z["kn"][:], z["e1"][:],
                                     ALU.mult)
        p.ph("vector", prods1)

        def prods2(eng, side=side):
            eng.tensor_scalar(z["ctb"][:], z["scr"][:], bcol[side], None,
                              ALU.mult)
            last = eng.tensor_tensor(z["ktb"][:], z["kn"][:], z["e3"][:],
                                     ALU.mult)
            if side == "S":
                last = eng.tensor_tensor(z["cpb"][:], z["kn"][:], z["e2"][:],
                                         ALU.mult)
            return last
        p.ph("vector", prods2, hard=True)

        names = ["qpb", "ctb", "ktb"] + (["cpb"] if side == "S" else [])

        def trs(eng, names=tuple(names)):
            last = None
            for i, nm in enumerate(names):
                last = eng.transpose(ps_tr[0:64, i * 128:(i + 1) * 128],
                                     z[nm][:], cc.ident[:])
            return last
        p.ph("tensor", trs)

        def trev(eng, names=tuple(names)):
            last = None
            for i, nm in enumerate(names):
                last = eng.tensor_copy(z[nm + "T"][:],
                                       ps_tr[0:64, i * 128:(i + 1) * 128])
            return last
        p.ph("vector", trev)

        def mm_A(eng, side=side):
            eng.matmul(ps_score[:, 0:128], z["ctbT"][:], z["ktbT"][:],
                       start=True, stop=True)
            eng.matmul(ps_score[:, 512:640], z["ktbT"][:], z["ctbT"][:],
                       start=True, stop=True)
            last = eng.matmul(ps_score[:, 1024:1152], z["ktbT"][:], z["qpbT"][:],
                              start=True, stop=True)
            if side == "S":
                last = eng.matmul(ps_score[:, 1536:1664], z["ktbT"][:],
                                  z["cpbT"][:], start=True, stop=True)
            return last
        p.ph("tensor", mm_A)

        def evA(eng, side=side):
            eng.scalar_tensor_tensor(z["msb0"][:], ps_score[:, 0:128], -1.0,
                                     cc.m_ltS[:], ALU.mult, ALU.mult)
            eng.scalar_tensor_tensor(z["mtb0"][:], ps_score[:, 512:640], -1.0,
                                     cc.m_utS[:], ALU.mult, ALU.mult)
            last = eng.tensor_tensor(z["btb"][:], ps_score[:, 1024:1152],
                                     cc.m_ut[:], ALU.mult)
            if side == "S":
                last = eng.tensor_tensor(z["aptb"][:], ps_score[:, 1536:1664],
                                         cc.m_utS[:], ALU.mult)
            return last
        p.ph("vector", evA)

        p.ph("tensor", lambda eng, side=side: eng.matmul(
            ps_misc[:, 0:64], z["ctbT"][:], statebf[side][:],
            start=True, stop=True))
        p.ph("vector", lambda eng, side=side: eng.scalar_tensor_tensor(
            usb[side][:], vsrc[side][:], bcol[side], ps_misc[:, 0:64],
            ALU.mult, ALU.subtract))

        for j in range(7):
            cur, nxt = (z["msb0"], z["msb1"]) if j % 2 == 0 else (z["msb1"], z["msb0"])
            curT, nxtT = (z["mtb0"], z["mtb1"]) if j % 2 == 0 else (z["mtb1"], z["mtb0"])

            def mm_lvl(eng, cur=cur, curT=curT, side=side, j=j):
                last = eng.matmul(ps_misc[:, 64:128], curT[:], usb[side][:],
                                  start=True, stop=True)
                if j < 6:
                    eng.matmul(ps_score[:, 0:128], curT[:], cur[:],
                               start=True, stop=True)
                    last = eng.matmul(ps_score[:, 512:640], cur[:], curT[:],
                                      start=True, stop=True)
                return last
            p.ph("tensor", mm_lvl)

            def ev_lvl(eng, nxt=nxt, nxtT=nxtT, side=side, j=j):
                last = eng.tensor_tensor(usb[side][:], usb[side][:],
                                         ps_misc[:, 64:128], ALU.add)
                if j < 6:
                    eng.tensor_copy(nxt[:], ps_score[:, 0:128])
                    last = eng.tensor_copy(nxtT[:], ps_score[:, 512:640])
                return last
            p.ph("vector", ev_lvl)

        if side == "S":
            def mm_pred(eng):
                eng.matmul(ps_misc[:, 64:128], z["cpbT"][:], z["S_bf"][:],
                           start=True, stop=False)
                return eng.matmul(ps_misc[:, 64:128], z["aptb"][:], z["u_sb"][:],
                                  start=False, stop=True)
            p.ph("tensor", mm_pred)
            p.ph("vector", lambda eng: eng.tensor_tensor(
                z["scr"][:], z["vt"][:], ps_misc[:, 64:128], ALU.subtract))
            p.ph("vector", lambda eng: eng.tensor_scalar_min(
                z["scr"][:], z["scr"][:], 1.0), hard=True)
            p.ph("vector", lambda eng: eng.tensor_scalar_max(
                z["r_bf"][:], z["scr"][:], -1.0), hard=True)
            def saves(eng):
                eng.tensor_copy(z["btbS"][:], z["btb"][:])
                eng.tensor_copy(z["qpbTS"][:], z["qpbT"][:])
                return eng.tensor_copy(z["ktbS"][:], z["ktb"][:])
            p.ph("vector", saves)

    def mm_out(eng):
        eng.matmul(ps_misc[:, 64:128], z["qpbTS"][:], z["S_bf"][:],
                   start=True, stop=False)
        eng.matmul(ps_misc[:, 64:128], z["btbS"][:], z["u_sb"][:],
                   start=False, stop=False)
        eng.matmul(ps_misc[:, 64:128], z["qpbT"][:], z["R_bf"][:],
                   start=False, stop=False)
        return eng.matmul(ps_misc[:, 64:128], z["btb"][:], z["u2_sb"][:],
                          start=False, stop=True)
    p.ph("tensor", mm_out)
    p.ph("vector", lambda eng: eng.tensor_copy(z["otile"][:], ps_misc[:, 64:128]))
    p.ph("tensor", lambda eng: eng.transpose(ps_tr[0:64, 0:128], z["otile"][:],
                                             cc.ident[:]))
    p.ph("vector", lambda eng: eng.tensor_copy(z["ofm_st"][:],
                                               ps_tr[0:64, 0:128]))
    p.dma("sync", [(z["ofm_h"][(h % 2) * 64:(h % 2) * 64 + 64,
                               (h // 2) * T + t0:(h // 2) * T + t0 + 128],
                    z["ofm_st"][:])])

    def mm_upd(eng):
        eng.matmul(ps_misc[0:64, 128:192], z["ktbS"][:], z["u_sb"][:],
                   start=True, stop=True)
        return eng.matmul(ps_misc[0:64, 192:256], z["ktb"][:], z["u2_sb"][:],
                          start=True, stop=True)
    p.ph("tensor", mm_upd)

    def upd(eng):
        eng.tensor_tensor(z["S32"][:], z["S32"][:], ps_misc[0:64, 128:192],
                          ALU.add)
        return eng.tensor_tensor(z["R32"][:], z["R32"][:],
                                 ps_misc[0:64, 192:256], ALU.add)
    p.ph("vector", upd)

    def upd2(eng):
        eng.tensor_scalar(z["S32"][:], z["S32"][:],
                          z["pc_col"][0:64, 0:1], None, ALU.mult)
        return eng.tensor_scalar(z["R32"][:], z["R32"][:],
                                 z["pc_col"][0:64, 1:2], None, ALU.mult)
    p.ph("vector", upd2, hard=True)

    def updbf(eng):
        eng.tensor_copy(z["S_bf"][:], z["S32"][:])
        return eng.tensor_copy(z["R_bf"][:], z["R32"][:])
    p.ph("vector", updbf, hard=True)
    return nxt_ev


def build_layer3_attn(nc):
    t = {}
    t["x"] = nc.dram_tensor("x", [2, D, T], F32, kind="ExternalInput").ap()
    for nm in ("wq_t", "wk_t", "wv_t", "wa_t", "war_t", "wo_t"):
        t[nm] = nc.dram_tensor(nm, [D, D], F32, kind="ExternalInput").ap()
    t["wbg_t"] = nc.dram_tensor("wbg_t", [D, 32], F32, kind="ExternalInput").ap()
    t["ba_row"] = nc.dram_tensor("ba_row", [1, D], F32, kind="ExternalInput").ap()
    t["bar_row"] = nc.dram_tensor("bar_row", [1, D], F32, kind="ExternalInput").ap()
    t["bbg_row"] = nc.dram_tensor("bbg_row", [1, 32], F32, kind="ExternalInput").ap()
    t["lnw_row"] = nc.dram_tensor("lnw_row", [1, D], F32, kind="ExternalInput").ap()
    t["lnb_col"] = nc.dram_tensor("lnb_col", [128, 8], F32, kind="ExternalInput").ap()
    t["xnh"] = nc.dram_tensor("xnh", [2, 128, 8 * T], BF16, kind="Internal").ap()
    t["proj_h"] = nc.dram_tensor("proj_h", [2, 5, T, D], BF16, kind="Internal").ap()
    t["bgh"] = nc.dram_tensor("bgh", [2, T, 32], F32, kind="Internal").ap()
    t["ofm_h"] = nc.dram_tensor("ofm_h", [2, 128, 8 * T], BF16, kind="Internal").ap()
    t["y"] = nc.dram_tensor("y", [2, D, T], F32, kind="ExternalOutput").ap()

    p = Prog(nc)
    cc = Consts(p)
    W = 512
    bufs = {
        "x32": p.sb("x32", [128, W], F32),
        "xev": p.sb("xev", [128, W], F32),
        "xb": p.sb("xb", [128, W], BF16),
        "sq": p.sb("sq", [128, W], BF16),
        "tmp32": p.sb("tmp32", [128, W], F32),
        "mrow": p.sb("mrow", [1, W], F32),
        "vrow": p.sb("vrow", [1, W], F32),
        "mrow_bf": p.sb("mrow_bf", [1, W], BF16),
        "vrow_bf": p.sb("vrow_bf", [1, W], BF16),
        "ps_a": p.ps("ps_a", [128, 512], F32),
        "ps_b": p.ps("ps_b", [128, 512], F32),
        "ps_misc": p.ps("ps_misc", [128, 512], F32),
    }
    ps_score = p.ps("ps_score", [128, 2048], F32)
    ps_tr = p.ps("ps_tr", [128, 512], BF16)

    lnw_row = p.sb("lnw_row", [1, D], BF16)
    lnb_col = p.sb("lnb_col", [128, 8], F32)
    ba_row = p.sb("ba_row", [1, D], BF16)
    bar_row = p.sb("bar_row", [1, D], BF16)
    bbg_row = p.sb("bbg_row", [1, 32], BF16)
    p.dma("gpsimd", [(lnw_row[:], t["lnw_row"]), (ba_row[:], t["ba_row"]),
                     (bar_row[:], t["bar_row"]), (bbg_row[:], t["bbg_row"])])
    p.dma("sync", [(lnb_col[:], t["lnb_col"])])

    for s in range(2):
        emit_ln(p, cc, t["x"][s], t["xnh"][s], bufs, dst_hbm=True)
        with ExitStack() as sc_proj:
            xn2 = p.sb("xn2", [128, 8 * T], BF16, scope=sc_proj)
            emit_ln(p, cc, t["xnh"][s], xn2, bufs, w_row=lnw_row,
                    src_bf16=True, b_col=lnb_col)
            wq_sb = p.sb("wq_sb", [128, 8 * 512], BF16, scope=sc_proj)
            ptile = p.sb("ptile", [128, 512], BF16, scope=sc_proj)
            projs = [("wq_t", 0, None), ("wk_t", 1, None), ("wv_t", 2, None),
                     ("wa_t", 3, ba_row), ("war_t", 4, bar_row)]
            for wname, pi, brow in projs:
                for grp in range(2):
                    p.dma("gpsimd",
                          [(wq_sb[:, c * 512:(c + 1) * 512],
                            t[wname][c * 128:(c + 1) * 128,
                                     grp * 512:(grp + 1) * 512])
                           for c in range(8)])
                    for tt in range(16):
                        t0 = tt * 128

                        def mm_p2(eng, grp=grp, t0=t0, brow=brow, wq_sb=wq_sb,
                                  xn2=xn2):
                            last = None
                            for c in range(8):
                                last = eng.matmul(
                                    bufs["ps_a"][:, 0:512],
                                    xn2[:, c * T + t0:c * T + t0 + 128],
                                    wq_sb[:, c * 512:(c + 1) * 512],
                                    start=(c == 0),
                                    stop=(c == 7 and brow is None))
                            if brow is not None:
                                last = eng.matmul(
                                    bufs["ps_a"][:, 0:512],
                                    cc.ones_row[0:1, 0:128],
                                    brow[0:1, grp * 512:(grp + 1) * 512],
                                    start=False, stop=True)
                            return last
                        p.ph("tensor", mm_p2)

                        if brow is None:
                            p.ph("vector", lambda eng, ptile=ptile:
                                 eng.tensor_copy(ptile[:],
                                                 bufs["ps_a"][:, 0:512]))
                        else:
                            p.ph("scalar", lambda eng, ptile=ptile:
                                 eng.activation(ptile[:], bufs["ps_a"][:, 0:512],
                                                AF.Sigmoid))
                        p.dma("sync",
                              [(t["proj_h"][s, pi, t0:t0 + 128,
                                            grp * 512:(grp + 1) * 512],
                                ptile[:])])
            wbg_sb = p.sb("wbg_sb", [128, 8, 32], BF16, scope=sc_proj)
            pbg = p.sb("pbg", [128, 32], F32, scope=sc_proj)
            p.dma("gpsimd", [(wbg_sb[:],
                              t["wbg_t"].rearrange("(c p) e -> p c e", p=128))])
            for tt in range(16):
                t0 = tt * 128

                def mm_bg(eng, t0=t0, wbg_sb=wbg_sb, xn2=xn2):
                    for c in range(8):
                        eng.matmul(bufs["ps_a"][:, 0:32],
                                   xn2[:, c * T + t0:c * T + t0 + 128],
                                   wbg_sb[:, c, :], start=(c == 0), stop=False)
                    return eng.matmul(bufs["ps_a"][:, 0:32],
                                      cc.ones_row[0:1, 0:128], bbg_row[0:1, :],
                                      start=False, stop=True)
                p.ph("tensor", mm_bg)
                p.ph("scalar", lambda eng, pbg=pbg: eng.activation(
                    pbg[:], bufs["ps_a"][:, 0:32], AF.Sigmoid))
                p.dma("sync", [(t["bgh"][s, t0:t0 + 128, :], pbg[:])])

        with ExitStack() as sc_scan:
            z = {"bufs": bufs, "ps_score": ps_score, "ps_tr": ps_tr,
                 "proj_h": t["proj_h"][s], "bgh": t["bgh"][s],
                 "ofm_h": t["ofm_h"][s]}
            for nm in ("qtA", "ktA", "vtA", "atA", "artA",
                       "qtB", "ktB", "vtB", "atB", "artB",
                       "kn", "lna_bf", "qpb",
                       "ctb", "ktb", "cpb", "r_bf", "ktbS"):
                z[nm] = p.sb(nm, [128, 64], BF16, scope=sc_scan)
            z["bg_sbA"] = p.sb("bg_sbA", [128, 32], F32, scope=sc_scan)
            z["bg_sbB"] = p.sb("bg_sbB", [128, 32], F32, scope=sc_scan)
            for nm in ("scr", "lna32", "e1", "e2", "e3"):
                z[nm] = p.sb(nm, [128, 64], F32, scope=sc_scan)
            for nm in ("qpbT", "ctbT", "ktbT", "cpbT", "qpbTS"):
                z[nm] = p.sb(nm, [64, 128], BF16, scope=sc_scan)
            for nm in ("msb0", "msb1", "mtb0", "mtb1", "btb", "btbS", "aptb"):
                z[nm] = p.sb(nm, [128, 128], BF16, scope=sc_scan)
            z["u_sb"] = p.sb("u_sb", [128, 64], BF16, scope=sc_scan)
            z["u2_sb"] = p.sb("u2_sb", [128, 64], BF16, scope=sc_scan)
            z["otile"] = p.sb("otile", [128, 64], BF16, scope=sc_scan)
            z["ofm_st"] = p.sb("ofm_st", [64, 128], BF16, scope=sc_scan)
            z["S32"] = p.sb("S32", [64, 64], F32, scope=sc_scan)
            z["R32"] = p.sb("R32", [64, 64], F32, scope=sc_scan)
            z["S_bf"] = p.sb("S_bf", [64, 64], BF16, scope=sc_scan)
            z["R_bf"] = p.sb("R_bf", [64, 64], BF16, scope=sc_scan)
            z["ncol"] = p.sb("ncol", [128, 4], F32, scope=sc_scan)
            z["pc_col"] = p.sb("pc_col", [64, 2], F32, scope=sc_scan)
            units = [(h, ch) for h in range(16) for ch in range(16)]
            ld_ev = emit_l3_tiles(p, z, 0, 0, "A", chain=True)
            for ui, (h, ch) in enumerate(units):
                if ch == 0:
                    def init(eng):
                        eng.memset(z["S32"][:], 0.0)
                        eng.memset(z["R32"][:], 0.0)
                        eng.memset(z["S_bf"][:], 0.0)
                        return eng.memset(z["R_bf"][:], 0.0)
                    p.ph("vector", init)
                par = "A" if ui % 2 == 0 else "B"
                nxt = units[ui + 1] if ui + 1 < len(units) else None
                nxt_ld = emit_l3_chunk(p, cc, z, s, h, ch, par, ld_ev, nxt)
                ld_ev = nxt_ld

        with ExitStack() as sc_out:
            wproj_sb = p.sb("wproj_sb", [128, 8 * 128], BF16, scope=sc_out)
            ofm_sb = p.sb("ofm_sb", [128, 8 * 512], BF16, scope=sc_out)
            xnb_sb = p.sb("xnb_sb", [128, 512], BF16, scope=sc_out)
            for tc in range(4):
                t0 = tc * 512
                p.dma("sync", [(ofm_sb[:, c * 512:(c + 1) * 512],
                                t["ofm_h"][s][:, c * T + t0:c * T + t0 + 512])
                               for c in range(8)])
                for f in range(8):
                    p.dma("gpsimd",
                          [(wproj_sb[:].rearrange("p (c g) -> p c g", c=8),
                            t["wo_t"][:, f * 128:(f + 1) * 128].rearrange(
                                "(c p) g -> p c g", p=128))])

                    def mm_o(eng, ofm_sb=ofm_sb, wproj_sb=wproj_sb):
                        last = None
                        for c in range(8):
                            last = eng.matmul(
                                bufs["ps_a"][:, 0:512],
                                wproj_sb[:, c * 128:(c + 1) * 128],
                                ofm_sb[:, c * 512:(c + 1) * 512],
                                start=(c == 0), stop=(c == 7))
                        return last
                    p.ph("tensor", mm_o)
                    p.dma("sync",
                          [(bufs["x32"][:, 0:512],
                            t["x"][s, f * 128:(f + 1) * 128, t0:t0 + 512]),
                           (xnb_sb[:],
                            t["xnh"][s][:, f * T + t0:f * T + t0 + 512])])

                    def fin(eng):
                        eng.tensor_tensor(bufs["xev"][:, 0:512],
                                          bufs["ps_a"][:, 0:512], xnb_sb[:],
                                          ALU.add)
                        return eng.tensor_tensor(bufs["xev"][:, 0:512],
                                                 bufs["xev"][:, 0:512],
                                                 bufs["x32"][:, 0:512], ALU.add)
                    p.ph("vector", fin)
                    ev = p.dma("sync",
                               [(t["y"][s, f * 128:(f + 1) * 128, t0:t0 + 512],
                                 bufs["xev"][:, 0:512])])

    p.ph("sync", lambda eng: eng.nop(), waits=[ev])
    p.finalize()
    p.close()
    return nc


def run_layer3_attn(x_fm8, w):
    nc = bass.Bass("TRN2", target_bir_lowering=False)
    build_layer3_attn(nc)
    wbg = np.concatenate([np.asarray(w["L3_attn_Wb"]), np.asarray(w["L3_attn_Wg"])],
                         axis=0)
    bbg = np.concatenate([np.asarray(w["L3_attn_bb"]), np.asarray(w["L3_attn_bg"])])
    base = {
        "wq_t": np.ascontiguousarray(np.asarray(w["L3_attn_Wq"]).T),
        "wk_t": np.ascontiguousarray(np.asarray(w["L3_attn_Wk"]).T),
        "wv_t": np.ascontiguousarray(np.asarray(w["L3_attn_Wv"]).T),
        "wa_t": np.ascontiguousarray(np.asarray(w["L3_attn_Wa"]).T),
        "war_t": np.ascontiguousarray(np.asarray(w["L3_attn_WaR"]).T),
        "wo_t": np.ascontiguousarray(np.asarray(w["L3_attn_Wo"]).T),
        "wbg_t": np.ascontiguousarray(wbg.T),
        "ba_row": np.asarray(w["L3_attn_ba"]).reshape(1, -1).astype(np.float32),
        "bar_row": np.asarray(w["L3_attn_baR"]).reshape(1, -1).astype(np.float32),
        "bbg_row": np.ascontiguousarray(bbg.reshape(1, -1).astype(np.float32)),
        "lnw_row": np.asarray(w["L3_attn_ln_w"]).reshape(1, -1).astype(np.float32),
        "lnb_col": np.ascontiguousarray(
            np.asarray(w["L3_attn_ln_b"]).reshape(8, 128).T),
    }
    in_maps = []
    for c in range(NCORES):
        m = dict(base)
        m["x"] = x_fm8[c]
        in_maps.append(m)
    res = run_bass_kernel_spmd(nc, in_maps, core_ids=list(range(NCORES)))
    return [res.results[c]["y"] for c in range(NCORES)]



L3_CAP = 5120.0


def build_layer3_ffn(nc):
    t = {}
    t["x"] = nc.dram_tensor("x", [2, D, T], F32, kind="ExternalInput").ap()
    t["xa"] = nc.dram_tensor("xa", [2, D, T], F32, kind="ExternalInput").ap()
    t["xfull"] = nc.dram_tensor("xfull", [16, T, D], F32, kind="ExternalInput").ap()
    t["xown"] = nc.dram_tensor("xown", [2, T, D], F32, kind="ExternalInput").ap()
    t["prefrep"] = nc.dram_tensor("prefrep", [1, 256], F32, kind="ExternalInput").ap()
    t["rw_t"] = nc.dram_tensor("rw_t", [D, 8], F32, kind="ExternalInput").ap()
    t["sg"] = nc.dram_tensor("sg", [D, 8192], F32, kind="ExternalInput").ap()
    t["su"] = nc.dram_tensor("su", [D, 8192], F32, kind="ExternalInput").ap()
    t["sd"] = nc.dram_tensor("sd", [8192, D], F32, kind="ExternalInput").ap()
    t["rg"] = nc.dram_tensor("rg", [8, D, 4096], F32, kind="ExternalInput").ap()
    t["ru"] = nc.dram_tensor("ru", [8, D, 4096], F32, kind="ExternalInput").ap()
    t["rd"] = nc.dram_tensor("rd", [8, 4096, D], F32, kind="ExternalInput").ap()
    t["xnh"] = nc.dram_tensor("xnh", [2, 128, 8 * T], BF16, kind="Internal").ap()
    t["wgt_h"] = nc.dram_tensor("wgt_h", [2, T, 8], F32, kind="Internal").ap()
    t["macc"] = nc.dram_tensor("macc", [2, T, D], F32, kind="Internal").ap()
    t["y"] = nc.dram_tensor("y", [2, D, T], F32, kind="ExternalOutput").ap()

    p = Prog(nc)
    cc = Consts(p)
    W = 512
    bufs = {
        "x32": p.sb("x32", [128, W], F32),
        "xev": p.sb("xev", [128, W], F32),
        "xb": p.sb("xb", [128, W], BF16),
        "sq": p.sb("sq", [128, W], BF16),
        "tmp32": p.sb("tmp32", [128, W], F32),
        "mrow": p.sb("mrow", [1, W], F32),
        "vrow": p.sb("vrow", [1, W], F32),
        "mrow_bf": p.sb("mrow_bf", [1, W], BF16),
        "vrow_bf": p.sb("vrow_bf", [1, W], BF16),
        "ps_a": p.ps("ps_a", [128, 512], F32),
        "ps_b": p.ps("ps_b", [128, 512], F32),
        "ps_misc": p.ps("ps_misc", [128, 512], F32),
    }
    ps_score = p.ps("ps_score", [128, 2048], F32)
    ps_tr = p.ps("ps_tr", [128, 512], BF16)

    for s in range(2):
        emit_ln(p, cc, t["x"][s], t["xnh"][s], bufs, dst_hbm=True)

    with ExitStack() as sc_r:
        xt = p.sb("xt", [128, D], F32, scope=sc_r)
        xnt = p.sb("xnt", [128, D], BF16, scope=sc_r)
        xnr = p.sb("xnr", [128, D], BF16, scope=sc_r)
        xfmt = p.sb("xfmt", [128, 8 * 128], BF16, scope=sc_r)
        rw_sb = p.sb("rw_sb", [128, 8, 8], BF16, scope=sc_r)
        junk = p.sb("junk", [128, D], F32, scope=sc_r)
        ncol = p.sb("ncol", [128, 8], F32, scope=sc_r)
        prow = p.sb("prow", [128, 8], F32, scope=sc_r)
        ptmp = p.sb("ptmp", [128, 8], F32, scope=sc_r)
        mcol = p.sb("mcol", [128, 4], F32, scope=sc_r)
        wdw = p.sb("wdw", [128, 8], F32, scope=sc_r)
        ind = p.sb("ind", [128, 8], F32, scope=sc_r)
        ind_bf = p.sb("ind_bf", [128, 8], BF16, scope=sc_r)
        base32 = p.sb("base32", [1, 8], F32, scope=sc_r)
        base_bf = p.sb("base_bf", [1, 8], BF16, scope=sc_r)
        vm = p.sb("vm", [128, 8], F32, scope=sc_r)
        cnt_row = p.sb("cnt_row", [1, 128], F32, scope=sc_r)
        pmrep = p.sb("pmrep", [1, 256], F32, scope=sc_r)
        pbase = p.sb("pbase", [1, 16], F32, scope=sc_r)

        p.dma("gpsimd", [(rw_sb[:], t["rw_t"].rearrange("(c p) e -> p c e", p=128))])
        p.dma("sync", [(pmrep[:], t["prefrep"])])

        eps32 = float(np.finfo(np.float32).eps)

        def emit_route_tile(src_ap, store_slot, g, tt):
            t0 = tt * 128
            p.dma("sync", [(xt[:], src_ap)])
            p.ph("vector", lambda eng: eng.reduce_sum(ncol[:, 0:1], xt[:],
                                                      axis=AX.X))
            p.ph("scalar", lambda eng: eng.activation(
                junk[:], xt[:], AF.Square, accum_out=ncol[:, 1:2]))

            def st1(eng):
                eng.tensor_scalar(ncol[:, 0:1], ncol[:, 0:1], 1.0 / D,
                                  None, ALU.mult)
                return eng.tensor_scalar(ncol[:, 1:2], ncol[:, 1:2],
                                         1.0 / D, None, ALU.mult)
            p.ph("vector", st1, hard=True)

            p.ph("vector", lambda eng: eng.tensor_tensor(
                ncol[:, 2:3], ncol[:, 0:1], ncol[:, 0:1], ALU.mult), hard=True)
            p.ph("vector", lambda eng: eng.tensor_tensor(
                ncol[:, 2:3], ncol[:, 1:2], ncol[:, 2:3], ALU.subtract),
                hard=True)
            p.ph("vector", lambda eng: eng.tensor_scalar(
                ncol[:, 2:3], ncol[:, 2:3], 1e-5, None, ALU.add), hard=True)
            p.ph("scalar", lambda eng: eng.activation(
                ncol[:, 3:4], ncol[:, 2:3], AF.Sqrt))
            p.ph("vector", lambda eng: eng.reciprocal(
                ncol[:, 4:5], ncol[:, 3:4]))

            def nrm(eng):
                return eng.tensor_scalar(junk[:], xt[:], ncol[:, 0:1], None,
                                         ALU.subtract)
            p.ph("vector", nrm, hard=True)
            p.ph("vector", lambda eng: eng.tensor_scalar(
                xnt[:], junk[:], ncol[:, 4:5], None, ALU.mult), hard=True)
            p.ph("scalar", lambda eng: eng.activation(
                junk[:], xnt[:], AF.Square, accum_out=ncol[:, 5:6]), hard=True)
            p.ph("vector", lambda eng: eng.tensor_scalar(
                ncol[:, 5:6], ncol[:, 5:6], 1.0 / D, None, ALU.mult),
                hard=True)
            p.ph("vector", lambda eng: eng.tensor_scalar(
                ncol[:, 5:6], ncol[:, 5:6], eps32, None, ALU.add), hard=True)
            p.ph("scalar", lambda eng: eng.activation(
                ncol[:, 6:7], ncol[:, 5:6], AF.Sqrt))
            p.ph("vector", lambda eng: eng.reciprocal(
                ncol[:, 7:8], ncol[:, 6:7]))
            p.ph("vector", lambda eng: eng.tensor_scalar(
                xnr[:], xnt[:], ncol[:, 7:8], None, ALU.mult), hard=True)

            for hseg in range(2):
                def trx(eng, hseg=hseg):
                    last = None
                    for j in range(4):
                        last = eng.transpose(
                            ps_tr[:, j * 128:(j + 1) * 128],
                            xnr[:, hseg * 512 + j * 128:hseg * 512 + (j + 1) * 128],
                            cc.ident[:])
                    return last
                p.ph("tensor", trx)
                p.ph("vector", lambda eng, hseg=hseg: eng.tensor_copy(
                    xfmt[:, hseg * 512:(hseg + 1) * 512], ps_tr[:, 0:512]))

            def mm_r(eng):
                last = None
                for c in range(8):
                    last = eng.matmul(bufs["ps_misc"][:, 0:8],
                                      xfmt[:, c * 128:(c + 1) * 128],
                                      rw_sb[:, c, :],
                                      start=(c == 0), stop=(c == 7))
                return last
            p.ph("tensor", mm_r)

            p.ph("vector", lambda eng: eng.reduce_max(
                mcol[:, 0:1], bufs["ps_misc"][:, 0:8], axis=AX.X))
            p.ph("vector", lambda eng: eng.tensor_scalar(
                mcol[:, 0:1], mcol[:, 0:1], -1.0, None, ALU.mult), hard=True)
            p.ph("scalar", lambda eng: eng.activation(
                prow[:], bufs["ps_misc"][:, 0:8], AF.Exp, bias=mcol[:, 0:1],
                accum_out=mcol[:, 1:2]))
            p.ph("vector", lambda eng: eng.reciprocal(
                mcol[:, 2:3], mcol[:, 1:2]))
            p.ph("vector", lambda eng: eng.tensor_scalar(
                prow[:], prow[:], mcol[:, 2:3], None, ALU.mult), hard=True)
            emit_top2(p, prow, ptmp, mcol, wdw[:, 0:8], 8, renorm=True,
                      ind_dst=ind[:, 0:8])
            p.ph("vector", lambda eng: eng.tensor_copy(ind_bf[:], ind[:]),
                 hard=True)

            if store_slot is not None:
                def mm_rank(eng):
                    eng.matmul(bufs["ps_a"][:, 0:8], cc.m_utS[:], ind_bf[:],
                               start=True, stop=False)
                    return eng.matmul(bufs["ps_a"][:, 0:8],
                                      cc.ones_row[0:1, :], base_bf[0:1, :],
                                      start=False, stop=True)
                p.ph("tensor", mm_rank)

                def valid(eng):
                    return eng.tensor_scalar(vm[:], bufs["ps_a"][:, 0:8],
                                             L3_CAP, None, ALU.is_lt)
                p.ph("vector", valid)
                p.ph("vector", lambda eng: eng.tensor_tensor(
                    vm[:], vm[:], wdw[:], ALU.mult), hard=True)
                p.dma("sync", [(t["wgt_h"][store_slot, t0:t0 + 128, :], vm[:])])

            p.ph("tensor", lambda eng: eng.matmul(
                bufs["ps_b"][0:1, 0:8], cc.ones_col[:], ind_bf[:],
                start=True, stop=True))

            def upd(eng):
                return eng.tensor_tensor(base32[:], base32[:],
                                         bufs["ps_b"][0:1, 0:8], ALU.add)
            p.ph("vector", upd, hard=True)
            p.ph("vector", lambda eng: eng.tensor_copy(base_bf[:], base32[:]),
                 hard=True)

        for g in range(16):
            p.ph("vector", lambda eng: (eng.memset(base32[:], 0.0),
                                        eng.memset(base_bf[:], 0.0))[-1])
            for tt in range(16):
                emit_route_tile(t["xfull"][g, tt * 128:(tt + 1) * 128, :],
                                None, g, tt)
            p.ph("vector", lambda eng, g=g: eng.tensor_copy(
                cnt_row[0:1, g * 8:(g + 1) * 8], base32[0:1, :]), hard=True)

        for sl in range(2):
            p.ph("vector", lambda eng, sl=sl: eng.tensor_tensor(
                junk[0:1, 0:128], cnt_row[0:1, :],
                pmrep[0:1, sl * 128:(sl + 1) * 128], ALU.mult), hard=True)
            p.ph("vector", lambda eng, sl=sl: eng.reduce_sum(
                pbase[0:1, sl * 8:(sl + 1) * 8],
                junk[0:1, 0:128].rearrange("p (g e) -> p e g", e=8),
                axis=AX.X), hard=True)

        for sl in range(2):
            def initb(eng, sl=sl):
                eng.tensor_copy(base32[0:1, :], pbase[0:1, sl * 8:(sl + 1) * 8])
                return eng.tensor_copy(base_bf[0:1, :],
                                       pbase[0:1, sl * 8:(sl + 1) * 8])
            p.ph("vector", initb, hard=True)
            for tt in range(16):
                emit_route_tile(t["xown"][sl, tt * 128:(tt + 1) * 128, :],
                                sl, None, tt)

    slabs = []
    for q in range(4):
        slabs.append(("s", q, 8))
    for e in range(8):
        for hf in range(2):
            slabs.append(("r", (e, hf), e))

    with ExitStack() as sc_s:
        wg_sb = p.sb("wg_sb", [128, 8 * 2048], BF16, scope=sc_s)
        wu_sb = p.sb("wu_sb", [128, 8 * 2048], BF16, scope=sc_s)
        wd_sb = p.sb("wd_sb", [128, 16 * D], BF16, scope=sc_s)
        h_fm = p.sb("h_fm", [128, 16 * 512], BF16, scope=sc_s)
        ubuf = p.sb("ubuf", [128, 4 * 512], BF16, scope=sc_s)
        xch = p.sb("xch", [128, 8 * 512], BF16, scope=sc_s)
        wgt3 = p.sb("wgt3", [128, 16, 9], F32, scope=sc_s)
        maccbs = [p.sb("maccb", [128, 512], F32, scope=sc_s) for _ in range(2)]
        macc_evs = [None, None]
        dnct = [0]

        for si, (kind, idx, widx) in enumerate(slabs):
            if kind == "s":
                q = idx
                gsrc = t["sg"][:, q * 2048:(q + 1) * 2048]
                usrc = t["su"][:, q * 2048:(q + 1) * 2048]
                dsrc = t["sd"][q * 2048:(q + 1) * 2048, :]
            else:
                e, hf = idx
                gsrc = t["rg"][e][:, hf * 2048:(hf + 1) * 2048]
                usrc = t["ru"][e][:, hf * 2048:(hf + 1) * 2048]
                dsrc = t["rd"][e][hf * 2048:(hf + 1) * 2048, :]
            p.dma("gpsimd",
                  [(wg_sb[:, c * 2048:(c + 1) * 2048],
                    gsrc[c * 128:(c + 1) * 128, :]) for c in range(8)] +
                  [(wu_sb[:, c * 2048:(c + 1) * 2048],
                    usrc[c * 128:(c + 1) * 128, :]) for c in range(8)] +
                  [(wd_sb[:, ic * D:(ic + 1) * D],
                    dsrc[ic * 128:(ic + 1) * 128, :]) for ic in range(16)])
            for s in range(2):
                p.dma("sync", [(wgt3[:, :, 0:8],
                                t["wgt_h"][s].rearrange("(tt p) e -> p tt e",
                                                        p=128))])
                p.ph("vector", lambda eng: eng.memset(wgt3[:, :, 8:9], 1.0))
                for tc in range(4):
                    t0 = tc * 512
                    p.dma("sync", [(xch[:, c * 512:(c + 1) * 512],
                                    t["xnh"][s][:, c * T + t0:c * T + t0 + 512])
                                   for c in range(8)])
                    for ib in range(4):
                        def mm_g(eng, ib=ib):
                            last = None
                            for j in range(4):
                                ic = ib * 4 + j
                                for c in range(8):
                                    last = eng.matmul(
                                        ps_score[:, j * 512:(j + 1) * 512],
                                        wg_sb[:, c * 2048 + ic * 128:c * 2048 + ic * 128 + 128],
                                        xch[:, c * 512:(c + 1) * 512],
                                        start=(c == 0), stop=(c == 7))
                            return last
                        p.ph("tensor", mm_g)

                        def silu(eng, ib=ib):
                            last = None
                            for j in range(4):
                                ic = ib * 4 + j
                                last = eng.activation(
                                    h_fm[:, ic * 512:(ic + 1) * 512],
                                    ps_score[:, j * 512:(j + 1) * 512], AF.Silu)
                            return last
                        p.ph("scalar", silu)

                        def mm_u(eng, ib=ib):
                            last = None
                            for j in range(4):
                                ic = ib * 4 + j
                                for c in range(8):
                                    last = eng.matmul(
                                        ps_score[:, j * 512:(j + 1) * 512],
                                        wu_sb[:, c * 2048 + ic * 128:c * 2048 + ic * 128 + 128],
                                        xch[:, c * 512:(c + 1) * 512],
                                        start=(c == 0), stop=(c == 7))
                            return last
                        p.ph("tensor", mm_u)

                        def ev_u(eng):
                            last = None
                            for j in range(4):
                                last = eng.tensor_copy(
                                    ubuf[:, j * 512:(j + 1) * 512],
                                    ps_score[:, j * 512:(j + 1) * 512])
                            return last
                        p.ph("vector", ev_u)

                        def mul_h(eng, ib=ib):
                            last = None
                            for j in range(4):
                                ic = ib * 4 + j
                                ap = h_fm[:, ic * 512:(ic + 1) * 512]
                                last = eng.tensor_tensor(
                                    ap, ap, ubuf[:, j * 512:(j + 1) * 512],
                                    ALU.mult)
                            return last
                        p.ph("vector", mul_h)

                    for ts in range(4):
                        for fg in range(2):
                            def mm_d(eng, ts=ts, fg=fg):
                                last = None
                                for ic in range(16):
                                    last = eng.matmul(
                                        bufs["ps_a"][:, 0:512],
                                        h_fm[:, ic * 512 + ts * 128:ic * 512 + ts * 128 + 128],
                                        wd_sb[:, ic * D + fg * 512:ic * D + fg * 512 + 512],
                                        start=(ic == 0), stop=(ic == 15))
                                return last
                            p.ph("tensor", mm_d)
                            par = dnct[0] % 2
                            dnct[0] += 1
                            mb = maccbs[par]
                            p.ph("vector", lambda eng, tc=tc, ts=ts, widx=widx,
                                 mb=mb:
                                 eng.tensor_scalar(mb[:],
                                                   bufs["ps_a"][:, 0:512],
                                                   wgt3[:, tc * 4 + ts,
                                                        widx:widx + 1],
                                                   None, ALU.mult),
                                 waits=[p.tail, macc_evs[par]])
                            tglob = t0 + ts * 128
                            macc_evs[par] = p.dma(
                                "gpsimd",
                                [(t["macc"][s, tglob:tglob + 128,
                                            fg * 512:(fg + 1) * 512], mb[:])],
                                accum=(si > 0), chain=False, waits=[p.tail])

    with ExitStack() as sc_f:
        moefm = p.sb("moefm", [128, 8 * 512], BF16, scope=sc_f)
        m32 = p.sb("m32", [128, D], F32, scope=sc_f)
        mbf = p.sb("mbf", [128, D], BF16, scope=sc_f)
        first_f = [True]
        for s in range(2):
            for tc in range(4):
                t0 = tc * 512
                for tsl in range(4):
                    tg = t0 + tsl * 128
                    if first_f[0]:
                        first_f[0] = False
                        p.dma("sync", [(m32[:], t["macc"][s, tg:tg + 128, :])],
                              waits=[p.tail] + [e for e in macc_evs if e])
                    else:
                        p.dma("sync", [(m32[:], t["macc"][s, tg:tg + 128, :])])
                    p.ph("vector", lambda eng: eng.tensor_copy(mbf[:], m32[:]))
                    for half in range(2):
                        def trm(eng, half=half):
                            last = None
                            for j in range(4):
                                c = half * 4 + j
                                last = eng.transpose(
                                    ps_tr[:, j * 128:(j + 1) * 128],
                                    mbf[:, c * 128:(c + 1) * 128], cc.ident[:])
                            return last
                        p.ph("tensor", trm)

                        def evm(eng, half=half, tsl=tsl):
                            last = None
                            for j in range(4):
                                c = half * 4 + j
                                last = eng.tensor_copy(
                                    moefm[:, c * 512 + tsl * 128:c * 512 + tsl * 128 + 128],
                                    ps_tr[:, j * 128:(j + 1) * 128])
                            return last
                        p.ph("vector", evm)
                for f in range(8):
                    p.dma("sync",
                          [(bufs["x32"][:, 0:512],
                            t["xa"][s, f * 128:(f + 1) * 128, t0:t0 + 512])])

                    def fin(eng, f=f):
                        return eng.tensor_tensor(bufs["xev"][:, 0:512],
                                                 moefm[:, f * 512:(f + 1) * 512],
                                                 bufs["x32"][:, 0:512], ALU.add)
                    p.ph("vector", fin)
                    ev = p.dma("sync",
                               [(t["y"][s, f * 128:(f + 1) * 128, t0:t0 + 512],
                                 bufs["xev"][:, 0:512])])

    p.ph("sync", lambda eng: eng.nop(), waits=[ev])
    p.finalize()
    p.close()
    return nc


def run_layer3_ffn(x_fm8, xa_fm8, w):
    nc = bass.Bass("TRN2", target_bir_lowering=False)
    build_layer3_ffn(nc)
    xfull = np.zeros((16, T, D), dtype=np.float32)
    for c in range(NCORES):
        xfull[c] = x_fm8[c][0].T
        xfull[15 - c] = x_fm8[c][1].T
    xfull = np.ascontiguousarray(xfull)
    base = {
        "xfull": xfull,
        "rw_t": np.ascontiguousarray(np.asarray(w["L3_ffn_router_w"]).T),
        "sg": np.ascontiguousarray(np.asarray(w["L3_ffn_sg"])),
        "su": np.ascontiguousarray(np.asarray(w["L3_ffn_su"])),
        "sd": np.ascontiguousarray(np.asarray(w["L3_ffn_sd"])),
        "rg": np.ascontiguousarray(np.asarray(w["L3_ffn_rg"])),
        "ru": np.ascontiguousarray(np.asarray(w["L3_ffn_ru"])),
        "rd": np.ascontiguousarray(np.asarray(w["L3_ffn_rd"])),
    }
    in_maps = []
    for c in range(NCORES):
        m = dict(base)
        m["x"] = x_fm8[c]
        m["xa"] = xa_fm8[c]
        m["xown"] = np.ascontiguousarray(
            np.stack([xfull[c], xfull[15 - c]], axis=0))
        pm = np.zeros((2, 16, 8), dtype=np.float32)
        pm[0, :c] = 1.0
        pm[1, :15 - c] = 1.0
        m["prefrep"] = np.ascontiguousarray(pm.reshape(1, 256))
        in_maps.append(m)
    res = run_bass_kernel_spmd(nc, in_maps, core_ids=list(range(NCORES)))
    return [res.results[c]["y"] for c in range(NCORES)]



def _rope_tables_l0():
    hd = 64
    half = hd // 2
    inv = (1.0 / 10000.0 ** (np.arange(half, dtype=np.float32) / half)).astype(np.float32)
    tpos = np.arange(T, dtype=np.float32)
    freqs = tpos[:, None] * inv[None, :]
    cos = np.repeat(np.cos(freqs), 2, axis=-1).astype(np.float32)
    sin = np.repeat(np.sin(freqs), 2, axis=-1).astype(np.float32)
    sc = 1.0 / np.sqrt(np.float32(hd))
    import ml_dtypes
    out = np.stack([np.tile(cos, (1, 8)), np.tile(sin, (1, 8)),
                    np.tile(cos * sc, (1, 8)), np.tile(sin * sc, (1, 8))])
    return np.ascontiguousarray(out.astype(ml_dtypes.bfloat16))


def run_layer0(x, w):
    nc = bass.Bass("TRN2", target_bir_lowering=False)
    build_layer0(nc)
    ropes = _rope_tables_l0()
    wqkv_t = np.ascontiguousarray(np.asarray(w["L0_attn_Wqkv"]).T)
    wproj_t = np.ascontiguousarray(np.asarray(w["L0_attn_Wproj"]).T)
    bproj_col = np.ascontiguousarray(
        np.asarray(w["L0_attn_bproj"]).reshape(8, 128).T)
    wg_t = np.ascontiguousarray(np.asarray(w["L0_ffn_w_gate"]).T)
    wu_t = np.ascontiguousarray(np.asarray(w["L0_ffn_w_up"]).T)
    wd_t = np.ascontiguousarray(np.asarray(w["L0_ffn_w_down"]).T)
    in_maps = []
    for c in range(NCORES):
        xc = np.ascontiguousarray(
            np.asarray(x)[[c, 15 - c]].transpose(0, 2, 1).astype(np.float32))
        in_maps.append({
            "x": xc, "wqkv_t": wqkv_t, "wproj_t": wproj_t,
            "bproj_col": bproj_col, "ropes": ropes,
            "wg_t": wg_t, "wu_t": wu_t, "wd_t": wd_t,
        })
    res = run_bass_kernel_spmd(nc, in_maps, core_ids=list(range(NCORES)))
    return [np.asarray(res.results[c]["y"]) for c in range(NCORES)]


def fm8_gather(ys):
    x = np.zeros((16, T, D), dtype=np.float32)
    for c in range(NCORES):
        x[c] = ys[c][0].T
        x[15 - c] = ys[c][1].T
    return x


def kernel(**inputs):
    return run_all(inputs)


def decls_layer0(nc, pref, given=None):
    t = dict(given or {})
    if "x" not in t:
        t["x"] = nc.dram_tensor(pref + "x", [2, D, T], F32, kind="ExternalInput").ap()
    if "wqkv_t" not in t:
        t["wqkv_t"] = nc.dram_tensor(pref + "wqkv_t", [D, 3 * D], F32, kind="ExternalInput").ap()
    if "wproj_t" not in t:
        t["wproj_t"] = nc.dram_tensor(pref + "wproj_t", [D, D], F32, kind="ExternalInput").ap()
    if "bproj_col" not in t:
        t["bproj_col"] = nc.dram_tensor(pref + "bproj_col", [128, 8], F32, kind="ExternalInput").ap()
    if "ropes" not in t:
        t["ropes"] = nc.dram_tensor(pref + "ropes", [4, T, 512], BF16, kind="ExternalInput").ap()
    if "wg_t" not in t:
        t["wg_t"] = nc.dram_tensor(pref + "wg_t", [D, 3 * D], I32, kind="ExternalInput").ap()
    if "wu_t" not in t:
        t["wu_t"] = nc.dram_tensor(pref + "wu_t", [D, 3 * D], I32, kind="ExternalInput").ap()
    if "wd_t" not in t:
        t["wd_t"] = nc.dram_tensor(pref + "wd_t", [3 * D, D], I32, kind="ExternalInput").ap()
    if "xa" not in t:
        t["xa"] = nc.dram_tensor(pref + "xa", [2, D, T], F32, kind="Internal").ap()
    if "y" not in t:
        t["y"] = nc.dram_tensor(pref + "y", [2, D, T], F32, kind="ExternalOutput").ap()
    return t

def emit_layer0(p, cc, bufs, t):
    nc = p.nc
    W = 512
    ps_score = bufs["ps_score"]
    ps_tr = bufs["ps_tr"]

    xn_bf = p.sb("xn_bf", [128, 8 * T], BF16)
    bias_col = p.sb("bias_col", [128, 8], F32)
    p.dma("sync", [(bias_col[:], t["bproj_col"])])

    st = {
        "PT": p.sb("PT", [128, 8 * 512], BF16),
        "negb": p.sb("negb", [1, T], BF16),
        "avtm": p.sb("avtm", [128, 256], BF16),
        "rcp_col": p.sb("rcp_col", [128, 4], F32),
        "m8": p.sb("m8", [1, 2], F32),
        "ps_score": ps_score,
        "ps_b": bufs["ps_b"],
        "ps_tr": ps_tr,
        "ps_misc": bufs["ps_misc"],
        "sq": bufs["sq"],
        "mrow": bufs["mrow"],
    }
    st["emit_negb"] = lambda p_, h: emit_negb_normbound(p_, cc, st, h)

    for s in range(2):
        with ExitStack() as attn_scope:
            qfm = p.sb("qfm", [128, 8 * T], BF16, scope=attn_scope)
            kfm = p.sb("kfm", [128, 8 * T], BF16, scope=attn_scope)
            vtm = p.sb("vtm", [128, 16, 16, 65], BF16, scope=attn_scope)
            av_fm = p.sb("av_fm", [128, 8 * T], BF16, scope=attn_scope)
            wq_sb = p.sb("wq_sb", [128, 8 * 512], BF16, scope=attn_scope)
            ropes_sb = p.sb("ropes_sb", [128, 2 * 512], BF16, scope=attn_scope)
            qtm = p.sb("qtm", [128, 512], BF16, scope=attn_scope)
            st.update(qfm=qfm, kfm=kfm, vtm=vtm, av_fm=av_fm)

            emit_ln(p, cc, t["x"][s], xn_bf, bufs)

            p.ph("vector", lambda eng: eng.memset(vtm[:, :, :, 64:65], 1.0))

            for grp in range(6):
                p.dma("gpsimd",
                      [(wq_sb[:, c * 512:(c + 1) * 512],
                        t["wqkv_t"][c * 128:(c + 1) * 128,
                                    grp * 512:(grp + 1) * 512])
                       for c in range(8)])
                is_q = grp < 2
                is_k = 2 <= grp < 4
                for tt in range(16):
                    t0 = tt * 128

                    def mm_qkv(eng, grp=grp, t0=t0):
                        last = None
                        for c in range(8):
                            last = eng.matmul(
                                bufs["ps_a"][:, 0:512],
                                xn_bf[:, c * T + t0:c * T + t0 + 128],
                                wq_sb[:, c * 512:(c + 1) * 512],
                                start=(c == 0), stop=(c == 7))
                        return last
                    p.ph("tensor", mm_qkv)

                    if is_q or is_k:
                        ri = 2 if is_q else 0
                        p.dma("sync",
                              [(ropes_sb[:, 0:512],
                                t["ropes"][ri, t0:t0 + 128, :]),
                               (ropes_sb[:, 512:1024],
                                t["ropes"][ri + 1, t0:t0 + 128, :])])

                        def rope(eng):
                            ps = bufs["ps_a"]
                            pe = ps[:, 0:512].rearrange("p (x two) -> p x two", two=2)
                            ce = ropes_sb[:, 0:512].rearrange("p (x two) -> p x two", two=2)
                            se = ropes_sb[:, 512:1024].rearrange("p (x two) -> p x two", two=2)
                            qe = qtm[:, 0:512].rearrange("p (x two) -> p x two", two=2)
                            te = bufs["tmp32"][:, 0:256]
                            to = bufs["tmpo"][:, 0:256]
                            eng.tensor_tensor(te, pe[:, :, 0], ce[:, :, 0], ALU.mult)
                            eng.tensor_tensor(to, pe[:, :, 1], se[:, :, 0], ALU.mult)
                            eng.tensor_tensor(qe[:, :, 0], te, to, ALU.subtract)
                            eng.tensor_tensor(te, pe[:, :, 1], ce[:, :, 1], ALU.mult)
                            eng.tensor_tensor(to, pe[:, :, 0], se[:, :, 1], ALU.mult)
                            return eng.tensor_tensor(qe[:, :, 1], te, to, ALU.add)
                        p.ph("vector", rope)

                        def tr(eng):
                            last = None
                            for j in range(4):
                                last = eng.transpose(
                                    ps_tr[:, j * 128:(j + 1) * 128],
                                    qtm[:, j * 128:(j + 1) * 128], cc.ident[:])
                            return last
                        p.ph("tensor", tr)

                        dstt = qfm if is_q else kfm

                        def ev(eng, grp=grp, t0=t0, dstt=dstt):
                            last = None
                            for j in range(4):
                                pair = (grp % 2) * 4 + j
                                for half in range(2):
                                    last = eng.tensor_copy(
                                        dstt[half * 64:half * 64 + 64,
                                             pair * T + t0:pair * T + t0 + 128],
                                        ps_tr[half * 64:half * 64 + 64,
                                              j * 128:(j + 1) * 128])
                            return last
                        p.ph("vector", ev)
                    else:
                        def evv(eng, grp=grp, tt=tt):
                            last = None
                            for j in range(8):
                                h = (grp - 4) * 8 + j
                                last = eng.tensor_copy(
                                    vtm[:, h, tt, 0:64],
                                    bufs["ps_a"][:, j * 64:(j + 1) * 64])
                            return last
                        p.ph("vector", evv)

            emit_attn_core(p, cc, st, heads=16, causal=True, window=512)

            wproj_sb = p.sb("wproj_sb", [128, 8 * 128], BF16, scope=attn_scope)
            for f in range(8):
                p.dma("gpsimd",
                      [(wproj_sb[:].rearrange("p (c g) -> p c g", c=8),
                        t["wproj_t"][:, f * 128:(f + 1) * 128].rearrange(
                            "(c p) g -> p c g", p=128))])
                for tc in range(4):
                    t0 = tc * 512

                    def mm_o(eng, t0=t0):
                        last = None
                        for c in range(8):
                            last = eng.matmul(
                                bufs["ps_a"][:, 0:512],
                                wproj_sb[:, c * 128:(c + 1) * 128],
                                av_fm[:, c * T + t0:c * T + t0 + 512],
                                start=(c == 0), stop=(c == 7))
                        return last
                    p.ph("tensor", mm_o)
                    p.dma("sync", [(bufs["x32"][:, 0:512],
                                    t["x"][s, f * 128:(f + 1) * 128, t0:t0 + 512])])

                    def evac(eng, f=f):
                        return eng.scalar_tensor_tensor(
                            bufs["xev"][:, 0:512], bufs["ps_a"][:, 0:512],
                            bias_col[:, f:f + 1], bufs["x32"][:, 0:512],
                            ALU.add, ALU.add)
                    p.ph("vector", evac)
                    p.dma("sync", [(t["xa"][s, f * 128:(f + 1) * 128, t0:t0 + 512],
                                    bufs["xev"][:, 0:512])])

        with ExitStack() as ffn_scope:
            hbuf = p.sb("hbuf", [128, 24 * 512], BF16, scope=ffn_scope)
            ubuf = p.sb("ubuf", [128, 4 * 512], BF16, scope=ffn_scope)
            wg_sb = p.sb("wg_sb", [128, 8 * 512], BF16, scope=ffn_scope)
            wu_sb = p.sb("wu_sb", [128, 8 * 512], BF16, scope=ffn_scope)
            wd_sb = p.sb("wd_sb", [128, 24 * 128], BF16, scope=ffn_scope)
            sc = 1.0 / 127.0
            for tc in range(4):
                t0 = tc * 512
                for hs in range(6):
                    p.dma("gpsimd",
                          [(wg_sb[:, c * 512:(c + 1) * 512],
                            t["wg_t"][c * 128:(c + 1) * 128,
                                      hs * 512:(hs + 1) * 512])
                           for c in range(8)] +
                          [(wu_sb[:, c * 512:(c + 1) * 512],
                            t["wu_t"][c * 128:(c + 1) * 128,
                                      hs * 512:(hs + 1) * 512])
                           for c in range(8)])

                    def mm_g(eng, hs=hs, t0=t0):
                        last = None
                        for j in range(4):
                            for c in range(8):
                                last = eng.matmul(
                                    ps_score[:, j * 512:(j + 1) * 512],
                                    wg_sb[:, c * 512 + j * 128:c * 512 + j * 128 + 128],
                                    xn_bf[:, c * T + t0:c * T + t0 + 512],
                                    start=(c == 0), stop=(c == 7))
                        return last
                    p.ph("tensor", mm_g)

                    def silu(eng, hs=hs):
                        last = None
                        for j in range(4):
                            last = eng.activation(
                                hbuf[:, (hs * 4 + j) * 512:(hs * 4 + j + 1) * 512],
                                ps_score[:, j * 512:(j + 1) * 512], AF.Silu,
                                scale=sc)
                        return last
                    p.ph("scalar", silu)

                    def mm_u(eng, hs=hs, t0=t0):
                        last = None
                        for j in range(4):
                            for c in range(8):
                                last = eng.matmul(
                                    ps_score[:, j * 512:(j + 1) * 512],
                                    wu_sb[:, c * 512 + j * 128:c * 512 + j * 128 + 128],
                                    xn_bf[:, c * T + t0:c * T + t0 + 512],
                                    start=(c == 0), stop=(c == 7))
                        return last
                    p.ph("tensor", mm_u)

                    def ev_u(eng):
                        last = None
                        for j in range(4):
                            last = eng.tensor_copy(
                                ubuf[:, j * 512:(j + 1) * 512],
                                ps_score[:, j * 512:(j + 1) * 512])
                        return last
                    p.ph("vector", ev_u)

                    def mul_h(eng, hs=hs):
                        last = None
                        for j in range(4):
                            ap = hbuf[:, (hs * 4 + j) * 512:(hs * 4 + j + 1) * 512]
                            last = eng.tensor_tensor(
                                ap, ap, ubuf[:, j * 512:(j + 1) * 512], ALU.mult)
                        return last
                    p.ph("vector", mul_h)

                for f in range(8):
                    p.dma("gpsimd",
                          [(wd_sb[:].rearrange("p (c g) -> p c g", c=24),
                            t["wd_t"][:, f * 128:(f + 1) * 128].rearrange(
                                "(c p) g -> p c g", p=128))])

                    def mm_d(eng, t0=t0):
                        last = None
                        for cj in range(24):
                            last = eng.matmul(
                                bufs["ps_a"][:, 0:512],
                                wd_sb[:, cj * 128:(cj + 1) * 128],
                                hbuf[:, cj * 512:(cj + 1) * 512],
                                start=(cj == 0), stop=(cj == 23))
                        return last
                    p.ph("tensor", mm_d)
                    p.dma("sync", [(bufs["x32"][:, 0:512],
                                    t["xa"][s, f * 128:(f + 1) * 128, t0:t0 + 512])])

                    def evac(eng):
                        return eng.scalar_tensor_tensor(
                            bufs["xev"][:, 0:512], bufs["ps_a"][:, 0:512],
                            sc * sc, bufs["x32"][:, 0:512], ALU.mult, ALU.add)
                    p.ph("vector", evac)
                    ev = p.dma("sync",
                               [(t["y"][s, f * 128:(f + 1) * 128, t0:t0 + 512],
                                 bufs["xev"][:, 0:512])])

    return ev

def decls_layer1_attn(nc, pref, given=None):
    t = dict(given or {})
    if "x" not in t:
        t["x"] = nc.dram_tensor(pref + "x", [2, D, T], F32, kind="ExternalInput").ap()
    if "wqkv_t" not in t:
        t["wqkv_t"] = nc.dram_tensor(pref + "wqkv_t", [D, 3 * D], F32, kind="ExternalInput").ap()
    if "bqkv_row" not in t:
        t["bqkv_row"] = nc.dram_tensor(pref + "bqkv_row", [1, 3 * D], F32, kind="ExternalInput").ap()
    if "wout_t" not in t:
        t["wout_t"] = nc.dram_tensor(pref + "wout_t", [D, D], F32, kind="ExternalInput").ap()
    if "bout_col" not in t:
        t["bout_col"] = nc.dram_tensor(pref + "bout_col", [128, 8], F32, kind="ExternalInput").ap()
    if "y" not in t:
        t["y"] = nc.dram_tensor(pref + "y", [2, D, T], F32, kind="ExternalOutput").ap()
    return t

def emit_layer1_attn(p, cc, bufs, t):
    nc = p.nc
    W = 512
    ps_score = bufs["ps_score"]
    ps_tr = bufs["ps_tr"]

    xn_bf = p.sb("xn_bf", [128, 8 * T], BF16)
    bias_col = p.sb("bias_col", [128, 8], F32)
    bqkv = p.sb("bqkv", [1, 3 * D], BF16)
    p.dma("sync", [(bias_col[:], t["bout_col"])])
    p.dma("gpsimd", [(bqkv[:], t["bqkv_row"])])

    st = {
        "PT": p.sb("PT", [128, 16 * 256], BF16),
        "negb": p.sb("negb", [1, T], BF16),
        "avtm": p.sb("avtm", [128, 256], BF16),
        "rcp_col": p.sb("rcp_col", [128, 4], F32),
        "m8": p.sb("m8", [1, 2], F32),
        "ps_score": ps_score,
        "ps_b": bufs["ps_b"],
        "ps_tr": ps_tr,
        "ps_misc": bufs["ps_misc"],
        "sq": bufs["sq"],
        "mrow": bufs["mrow"],
    }
    st["emit_negb"] = lambda p_, h: emit_negb_normbound(p_, cc, st, h)

    for s in range(2):
        with ExitStack() as attn_scope:
            qfm = p.sb("qfm", [128, 8 * T], BF16, scope=attn_scope)
            kfm = p.sb("kfm", [128, 8 * T], BF16, scope=attn_scope)
            vtm = p.sb("vtm", [128, 16, 16, 65], BF16, scope=attn_scope)
            av_fm = p.sb("av_fm", [128, 8 * T], BF16, scope=attn_scope)
            wq_sb = p.sb("wq_sb", [128, 8 * 512], BF16, scope=attn_scope)
            qtm = p.sb("qtm", [128, 512], BF16, scope=attn_scope)
            st.update(qfm=qfm, kfm=kfm, vtm=vtm, av_fm=av_fm)

            emit_ln(p, cc, t["x"][s], xn_bf, bufs)
            p.ph("vector", lambda eng, vtm=vtm: eng.memset(vtm[:, :, :, 64:65], 1.0))

            for grp in range(6):
                p.dma("gpsimd",
                      [(wq_sb[:, c * 512:(c + 1) * 512],
                        t["wqkv_t"][c * 128:(c + 1) * 128,
                                    grp * 512:(grp + 1) * 512])
                       for c in range(8)])
                is_q = grp < 2
                is_k = 2 <= grp < 4
                for tt in range(16):
                    t0 = tt * 128

                    def mm_qkv(eng, grp=grp, t0=t0):
                        for c in range(8):
                            eng.matmul(
                                bufs["ps_a"][:, 0:512],
                                xn_bf[:, c * T + t0:c * T + t0 + 128],
                                wq_sb[:, c * 512:(c + 1) * 512],
                                start=(c == 0), stop=False)
                        return eng.matmul(
                            bufs["ps_a"][:, 0:512], cc.ones_row[0:1, 0:128],
                            bqkv[0:1, grp * 512:(grp + 1) * 512],
                            start=False, stop=True)
                    p.ph("tensor", mm_qkv)

                    if is_q or is_k:
                        scale = 0.125 if is_q else 1.0

                        def cvt(eng, scale=scale):
                            return eng.tensor_scalar(qtm[:, 0:512],
                                                     bufs["ps_a"][:, 0:512],
                                                     scale, None, ALU.mult)
                        p.ph("vector", cvt)

                        def tr(eng):
                            last = None
                            for j in range(4):
                                last = eng.transpose(
                                    ps_tr[:, j * 128:(j + 1) * 128],
                                    qtm[:, j * 128:(j + 1) * 128], cc.ident[:])
                            return last
                        p.ph("tensor", tr)

                        dstt = qfm if is_q else kfm

                        def ev(eng, grp=grp, t0=t0, dstt=dstt):
                            last = None
                            for j in range(4):
                                pair = (grp % 2) * 4 + j
                                for half in range(2):
                                    last = eng.tensor_copy(
                                        dstt[half * 64:half * 64 + 64,
                                             pair * T + t0:pair * T + t0 + 128],
                                        ps_tr[half * 64:half * 64 + 64,
                                              j * 128:(j + 1) * 128])
                            return last
                        p.ph("vector", ev)
                    else:
                        def evv(eng, grp=grp, tt=tt, vtm=vtm):
                            last = None
                            for j in range(8):
                                h = (grp - 4) * 8 + j
                                last = eng.tensor_copy(
                                    vtm[:, h, tt, 0:64],
                                    bufs["ps_a"][:, j * 64:(j + 1) * 64])
                            return last
                        p.ph("vector", evv)

            emit_attn_core(p, cc, st, heads=16, causal=False, window=None, qw=256)

            wproj_sb = p.sb("wproj_sb", [128, 8 * 128], BF16, scope=attn_scope)
            for f in range(8):
                p.dma("gpsimd",
                      [(wproj_sb[:].rearrange("p (c g) -> p c g", c=8),
                        t["wout_t"][:, f * 128:(f + 1) * 128].rearrange(
                            "(c p) g -> p c g", p=128))])
                for tc in range(4):
                    t0 = tc * 512

                    def mm_o(eng, t0=t0, av_fm=av_fm, wproj_sb=wproj_sb):
                        last = None
                        for c in range(8):
                            last = eng.matmul(
                                bufs["ps_a"][:, 0:512],
                                wproj_sb[:, c * 128:(c + 1) * 128],
                                av_fm[:, c * T + t0:c * T + t0 + 512],
                                start=(c == 0), stop=(c == 7))
                        return last
                    p.ph("tensor", mm_o)
                    p.dma("sync", [(bufs["x32"][:, 0:512],
                                    t["x"][s, f * 128:(f + 1) * 128, t0:t0 + 512])])

                    def evac(eng, f=f):
                        return eng.scalar_tensor_tensor(
                            bufs["xev"][:, 0:512], bufs["ps_a"][:, 0:512],
                            bias_col[:, f:f + 1], bufs["x32"][:, 0:512],
                            ALU.add, ALU.add)
                    p.ph("vector", evac)
                    ev = p.dma("sync",
                               [(t["y"][s, f * 128:(f + 1) * 128, t0:t0 + 512],
                                 bufs["xev"][:, 0:512])])

    return ev

def decls_layer1_ffn(nc, pref, given=None):
    t = dict(given or {})
    if "x" not in t:
        t["x"] = nc.dram_tensor(pref + "x", [2, D, T], F32, kind="ExternalInput").ap()
    if "xa" not in t:
        t["xa"] = nc.dram_tensor(pref + "xa", [2, D, T], F32, kind="ExternalInput").ap()
    if "ln1_row" not in t:
        t["ln1_row"] = nc.dram_tensor(pref + "ln1_row", [1, D], F32, kind="ExternalInput").ap()
    if "ln2_row" not in t:
        t["ln2_row"] = nc.dram_tensor(pref + "ln2_row", [1, D], F32, kind="ExternalInput").ap()
    if "kvd_t" not in t:
        t["kvd_t"] = nc.dram_tensor(pref + "kvd_t", [D, 256], F32, kind="ExternalInput").ap()
    if "qd_t" not in t:
        t["qd_t"] = nc.dram_tensor(pref + "qd_t", [D, 256], F32, kind="ExternalInput").ap()
    if "ku_t" not in t:
        t["ku_t"] = nc.dram_tensor(pref + "ku_t", [256, D], F32, kind="ExternalInput").ap()
    if "qu_t" not in t:
        t["qu_t"] = nc.dram_tensor(pref + "qu_t", [256, D], F32, kind="ExternalInput").ap()
    if "vu_t" not in t:
        t["vu_t"] = nc.dram_tensor(pref + "vu_t", [256, D], F32, kind="ExternalInput").ap()
    if "rq_t" not in t:
        t["rq_t"] = nc.dram_tensor(pref + "rq_t", [256, 512], F32, kind="ExternalInput").ap()
    if "rk_t" not in t:
        t["rk_t"] = nc.dram_tensor(pref + "rk_t", [D, 512], F32, kind="ExternalInput").ap()
    if "ow_t" not in t:
        t["ow_t"] = nc.dram_tensor(pref + "ow_t", [D, D], F32, kind="ExternalInput").ap()
    if "rw_t" not in t:
        t["rw_t"] = nc.dram_tensor(pref + "rw_t", [D, 7], F32, kind="ExternalInput").ap()
    if "rb_row" not in t:
        t["rb_row"] = nc.dram_tensor(pref + "rb_row", [1, 7], F32, kind="ExternalInput").ap()
    if "rg_all" not in t:
        t["rg_all"] = nc.dram_tensor(pref + "rg_all", [8, D, 2048], F32, kind="ExternalInput").ap()
    if "ru_all" not in t:
        t["ru_all"] = nc.dram_tensor(pref + "ru_all", [8, D, 2048], F32, kind="ExternalInput").ap()
    if "rd_all" not in t:
        t["rd_all"] = nc.dram_tensor(pref + "rd_all", [8, 2048, D], F32, kind="ExternalInput").ap()
    if "rope2" not in t:
        t["rope2"] = nc.dram_tensor(pref + "rope2", [2, T, 512], BF16, kind="ExternalInput").ap()
    if "xnh" not in t:
        t["xnh"] = nc.dram_tensor(pref + "xnh", [2, 128, 8 * T], BF16, kind="Internal").ap()
    if "t1h" not in t:
        t["t1h"] = nc.dram_tensor(pref + "t1h", [2, D, T], F32, kind="Internal").ap()
    if "macc" not in t:
        t["macc"] = nc.dram_tensor(pref + "macc", [2, T, D], F32, kind="Internal").ap()
    if "y" not in t:
        t["y"] = nc.dram_tensor(pref + "y", [2, D, T], F32, kind="ExternalOutput").ap()
    return t

def emit_layer1_ffn(p, cc, bufs, t):
    nc = p.nc
    W = 512
    ps_score = bufs["ps_score"]
    ps_tr = bufs["ps_tr"]

    ln1_row = p.sb("ln1_row", [1, D], BF16)
    ln2_row = p.sb("ln2_row", [1, D], BF16)
    rb_bf = p.sb("rb_bf", [1, 7], BF16)
    p.dma("gpsimd", [(ln1_row[:], t["ln1_row"]), (ln2_row[:], t["ln2_row"]),
                     (rb_bf[:], t["rb_row"])])

    st = {
        "negb": p.sb("negb", [1, T], BF16),
        "avtm": p.sb("avtm", [128, 256], BF16),
        "rcp_col": p.sb("rcp_col", [128, 4], F32),
        "m8": p.sb("m8", [1, 2], F32),
        "ps_score": ps_score,
        "ps_b": bufs["ps_b"],
        "ps_tr": ps_tr,
        "ps_misc": bufs["ps_misc"],
        "sq": bufs["sq"],
        "mrow": bufs["mrow"],
    }
    st["emit_negb"] = lambda p_, h: emit_negb_normbound(p_, cc, st, h)

    for s in range(2):
        with ExitStack() as sc_attn:
            qfm = p.sb("qfm", [128, 8 * T], BF16, scope=sc_attn)
            kfm = p.sb("kfm", [128, 8 * T], BF16, scope=sc_attn)
            vtm = p.sb("vtm", [128, 16, 16, 65], BF16, scope=sc_attn)
            st.update(qfm=qfm, kfm=kfm, vtm=vtm)
            p.ph("vector", lambda eng, vtm=vtm: eng.memset(vtm[:, :, :, 64:65], 1.0))

            with ExitStack() as sc_proj:
                emit_ln(p, cc, t["x"][s], t["xnh"][s], bufs, dst_hbm=True)
                xn2 = p.sb("xn2", [128, 8 * T], BF16, scope=sc_proj)
                emit_ln(p, cc, t["xnh"][s], xn2, bufs, w_row=ln1_row,
                        src_bf16=True)

                kvlat = p.sb("kvlat", [128, 2 * T], BF16, scope=sc_proj)
                qlat = p.sb("qlat", [128, 2 * T], BF16, scope=sc_proj)
                with ExitStack() as sc_lat:
                    kvd_sb = p.sb("kvd_sb", [128, 8 * 256], BF16, scope=sc_lat)
                    qd_sb = p.sb("qd_sb", [128, 8 * 256], BF16, scope=sc_lat)
                    p.dma("gpsimd",
                          [(kvd_sb[:].rearrange("p (c g) -> p c g", c=8),
                            t["kvd_t"].rearrange("(c p) g -> p c g", p=128)),
                           (qd_sb[:].rearrange("p (c g) -> p c g", c=8),
                            t["qd_t"].rearrange("(c p) g -> p c g", p=128))])

                    for dst, wsb in ((kvlat, kvd_sb), (qlat, qd_sb)):
                        for lc in range(2):
                            for tc in range(4):
                                t0 = tc * 512

                                def mm_lat(eng, wsb=wsb, lc=lc, t0=t0):
                                    last = None
                                    for c in range(8):
                                        last = eng.matmul(
                                            bufs["ps_a"][:, 0:512],
                                            wsb[:, c * 256 + lc * 128:c * 256 + lc * 128 + 128],
                                            xn2[:, c * T + t0:c * T + t0 + 512],
                                            start=(c == 0), stop=(c == 7))
                                    return last
                                p.ph("tensor", mm_lat)
                                p.ph("vector", lambda eng, dst=dst, lc=lc, t0=t0:
                                     eng.tensor_copy(dst[:, lc * T + t0:lc * T + t0 + 512],
                                                     bufs["ps_a"][:, 0:512]))

                ku_sb = p.sb("ku_sb", [128, 2 * D], BF16, scope=sc_proj)
                qu_sb = p.sb("qu_sb", [128, 2 * D], BF16, scope=sc_proj)
                vu_sb = p.sb("vu_sb", [128, 2 * D], BF16, scope=sc_proj)
                rq_sb = p.sb("rq_sb", [128, 2 * 512], BF16, scope=sc_proj)
                rk_sb = p.sb("rk_sb", [128, 8 * 512], BF16, scope=sc_proj)
                rop_sb = p.sb("rop_sb", [128, 2 * 512], BF16, scope=sc_proj)
                rtm = p.sb("rtm", [128, 512], BF16, scope=sc_proj)
                qtm = p.sb("qtm", [128, 512], BF16, scope=sc_proj)
                p.dma("gpsimd",
                      [(ku_sb[:].rearrange("p (c g) -> p c g", c=2),
                        t["ku_t"].rearrange("(c p) g -> p c g", p=128)),
                       (qu_sb[:].rearrange("p (c g) -> p c g", c=2),
                        t["qu_t"].rearrange("(c p) g -> p c g", p=128)),
                       (vu_sb[:].rearrange("p (c g) -> p c g", c=2),
                        t["vu_t"].rearrange("(c p) g -> p c g", p=128)),
                       (rq_sb[:].rearrange("p (c g) -> p c g", c=2),
                        t["rq_t"].rearrange("(c p) g -> p c g", p=128)),
                       (rk_sb[:].rearrange("p (c g) -> p c g", c=8),
                        t["rk_t"].rearrange("(c p) g -> p c g", p=128))])

                def v3(ap512, half):
                    return ap512.rearrange("p (h two g) -> p h two g",
                                           two=2, g=16)[:, :, half, :]

                for tt in range(16):
                    t0 = tt * 128
                    p.dma("sync", [(rop_sb[:, 0:512], t["rope2"][0, t0:t0 + 128, :]),
                                   (rop_sb[:, 512:1024], t["rope2"][1, t0:t0 + 128, :])])
                    for which in ("k", "q"):
                        if which == "k":
                            def mm_rot(eng, t0=t0):
                                last = None
                                for c in range(8):
                                    last = eng.matmul(
                                        bufs["ps_b"][:, 0:512],
                                        xn2[:, c * T + t0:c * T + t0 + 128],
                                        rk_sb[:, c * 512:(c + 1) * 512],
                                        start=(c == 0), stop=(c == 7))
                                return last
                        else:
                            def mm_rot(eng, t0=t0):
                                last = None
                                for c in range(2):
                                    last = eng.matmul(
                                        bufs["ps_b"][:, 0:512],
                                        qlat[:, c * T + t0:c * T + t0 + 128],
                                        rq_sb[:, c * 512:(c + 1) * 512],
                                        start=(c == 0), stop=(c == 1))
                                return last
                        p.ph("tensor", mm_rot)

                        def rope(eng):
                            ps = bufs["ps_b"][:, 0:512]
                            te = bufs["tmp32"][:, 0:256].rearrange(
                                "p (h g) -> p h g", g=16)
                            to = bufs["tmpo"][:, 0:256].rearrange(
                                "p (h g) -> p h g", g=16)
                            c1 = v3(rop_sb[:, 0:512], 0)
                            c2 = v3(rop_sb[:, 0:512], 1)
                            s1 = v3(rop_sb[:, 512:1024], 0)
                            s2 = v3(rop_sb[:, 512:1024], 1)
                            x1 = v3(ps, 0)
                            x2 = v3(ps, 1)
                            eng.tensor_tensor(te, x1, c1, ALU.mult)
                            eng.tensor_tensor(to, x2, s1, ALU.mult)
                            eng.tensor_tensor(v3(rtm[:, 0:512], 0), te, to,
                                              ALU.subtract)
                            eng.tensor_tensor(te, x2, c2, ALU.mult)
                            eng.tensor_tensor(to, x1, s2, ALU.mult)
                            return eng.tensor_tensor(v3(rtm[:, 0:512], 1), te, to,
                                                     ALU.add)
                        p.ph("vector", rope)

                        for grp in range(2):
                            if which == "k":
                                def mm_main(eng, grp=grp, t0=t0):
                                    last = None
                                    for c in range(2):
                                        last = eng.matmul(
                                            bufs["ps_a"][:, 0:512],
                                            kvlat[:, c * T + t0:c * T + t0 + 128],
                                            ku_sb[:, c * D + grp * 512:c * D + grp * 512 + 512],
                                            start=(c == 0), stop=(c == 1))
                                    return last
                            else:
                                def mm_main(eng, grp=grp, t0=t0):
                                    last = None
                                    for c in range(2):
                                        last = eng.matmul(
                                            bufs["ps_a"][:, 0:512],
                                            qlat[:, c * T + t0:c * T + t0 + 128],
                                            qu_sb[:, c * D + grp * 512:c * D + grp * 512 + 512],
                                            start=(c == 0), stop=(c == 1))
                                    return last
                            p.ph("tensor", mm_main)

                            scale = 0.125 if which == "q" else 1.0

                            def asm(eng, grp=grp, scale=scale):
                                ps = bufs["ps_a"][:, 0:512]
                                pv = ps.rearrange("p (h d) -> p h d", d=64)
                                qv = qtm[:, 0:512].rearrange("p (h d) -> p h d", d=64)
                                rv = rtm[:, grp * 256:(grp + 1) * 256].rearrange(
                                    "p (h g) -> p h g", g=32)
                                eng.tensor_scalar(qv[:, :, 32:64], pv[:, :, 32:64],
                                                  scale, None, ALU.mult)
                                return eng.tensor_scalar(qv[:, :, 0:32], rv,
                                                         scale, None, ALU.mult)
                            p.ph("vector", asm)

                            def tr(eng):
                                last = None
                                for j in range(4):
                                    last = eng.transpose(
                                        ps_tr[:, j * 128:(j + 1) * 128],
                                        qtm[:, j * 128:(j + 1) * 128], cc.ident[:])
                                return last
                            p.ph("tensor", tr)

                            dstt = kfm if which == "k" else qfm

                            def ev(eng, grp=grp, t0=t0, dstt=dstt):
                                last = None
                                for j in range(4):
                                    pair = grp * 4 + j
                                    for half in range(2):
                                        last = eng.tensor_copy(
                                            dstt[half * 64:half * 64 + 64,
                                                 pair * T + t0:pair * T + t0 + 128],
                                            ps_tr[half * 64:half * 64 + 64,
                                                  j * 128:(j + 1) * 128])
                                return last
                            p.ph("vector", ev)

                    for grp in range(2):
                        def mm_v(eng, grp=grp, t0=t0):
                            last = None
                            for c in range(2):
                                last = eng.matmul(
                                    bufs["ps_a"][:, 0:512],
                                    kvlat[:, c * T + t0:c * T + t0 + 128],
                                    vu_sb[:, c * D + grp * 512:c * D + grp * 512 + 512],
                                    start=(c == 0), stop=(c == 1))
                            return last
                        p.ph("tensor", mm_v)

                        def evv(eng, grp=grp, tt=tt, vtm=vtm):
                            last = None
                            for j in range(8):
                                h = grp * 8 + j
                                last = eng.tensor_copy(
                                    vtm[:, h, tt, 0:64],
                                    bufs["ps_a"][:, j * 64:(j + 1) * 64])
                            return last
                        p.ph("vector", evv)

            av_fm = p.sb("av_fm", [128, 8 * T], BF16, scope=sc_attn)
            PT = p.sb("PT", [128, 16 * 256], BF16, scope=sc_attn)
            st.update(av_fm=av_fm, PT=PT)
            emit_attn_core(p, cc, st, heads=16, causal=True, window=None, qw=256)

            wproj_sb = p.sb("wproj_sb", [128, 8 * 128], BF16, scope=sc_attn)
            xnb_sb = p.sb("xnb_sb", [128, 512], BF16, scope=sc_attn)
            for f in range(8):
                p.dma("gpsimd",
                      [(wproj_sb[:].rearrange("p (c g) -> p c g", c=8),
                        t["ow_t"][:, f * 128:(f + 1) * 128].rearrange(
                            "(c p) g -> p c g", p=128))])
                for tc in range(4):
                    t0 = tc * 512

                    def mm_o(eng, t0=t0, av_fm=av_fm, wproj_sb=wproj_sb):
                        last = None
                        for c in range(8):
                            last = eng.matmul(
                                bufs["ps_a"][:, 0:512],
                                wproj_sb[:, c * 128:(c + 1) * 128],
                                av_fm[:, c * T + t0:c * T + t0 + 512],
                                start=(c == 0), stop=(c == 7))
                        return last
                    p.ph("tensor", mm_o)
                    p.dma("sync", [(xnb_sb[:],
                                    t["xnh"][s][:, f * T + t0:f * T + t0 + 512])])
                    p.ph("vector", lambda eng: eng.tensor_tensor(
                        bufs["xev"][:, 0:512], bufs["ps_a"][:, 0:512],
                        xnb_sb[:], ALU.add))
                    p.dma("sync", [(t["t1h"][s, f * 128:(f + 1) * 128, t0:t0 + 512],
                                    bufs["xev"][:, 0:512])])

        with ExitStack() as sc_moe:
            xn3 = p.sb("xn3", [128, 8 * T], BF16, scope=sc_moe)
            emit_ln(p, cc, t["t1h"][s], xn3, bufs, w_row=ln2_row)

            rw_sb = p.sb("rw_sb", [128, 8, 7], BF16, scope=sc_moe)
            wgt_sb = p.sb("wgt_sb", [128, 16, 8], F32, scope=sc_moe)
            prow = p.sb("prow", [128, 8], F32, scope=sc_moe)
            ptmp = p.sb("ptmp", [128, 8], F32, scope=sc_moe)
            mcol = p.sb("mcol", [128, 4], F32, scope=sc_moe)
            p.dma("gpsimd", [(rw_sb[:],
                              t["rw_t"].rearrange("(c p) e -> p c e", p=128))])
            p.ph("vector", lambda eng: eng.memset(wgt_sb[:, :, 0:1], 1.0))
            for tt in range(16):
                t0 = tt * 128

                def mm_r(eng, t0=t0):
                    for c in range(8):
                        eng.matmul(bufs["ps_misc"][:, 0:7],
                                   xn3[:, c * T + t0:c * T + t0 + 128],
                                   rw_sb[:, c, :], start=(c == 0), stop=False)
                    return eng.matmul(bufs["ps_misc"][:, 0:7],
                                      cc.ones_row[0:1, 0:128], rb_bf[0:1, 0:7],
                                      start=False, stop=True)
                p.ph("tensor", mm_r)
                p.ph("scalar", lambda eng: eng.activation(
                    prow[:, 0:7], bufs["ps_misc"][:, 0:7], AF.Sigmoid))

                emit_top2(p, prow, ptmp, mcol, wgt_sb[:, tt, 1:8], 7)

            wg_sb = p.sb("wg_sb", [128, 8 * 2048], BF16, scope=sc_moe)
            wu_sb = p.sb("wu_sb", [128, 8 * 2048], BF16, scope=sc_moe)
            wd_sb = p.sb("wd_sb", [128, 16 * D], BF16, scope=sc_moe)
            h_fm = p.sb("h_fm", [128, 16 * 512], BF16, scope=sc_moe)
            ubuf = p.sb("ubuf", [128, 4 * 512], BF16, scope=sc_moe)
            maccbs = [p.sb("maccb", [128, 512], F32, scope=sc_moe)
                      for _ in range(2)]
            macc_evs = [None, None]
            dnct = [0]

            for e in range(8):
                p.dma("gpsimd",
                      [(wg_sb[:, c * 2048:(c + 1) * 2048],
                        t["rg_all"][e, c * 128:(c + 1) * 128, :])
                       for c in range(8)] +
                      [(wu_sb[:, c * 2048:(c + 1) * 2048],
                        t["ru_all"][e, c * 128:(c + 1) * 128, :])
                       for c in range(8)] +
                      [(wd_sb[:, ic * D:(ic + 1) * D],
                        t["rd_all"][e, ic * 128:(ic + 1) * 128, :])
                       for ic in range(16)])
                for tc in range(4):
                    t0 = tc * 512
                    for ib in range(4):
                        def mm_g(eng, ib=ib, t0=t0):
                            last = None
                            for j in range(4):
                                ic = ib * 4 + j
                                for c in range(8):
                                    last = eng.matmul(
                                        ps_score[:, j * 512:(j + 1) * 512],
                                        wg_sb[:, c * 2048 + ic * 128:c * 2048 + ic * 128 + 128],
                                        xn3[:, c * T + t0:c * T + t0 + 512],
                                        start=(c == 0), stop=(c == 7))
                            return last
                        p.ph("tensor", mm_g)

                        def silu(eng, ib=ib):
                            last = None
                            for j in range(4):
                                ic = ib * 4 + j
                                last = eng.activation(
                                    h_fm[:, ic * 512:(ic + 1) * 512],
                                    ps_score[:, j * 512:(j + 1) * 512], AF.Silu)
                            return last
                        p.ph("scalar", silu)

                        def mm_u(eng, ib=ib, t0=t0):
                            last = None
                            for j in range(4):
                                ic = ib * 4 + j
                                for c in range(8):
                                    last = eng.matmul(
                                        ps_score[:, j * 512:(j + 1) * 512],
                                        wu_sb[:, c * 2048 + ic * 128:c * 2048 + ic * 128 + 128],
                                        xn3[:, c * T + t0:c * T + t0 + 512],
                                        start=(c == 0), stop=(c == 7))
                            return last
                        p.ph("tensor", mm_u)

                        def ev_u(eng):
                            last = None
                            for j in range(4):
                                last = eng.tensor_copy(
                                    ubuf[:, j * 512:(j + 1) * 512],
                                    ps_score[:, j * 512:(j + 1) * 512])
                            return last
                        p.ph("vector", ev_u)

                        def mul_h(eng, ib=ib):
                            last = None
                            for j in range(4):
                                ic = ib * 4 + j
                                ap = h_fm[:, ic * 512:(ic + 1) * 512]
                                last = eng.tensor_tensor(
                                    ap, ap, ubuf[:, j * 512:(j + 1) * 512],
                                    ALU.mult)
                            return last
                        p.ph("vector", mul_h)

                    for ts in range(4):
                        for fg in range(2):
                            def mm_d(eng, ts=ts, fg=fg):
                                last = None
                                for ic in range(16):
                                    last = eng.matmul(
                                        bufs["ps_a"][:, 0:512],
                                        h_fm[:, ic * 512 + ts * 128:ic * 512 + ts * 128 + 128],
                                        wd_sb[:, ic * D + fg * 512:ic * D + fg * 512 + 512],
                                        start=(ic == 0), stop=(ic == 15))
                                return last
                            p.ph("tensor", mm_d)
                            par = dnct[0] % 2
                            dnct[0] += 1
                            mb = maccbs[par]
                            p.ph("vector", lambda eng, tc=tc, ts=ts, e=e, mb=mb:
                                 eng.tensor_scalar(mb[:], bufs["ps_a"][:, 0:512],
                                                   wgt_sb[:, tc * 4 + ts, e:e + 1],
                                                   None, ALU.mult),
                                 waits=[p.tail, macc_evs[par]])
                            tglob = t0 + ts * 128
                            macc_evs[par] = ev = p.dma(
                                "gpsimd",
                                [(t["macc"][s, tglob:tglob + 128,
                                            fg * 512:(fg + 1) * 512], mb[:])],
                                accum=(e > 0), chain=False, waits=[p.tail])

            moefm = p.sb("moefm", [128, 8 * 512], BF16, scope=sc_moe)
            m32 = p.sb("m32", [128, D], F32, scope=sc_moe)
            mbf = p.sb("mbf", [128, D], BF16, scope=sc_moe)
            first_f = [True]
            for tc in range(4):
                t0 = tc * 512
                for tsl in range(4):
                    tg = t0 + tsl * 128
                    if first_f[0]:
                        first_f[0] = False
                        p.dma("sync", [(m32[:], t["macc"][s, tg:tg + 128, :])],
                              waits=[p.tail] + [e for e in macc_evs if e])
                    else:
                        p.dma("sync", [(m32[:], t["macc"][s, tg:tg + 128, :])])
                    p.ph("vector", lambda eng: eng.tensor_copy(mbf[:], m32[:]))
                    for half in range(2):
                        def trm(eng, half=half):
                            last = None
                            for j in range(4):
                                c = half * 4 + j
                                last = eng.transpose(
                                    ps_tr[:, j * 128:(j + 1) * 128],
                                    mbf[:, c * 128:(c + 1) * 128], cc.ident[:])
                            return last
                        p.ph("tensor", trm)

                        def evm(eng, half=half, tsl=tsl):
                            last = None
                            for j in range(4):
                                c = half * 4 + j
                                last = eng.tensor_copy(
                                    moefm[:, c * 512 + tsl * 128:c * 512 + tsl * 128 + 128],
                                    ps_tr[:, j * 128:(j + 1) * 128])
                            return last
                        p.ph("vector", evm)
                for f in range(8):
                    p.dma("sync",
                          [(bufs["x32"][:, 0:512],
                            t["t1h"][s, f * 128:(f + 1) * 128, t0:t0 + 512]),
                           (bufs["tmp32"][:, 0:512],
                            t["xa"][s, f * 128:(f + 1) * 128, t0:t0 + 512])])

                    def fin(eng, f=f):
                        eng.tensor_tensor(bufs["xev"][:, 0:512],
                                          moefm[:, f * 512:(f + 1) * 512],
                                          bufs["x32"][:, 0:512], ALU.add)
                        return eng.tensor_tensor(bufs["xev"][:, 0:512],
                                                 bufs["xev"][:, 0:512],
                                                 bufs["tmp32"][:, 0:512], ALU.add)
                    p.ph("vector", fin)
                    ev = p.dma("sync",
                               [(t["y"][s, f * 128:(f + 1) * 128, t0:t0 + 512],
                                 bufs["xev"][:, 0:512])])

    return ev

def decls_layer2(nc, pref, given=None):
    t = dict(given or {})
    if "x" not in t:
        t["x"] = nc.dram_tensor(pref + "x", [2, D, T], F32, kind="ExternalInput").ap()
    if "wv_t" not in t:
        t["wv_t"] = nc.dram_tensor(pref + "wv_t", [D, D], F32, kind="ExternalInput").ap()
    if "bv_row" not in t:
        t["bv_row"] = nc.dram_tensor(pref + "bv_row", [1, D], F32, kind="ExternalInput").ap()
    if "wo_t" not in t:
        t["wo_t"] = nc.dram_tensor(pref + "wo_t", [D, D], F32, kind="ExternalInput").ap()
    if "bo_col" not in t:
        t["bo_col"] = nc.dram_tensor(pref + "bo_col", [128, 8], F32, kind="ExternalInput").ap()
    if "wr_t" not in t:
        t["wr_t"] = nc.dram_tensor(pref + "wr_t", [D, 128], F32, kind="ExternalInput").ap()
    if "cent" not in t:
        t["cent"] = nc.dram_tensor(pref + "cent", [16, 128], F32, kind="ExternalInput").ap()
    if "temp" not in t:
        t["temp"] = nc.dram_tensor(pref + "temp", [1, 1], F32, kind="ExternalInput").ap()
    if "ie" not in t:
        t["ie"] = nc.dram_tensor(pref + "ie", [16, D, 512], F32, kind="ExternalInput").ap()
    if "oe" not in t:
        t["oe"] = nc.dram_tensor(pref + "oe", [16, 512, 64], F32, kind="ExternalInput").ap()
    if "xa" not in t:
        t["xa"] = nc.dram_tensor(pref + "xa", [2, D, T], F32, kind="Internal").ap()
    if "y" not in t:
        t["y"] = nc.dram_tensor(pref + "y", [2, D, T], F32, kind="ExternalOutput").ap()
    return t

def emit_layer2(p, cc, bufs, t):
    nc = p.nc
    W = 512
    ps_score = bufs["ps_score"]
    ps_tr = bufs["ps_tr"]

    xn_bf = p.sb("xn_bf", [128, 8 * T], BF16)
    bias_col = p.sb("bias_col", [128, 8], F32)
    bv_bf = p.sb("bv_bf", [1, D], BF16)
    p.dma("sync", [(bias_col[:], t["bo_col"])])
    p.dma("gpsimd", [(bv_bf[:], t["bv_row"])])

    st = {
        "negb": p.sb("negb", [1, T], BF16),
        "avtm": p.sb("avtm", [128, 256], BF16),
        "rcp_col": p.sb("rcp_col", [128, 4], F32),
        "m8": p.sb("m8", [1, 2], F32),
        "ps_score": ps_score,
        "ps_b": bufs["ps_b"],
        "ps_tr": ps_tr,
        "ps_misc": bufs["ps_misc"],
        "sq": bufs["sq"],
        "mrow": bufs["mrow"],
    }

    def emit_negb_diag(p_, h, seqlen=T):
        kfm, negb = st["kfm"], st["negb"]
        sq = st["sq"]
        ps_misc_ = st["ps_misc"]
        hb = (h % 2) * 64
        pcol = (h // 2) * seqlen
        for c in range(seqlen // 512):
            p_.ph("scalar", lambda eng, c=c: eng.activation(
                sq[0:64, 0:512],
                kfm[hb:hb + 64, pcol + c * 512:pcol + (c + 1) * 512], AF.Square))
            p_.ph("tensor", lambda eng: eng.matmul(
                ps_misc_[0:1, 0:512], cc.ones_col[0:64, :], sq[0:64, 0:512],
                start=True, stop=True))
            p_.ph("vector", lambda eng, c=c: eng.tensor_scalar(
                negb[0:1, c * 512:(c + 1) * 512], ps_misc_[0:1, 0:512],
                -0.125, None, ALU.mult))
    st["emit_negb"] = emit_negb_diag

    for s in range(2):
        with ExitStack() as sc_attn:
            qfm = p.sb("qfm", [128, 8 * T], BF16, scope=sc_attn)
            kfm = p.sb("kfm", [128, 8 * T], BF16, scope=sc_attn)
            vtm = p.sb("vtm", [128, 16, 16, 65], BF16, scope=sc_attn)
            av_fm = p.sb("av_fm", [128, 8 * T], BF16, scope=sc_attn)
            wq_sb = p.sb("wq_sb", [128, 8 * 512], BF16, scope=sc_attn)
            qtm = p.sb("qtm", [128, 512], BF16, scope=sc_attn)
            st.update(qfm=qfm, kfm=kfm, vtm=vtm, av_fm=av_fm)

            emit_ln(p, cc, t["x"][s], xn_bf, bufs)
            p.ph("vector", lambda eng, vtm=vtm: eng.memset(vtm[:, :, :, 64:65], 1.0))

            for grp in range(2):
                p.dma("gpsimd",
                      [(wq_sb[:, c * 512:(c + 1) * 512],
                        t["wv_t"][c * 128:(c + 1) * 128,
                                  grp * 512:(grp + 1) * 512])
                       for c in range(8)])
                for tt in range(16):
                    t0 = tt * 128

                    def mm_p(eng, grp=grp, t0=t0):
                        for c in range(8):
                            eng.matmul(
                                bufs["ps_a"][:, 0:512],
                                xn_bf[:, c * T + t0:c * T + t0 + 128],
                                wq_sb[:, c * 512:(c + 1) * 512],
                                start=(c == 0), stop=False)
                        return eng.matmul(
                            bufs["ps_a"][:, 0:512], cc.ones_row[0:1, 0:128],
                            bv_bf[0:1, grp * 512:(grp + 1) * 512],
                            start=False, stop=True)
                    p.ph("tensor", mm_p)

                    p.ph("vector", lambda eng: eng.tensor_copy(
                        qtm[:, 0:512], bufs["ps_a"][:, 0:512]))

                    def tr(eng):
                        last = None
                        for j in range(4):
                            last = eng.transpose(
                                ps_tr[:, j * 128:(j + 1) * 128],
                                qtm[:, j * 128:(j + 1) * 128], cc.ident[:])
                        return last
                    p.ph("tensor", tr)

                    def ev(eng, grp=grp, t0=t0, qfm=qfm, kfm=kfm):
                        last = None
                        for j in range(4):
                            pair = grp * 4 + j
                            for half in range(2):
                                srcp = ps_tr[half * 64:half * 64 + 64,
                                             j * 128:(j + 1) * 128]
                                eng.tensor_copy(
                                    kfm[half * 64:half * 64 + 64,
                                        pair * T + t0:pair * T + t0 + 128], srcp)
                                last = eng.tensor_scalar(
                                    qfm[half * 64:half * 64 + 64,
                                        pair * T + t0:pair * T + t0 + 128],
                                    srcp, 0.125, None, ALU.mult)
                        return last
                    p.ph("vector", ev)

                    def evv(eng, grp=grp, tt=tt, vtm=vtm):
                        last = None
                        for j in range(8):
                            h = grp * 8 + j
                            last = eng.tensor_copy(
                                vtm[:, h, tt, 0:64],
                                bufs["ps_a"][:, j * 64:(j + 1) * 64])
                        return last
                    p.ph("vector", evv)

            PT = p.sb("PT", [128, 16 * 256], BF16, scope=sc_attn)
            st.update(PT=PT)
            emit_attn_core(p, cc, st, heads=16, causal=False, window=None, qw=256)

            wproj_sb = p.sb("wproj_sb", [128, 8 * 128], BF16, scope=sc_attn)
            for f in range(8):
                p.dma("gpsimd",
                      [(wproj_sb[:].rearrange("p (c g) -> p c g", c=8),
                        t["wo_t"][:, f * 128:(f + 1) * 128].rearrange(
                            "(c p) g -> p c g", p=128))])
                for tc in range(4):
                    t0 = tc * 512

                    def mm_o(eng, t0=t0, av_fm=av_fm, wproj_sb=wproj_sb):
                        last = None
                        for c in range(8):
                            last = eng.matmul(
                                bufs["ps_a"][:, 0:512],
                                wproj_sb[:, c * 128:(c + 1) * 128],
                                av_fm[:, c * T + t0:c * T + t0 + 512],
                                start=(c == 0), stop=(c == 7))
                        return last
                    p.ph("tensor", mm_o)
                    p.dma("sync", [(bufs["x32"][:, 0:512],
                                    t["x"][s, f * 128:(f + 1) * 128, t0:t0 + 512])])

                    def evac(eng, f=f):
                        return eng.scalar_tensor_tensor(
                            bufs["xev"][:, 0:512], bufs["ps_a"][:, 0:512],
                            bias_col[:, f:f + 1], bufs["x32"][:, 0:512],
                            ALU.add, ALU.add)
                    p.ph("vector", evac)
                    p.dma("sync", [(t["xa"][s, f * 128:(f + 1) * 128, t0:t0 + 512],
                                    bufs["xev"][:, 0:512])])

        with ExitStack() as sc_ffn:
            wr_sb = p.sb("wr_sb", [128, 8 * 128], BF16, scope=sc_ffn)
            cn_sb = p.sb("cn_sb", [16, 128], F32, scope=sc_ffn)
            cnb = p.sb("cnb", [16, 128], BF16, scope=sc_ffn)
            cnT = p.sb("cnT", [128, 16], BF16, scope=sc_ffn)
            ncol = p.sb("ncol", [128, 4], F32, scope=sc_ffn)
            zn_tm = p.sb("zn_tm", [128, 128], BF16, scope=sc_ffn)
            znT = p.sb("znT", [128, 128], BF16, scope=sc_ffn)
            lgt = p.sb("lgt", [128, 16], F32, scope=sc_ffn)
            prow = p.sb("prow", [128, 16], F32, scope=sc_ffn)
            ptmp = p.sb("ptmp", [128, 16], F32, scope=sc_ffn)
            mcol = p.sb("mcol", [128, 4], F32, scope=sc_ffn)
            wgt_sb = p.sb("wgt_sb", [128, 16, 16], F32, scope=sc_ffn)
            tcol = p.sb("tcol", [128, 1], F32, scope=sc_ffn)
            trow = p.sb("trow", [1, 1], F32, scope=sc_ffn)
            junk = p.sb("junk", [128, 512], F32, scope=sc_ffn)

            p.dma("gpsimd", [(wr_sb[:].rearrange("p (c g) -> p c g", c=8),
                              t["wr_t"].rearrange("(c p) g -> p c g", p=128)),
                             (cn_sb[:], t["cent"]),
                             (trow[:], t["temp"])])
            p.ph("scalar", lambda eng: eng.activation(junk[0:16, 0:128],
                                                      cn_sb[:], AF.Square,
                                                      accum_out=ncol[0:16, 0:1]))
            p.ph("scalar", lambda eng: eng.activation(ncol[0:16, 1:2],
                                                      ncol[0:16, 0:1], AF.Sqrt))
            p.ph("vector", lambda eng: eng.tensor_scalar_max(
                ncol[0:16, 1:2], ncol[0:16, 1:2], 1e-12), hard=True)
            p.ph("vector", lambda eng: eng.reciprocal(
                ncol[0:16, 2:3], ncol[0:16, 1:2]), hard=True)
            p.ph("vector", lambda eng: eng.tensor_scalar(
                cnb[:], cn_sb[:], ncol[0:16, 2:3], None, ALU.mult), hard=True)
            p.ph("tensor", lambda eng: eng.transpose(
                ps_tr[:, 0:16], cnb[:], cc.ident[0:16, 0:16]))
            p.ph("vector", lambda eng: eng.tensor_copy(cnT[:], ps_tr[:, 0:16]))
            p.ph("scalar", lambda eng: eng.activation(trow[:], trow[:], AF.Exp))
            p.ph("vector", lambda eng: eng.tensor_scalar(
                trow[:], trow[:], 1.0 / math.sqrt(128.0), None, ALU.mult),
                hard=True)
            tr_bf = p.sb("tr_bf", [1, 1], BF16, scope=sc_ffn)
            p.ph("vector", lambda eng: eng.tensor_copy(tr_bf[:], trow[:]),
                 hard=True)
            p.ph("tensor", lambda eng: eng.matmul(
                bufs["ps_misc"][:, 0:1], cc.ones_row[0:1, :], tr_bf[:],
                start=True, stop=True))
            p.ph("vector", lambda eng: eng.tensor_copy(
                tcol[:], bufs["ps_misc"][:, 0:1]))

            for tt in range(16):
                t0 = tt * 128
                def mm_z(eng, t0=t0):
                    last = None
                    for c in range(8):
                        last = eng.matmul(
                            bufs["ps_a"][:, 0:128],
                            xn_bf[:, c * T + t0:c * T + t0 + 128],
                            wr_sb[:, c * 128:(c + 1) * 128],
                            start=(c == 0), stop=(c == 7))
                    return last
                p.ph("tensor", mm_z)
                p.ph("scalar", lambda eng: eng.activation(
                    junk[:, 0:128], bufs["ps_a"][:, 0:128], AF.Square,
                    accum_out=ncol[:, 0:1]))
                p.ph("scalar", lambda eng: eng.activation(
                    ncol[:, 1:2], ncol[:, 0:1], AF.Sqrt), hard=True)
                p.ph("vector", lambda eng: eng.tensor_scalar_max(
                    ncol[:, 1:2], ncol[:, 1:2], 1e-12))
                p.ph("vector", lambda eng: eng.reciprocal(
                    ncol[:, 2:3], ncol[:, 1:2]), hard=True)
                p.ph("vector", lambda eng: eng.tensor_scalar(
                    zn_tm[:], bufs["ps_a"][:, 0:128], ncol[:, 2:3], None,
                    ALU.mult), hard=True)
                p.ph("tensor", lambda eng: eng.transpose(
                    ps_tr[:, 0:128], zn_tm[:], cc.ident[:]))
                p.ph("vector", lambda eng: eng.tensor_copy(
                    znT[:], ps_tr[:, 0:128]))
                p.ph("tensor", lambda eng: eng.matmul(
                    bufs["ps_misc"][:, 0:16], znT[:], cnT[:],
                    start=True, stop=True))
                p.ph("vector", lambda eng: eng.tensor_scalar(
                    lgt[:], bufs["ps_misc"][:, 0:16], tcol[:, 0:1], None,
                    ALU.mult))
                p.ph("vector", lambda eng: eng.reduce_max(
                    mcol[:, 0:1], lgt[:], axis=AX.X))
                p.ph("vector", lambda eng: eng.tensor_scalar(
                    mcol[:, 0:1], mcol[:, 0:1], -1.0, None, ALU.mult),
                    hard=True)
                p.ph("scalar", lambda eng: eng.activation(
                    prow[:], lgt[:], AF.Exp, bias=mcol[:, 0:1],
                    accum_out=mcol[:, 1:2]))
                p.ph("vector", lambda eng: eng.reciprocal(
                    mcol[:, 2:3], mcol[:, 1:2]))
                p.ph("vector", lambda eng: eng.tensor_scalar(
                    prow[:], prow[:], mcol[:, 2:3], None, ALU.mult), hard=True)
                emit_top2(p, prow, ptmp, mcol, wgt_sb[:, tt, 0:16], 16,
                          renorm=False)

            ie_sb = p.sb("ie_sb", [128, 8 * 512], BF16, scope=sc_ffn)
            oe_sb = p.sb("oe_sb", [128, 4 * 64], BF16, scope=sc_ffn)
            hh = p.sb("hh", [128, 4 * 512], BF16, scope=sc_ffn)
            y_tm = p.sb("y_tm", [128, 4 * D], BF16, scope=sc_ffn)
            yfm = p.sb("yfm", [128, 8 * 512], BF16, scope=sc_ffn)
            for tc in range(4):
                t0 = tc * 512
                for h in range(16):
                    p.dma("gpsimd",
                          [(ie_sb[:, c * 512:(c + 1) * 512],
                            t["ie"][h, c * 128:(c + 1) * 128, :])
                           for c in range(8)] +
                          [(oe_sb[:].rearrange("p (c g) -> p c g", c=4),
                            t["oe"][h].rearrange("(c p) g -> p c g", p=128))])

                    def mm_h(eng, t0=t0):
                        last = None
                        for hc in range(4):
                            for c in range(8):
                                last = eng.matmul(
                                    ps_score[:, hc * 512:(hc + 1) * 512],
                                    ie_sb[:, c * 512 + hc * 128:c * 512 + hc * 128 + 128],
                                    xn_bf[:, c * T + t0:c * T + t0 + 512],
                                    start=(c == 0), stop=(c == 7))
                        return last
                    p.ph("tensor", mm_h)

                    def gelu(eng):
                        last = None
                        for hc in range(4):
                            last = eng.activation(
                                hh[:, hc * 512:(hc + 1) * 512],
                                ps_score[:, hc * 512:(hc + 1) * 512], AF.Gelu)
                        return last
                    p.ph("scalar", gelu)

                    def mm_oe(eng):
                        last = None
                        for ts in range(4):
                            po = bufs["ps_b"][:, ts * 64:(ts + 1) * 64]
                            for hc in range(4):
                                last = eng.matmul(
                                    po,
                                    hh[:, hc * 512 + ts * 128:hc * 512 + ts * 128 + 128],
                                    oe_sb[:, hc * 64:(hc + 1) * 64],
                                    start=(hc == 0), stop=(hc == 3))
                        return last
                    p.ph("tensor", mm_oe)

                    def evy(eng, h=h, tc=tc):
                        last = None
                        for ts in range(4):
                            last = eng.tensor_scalar(
                                y_tm[:, ts * D + h * 64:ts * D + h * 64 + 64],
                                bufs["ps_b"][:, ts * 64:(ts + 1) * 64],
                                wgt_sb[:, tc * 4 + ts, h:h + 1], None, ALU.mult)
                        return last
                    p.ph("vector", evy)

                for tsl in range(4):
                    for half in range(2):
                        def trm(eng, half=half, tsl=tsl):
                            last = None
                            for j in range(4):
                                c = half * 4 + j
                                last = eng.transpose(
                                    ps_tr[:, j * 128:(j + 1) * 128],
                                    y_tm[:, tsl * D + c * 128:tsl * D + (c + 1) * 128],
                                    cc.ident[:])
                            return last
                        p.ph("tensor", trm)

                        def evm(eng, half=half, tsl=tsl):
                            last = None
                            for j in range(4):
                                c = half * 4 + j
                                last = eng.tensor_copy(
                                    yfm[:, c * 512 + tsl * 128:c * 512 + tsl * 128 + 128],
                                    ps_tr[:, j * 128:(j + 1) * 128])
                            return last
                        p.ph("vector", evm)
                for f in range(8):
                    p.dma("sync",
                          [(bufs["x32"][:, 0:512],
                            t["xa"][s, f * 128:(f + 1) * 128, t0:t0 + 512])])

                    def fin(eng, f=f):
                        return eng.tensor_tensor(bufs["xev"][:, 0:512],
                                                 yfm[:, f * 512:(f + 1) * 512],
                                                 bufs["x32"][:, 0:512], ALU.add)
                    p.ph("vector", fin)
                    ev = p.dma("sync",
                               [(t["y"][s, f * 128:(f + 1) * 128, t0:t0 + 512],
                                 bufs["xev"][:, 0:512])])

    return ev

def decls_layer3_attn(nc, pref, given=None):
    t = dict(given or {})
    if "x" not in t:
        t["x"] = nc.dram_tensor(pref + "x", [2, D, T], F32, kind="ExternalInput").ap()
    for nm in ("wq_t", "wk_t", "wv_t", "wa_t", "war_t", "wo_t"):
        if nm not in t:
            t[nm] = nc.dram_tensor(pref + nm, [D, D], F32, kind="ExternalInput").ap()
    if "wbg_t" not in t:
        t["wbg_t"] = nc.dram_tensor(pref + "wbg_t", [D, 32], F32, kind="ExternalInput").ap()
    if "ba_row" not in t:
        t["ba_row"] = nc.dram_tensor(pref + "ba_row", [1, D], F32, kind="ExternalInput").ap()
    if "bar_row" not in t:
        t["bar_row"] = nc.dram_tensor(pref + "bar_row", [1, D], F32, kind="ExternalInput").ap()
    if "bbg_row" not in t:
        t["bbg_row"] = nc.dram_tensor(pref + "bbg_row", [1, 32], F32, kind="ExternalInput").ap()
    if "lnw_row" not in t:
        t["lnw_row"] = nc.dram_tensor(pref + "lnw_row", [1, D], F32, kind="ExternalInput").ap()
    if "lnb_col" not in t:
        t["lnb_col"] = nc.dram_tensor(pref + "lnb_col", [128, 8], F32, kind="ExternalInput").ap()
    if "xnh" not in t:
        t["xnh"] = nc.dram_tensor(pref + "xnh", [2, 128, 8 * T], BF16, kind="Internal").ap()
    if "proj_h" not in t:
        t["proj_h"] = nc.dram_tensor(pref + "proj_h", [2, 5, T, D], BF16, kind="Internal").ap()
    if "bgh" not in t:
        t["bgh"] = nc.dram_tensor(pref + "bgh", [2, T, 32], F32, kind="Internal").ap()
    if "ofm_h" not in t:
        t["ofm_h"] = nc.dram_tensor(pref + "ofm_h", [2, 128, 8 * T], BF16, kind="Internal").ap()
    if "y" not in t:
        t["y"] = nc.dram_tensor(pref + "y", [2, D, T], F32, kind="ExternalOutput").ap()
    return t

def emit_layer3_attn(p, cc, bufs, t):
    nc = p.nc
    W = 512
    ps_score = bufs["ps_score"]
    ps_tr = bufs["ps_tr"]

    lnw_row = p.sb("lnw_row", [1, D], BF16)
    lnb_col = p.sb("lnb_col", [128, 8], F32)
    ba_row = p.sb("ba_row", [1, D], BF16)
    bar_row = p.sb("bar_row", [1, D], BF16)
    bbg_row = p.sb("bbg_row", [1, 32], BF16)
    p.dma("gpsimd", [(lnw_row[:], t["lnw_row"]), (ba_row[:], t["ba_row"]),
                     (bar_row[:], t["bar_row"]), (bbg_row[:], t["bbg_row"])])
    p.dma("sync", [(lnb_col[:], t["lnb_col"])])

    for s in range(2):
        emit_ln(p, cc, t["x"][s], t["xnh"][s], bufs, dst_hbm=True)
        with ExitStack() as sc_proj:
            xn2 = p.sb("xn2", [128, 8 * T], BF16, scope=sc_proj)
            emit_ln(p, cc, t["xnh"][s], xn2, bufs, w_row=lnw_row,
                    src_bf16=True, b_col=lnb_col)
            wq_sb = p.sb("wq_sb", [128, 8 * 512], BF16, scope=sc_proj)
            ptile = p.sb("ptile", [128, 512], BF16, scope=sc_proj)
            projs = [("wq_t", 0, None), ("wk_t", 1, None), ("wv_t", 2, None),
                     ("wa_t", 3, ba_row), ("war_t", 4, bar_row)]
            for wname, pi, brow in projs:
                for grp in range(2):
                    p.dma("gpsimd",
                          [(wq_sb[:, c * 512:(c + 1) * 512],
                            t[wname][c * 128:(c + 1) * 128,
                                     grp * 512:(grp + 1) * 512])
                           for c in range(8)])
                    for tt in range(16):
                        t0 = tt * 128

                        def mm_p2(eng, grp=grp, t0=t0, brow=brow, wq_sb=wq_sb,
                                  xn2=xn2):
                            last = None
                            for c in range(8):
                                last = eng.matmul(
                                    bufs["ps_a"][:, 0:512],
                                    xn2[:, c * T + t0:c * T + t0 + 128],
                                    wq_sb[:, c * 512:(c + 1) * 512],
                                    start=(c == 0),
                                    stop=(c == 7 and brow is None))
                            if brow is not None:
                                last = eng.matmul(
                                    bufs["ps_a"][:, 0:512],
                                    cc.ones_row[0:1, 0:128],
                                    brow[0:1, grp * 512:(grp + 1) * 512],
                                    start=False, stop=True)
                            return last
                        p.ph("tensor", mm_p2)

                        if brow is None:
                            p.ph("vector", lambda eng, ptile=ptile:
                                 eng.tensor_copy(ptile[:],
                                                 bufs["ps_a"][:, 0:512]))
                        else:
                            p.ph("scalar", lambda eng, ptile=ptile:
                                 eng.activation(ptile[:], bufs["ps_a"][:, 0:512],
                                                AF.Sigmoid))
                        p.dma("sync",
                              [(t["proj_h"][s, pi, t0:t0 + 128,
                                            grp * 512:(grp + 1) * 512],
                                ptile[:])])
            wbg_sb = p.sb("wbg_sb", [128, 8, 32], BF16, scope=sc_proj)
            pbg = p.sb("pbg", [128, 32], F32, scope=sc_proj)
            p.dma("gpsimd", [(wbg_sb[:],
                              t["wbg_t"].rearrange("(c p) e -> p c e", p=128))])
            for tt in range(16):
                t0 = tt * 128

                def mm_bg(eng, t0=t0, wbg_sb=wbg_sb, xn2=xn2):
                    for c in range(8):
                        eng.matmul(bufs["ps_a"][:, 0:32],
                                   xn2[:, c * T + t0:c * T + t0 + 128],
                                   wbg_sb[:, c, :], start=(c == 0), stop=False)
                    return eng.matmul(bufs["ps_a"][:, 0:32],
                                      cc.ones_row[0:1, 0:128], bbg_row[0:1, :],
                                      start=False, stop=True)
                p.ph("tensor", mm_bg)
                p.ph("scalar", lambda eng, pbg=pbg: eng.activation(
                    pbg[:], bufs["ps_a"][:, 0:32], AF.Sigmoid))
                p.dma("sync", [(t["bgh"][s, t0:t0 + 128, :], pbg[:])])

        with ExitStack() as sc_scan:
            z = {"bufs": bufs, "ps_score": ps_score, "ps_tr": ps_tr,
                 "proj_h": t["proj_h"][s], "bgh": t["bgh"][s],
                 "ofm_h": t["ofm_h"][s]}
            for nm in ("qtA", "ktA", "vtA", "atA", "artA",
                       "qtB", "ktB", "vtB", "atB", "artB",
                       "kn", "lna_bf", "qpb",
                       "ctb", "ktb", "cpb", "r_bf", "ktbS"):
                z[nm] = p.sb(nm, [128, 64], BF16, scope=sc_scan)
            z["bg_sbA"] = p.sb("bg_sbA", [128, 32], F32, scope=sc_scan)
            z["bg_sbB"] = p.sb("bg_sbB", [128, 32], F32, scope=sc_scan)
            for nm in ("scr", "lna32", "e1", "e2", "e3"):
                z[nm] = p.sb(nm, [128, 64], F32, scope=sc_scan)
            for nm in ("qpbT", "ctbT", "ktbT", "cpbT", "qpbTS"):
                z[nm] = p.sb(nm, [64, 128], BF16, scope=sc_scan)
            for nm in ("msb0", "msb1", "mtb0", "mtb1", "btb", "btbS", "aptb"):
                z[nm] = p.sb(nm, [128, 128], BF16, scope=sc_scan)
            z["u_sb"] = p.sb("u_sb", [128, 64], BF16, scope=sc_scan)
            z["u2_sb"] = p.sb("u2_sb", [128, 64], BF16, scope=sc_scan)
            z["otile"] = p.sb("otile", [128, 64], BF16, scope=sc_scan)
            z["ofm_st"] = p.sb("ofm_st", [64, 128], BF16, scope=sc_scan)
            z["S32"] = p.sb("S32", [64, 64], F32, scope=sc_scan)
            z["R32"] = p.sb("R32", [64, 64], F32, scope=sc_scan)
            z["S_bf"] = p.sb("S_bf", [64, 64], BF16, scope=sc_scan)
            z["R_bf"] = p.sb("R_bf", [64, 64], BF16, scope=sc_scan)
            z["ncol"] = p.sb("ncol", [128, 4], F32, scope=sc_scan)
            z["pc_col"] = p.sb("pc_col", [64, 2], F32, scope=sc_scan)
            units = [(h, ch) for h in range(16) for ch in range(16)]
            ld_ev = emit_l3_tiles(p, z, 0, 0, "A", chain=True)
            for ui, (h, ch) in enumerate(units):
                if ch == 0:
                    def init(eng):
                        eng.memset(z["S32"][:], 0.0)
                        eng.memset(z["R32"][:], 0.0)
                        eng.memset(z["S_bf"][:], 0.0)
                        return eng.memset(z["R_bf"][:], 0.0)
                    p.ph("vector", init)
                par = "A" if ui % 2 == 0 else "B"
                nxt = units[ui + 1] if ui + 1 < len(units) else None
                nxt_ld = emit_l3_chunk(p, cc, z, s, h, ch, par, ld_ev, nxt)
                ld_ev = nxt_ld

        with ExitStack() as sc_out:
            wproj_sb = p.sb("wproj_sb", [128, 8 * 128], BF16, scope=sc_out)
            ofm_sb = p.sb("ofm_sb", [128, 8 * 512], BF16, scope=sc_out)
            xnb_sb = p.sb("xnb_sb", [128, 512], BF16, scope=sc_out)
            for tc in range(4):
                t0 = tc * 512
                p.dma("sync", [(ofm_sb[:, c * 512:(c + 1) * 512],
                                t["ofm_h"][s][:, c * T + t0:c * T + t0 + 512])
                               for c in range(8)])
                for f in range(8):
                    p.dma("gpsimd",
                          [(wproj_sb[:].rearrange("p (c g) -> p c g", c=8),
                            t["wo_t"][:, f * 128:(f + 1) * 128].rearrange(
                                "(c p) g -> p c g", p=128))])

                    def mm_o(eng, ofm_sb=ofm_sb, wproj_sb=wproj_sb):
                        last = None
                        for c in range(8):
                            last = eng.matmul(
                                bufs["ps_a"][:, 0:512],
                                wproj_sb[:, c * 128:(c + 1) * 128],
                                ofm_sb[:, c * 512:(c + 1) * 512],
                                start=(c == 0), stop=(c == 7))
                        return last
                    p.ph("tensor", mm_o)
                    p.dma("sync",
                          [(bufs["x32"][:, 0:512],
                            t["x"][s, f * 128:(f + 1) * 128, t0:t0 + 512]),
                           (xnb_sb[:],
                            t["xnh"][s][:, f * T + t0:f * T + t0 + 512])])

                    def fin(eng):
                        eng.tensor_tensor(bufs["xev"][:, 0:512],
                                          bufs["ps_a"][:, 0:512], xnb_sb[:],
                                          ALU.add)
                        return eng.tensor_tensor(bufs["xev"][:, 0:512],
                                                 bufs["xev"][:, 0:512],
                                                 bufs["x32"][:, 0:512], ALU.add)
                    p.ph("vector", fin)
                    ev = p.dma("sync",
                               [(t["y"][s, f * 128:(f + 1) * 128, t0:t0 + 512],
                                 bufs["xev"][:, 0:512])])

    return ev

def decls_layer3_ffn(nc, pref, given=None):
    t = dict(given or {})
    if "x" not in t:
        t["x"] = nc.dram_tensor(pref + "x", [2, D, T], F32, kind="ExternalInput").ap()
    if "xa" not in t:
        t["xa"] = nc.dram_tensor(pref + "xa", [2, D, T], F32, kind="ExternalInput").ap()
    if "prefrep" not in t:
        t["prefrep"] = nc.dram_tensor(pref + "prefrep", [1, 256], F32, kind="ExternalInput").ap()
    t["cnt_own"] = nc.dram_tensor(pref + "cnt_own", [1, 16], F32, kind="Internal").ap()
    t["cnt_all"] = nc.dram_tensor(pref + "cnt_all", [8, 16], F32, kind="Internal").ap()
    t["wr_h"] = nc.dram_tensor(pref + "wr_h", [2, T, 16], F32, kind="Internal").ap()
    if "rw_t" not in t:
        t["rw_t"] = nc.dram_tensor(pref + "rw_t", [D, 8], F32, kind="ExternalInput").ap()
    if "sg" not in t:
        t["sg"] = nc.dram_tensor(pref + "sg", [D, 8192], F32, kind="ExternalInput").ap()
    if "su" not in t:
        t["su"] = nc.dram_tensor(pref + "su", [D, 8192], F32, kind="ExternalInput").ap()
    if "sd" not in t:
        t["sd"] = nc.dram_tensor(pref + "sd", [8192, D], F32, kind="ExternalInput").ap()
    if "rg" not in t:
        t["rg"] = nc.dram_tensor(pref + "rg", [8, D, 4096], F32, kind="ExternalInput").ap()
    if "ru" not in t:
        t["ru"] = nc.dram_tensor(pref + "ru", [8, D, 4096], F32, kind="ExternalInput").ap()
    if "rd" not in t:
        t["rd"] = nc.dram_tensor(pref + "rd", [8, 4096, D], F32, kind="ExternalInput").ap()
    if "xnh" not in t:
        t["xnh"] = nc.dram_tensor(pref + "xnh", [2, 128, 8 * T], BF16, kind="Internal").ap()
    if "wgt_h" not in t:
        t["wgt_h"] = nc.dram_tensor(pref + "wgt_h", [2, T, 8], F32, kind="Internal").ap()
    if "macc" not in t:
        t["macc"] = nc.dram_tensor(pref + "macc", [2, T, D], F32, kind="Internal").ap()
    if "y" not in t:
        t["y"] = nc.dram_tensor(pref + "y", [2, D, T], F32, kind="ExternalOutput").ap()
    return t

def emit_layer3_ffn(p, cc, bufs, t):
    nc = p.nc
    W = 512
    ps_score = bufs["ps_score"]
    ps_tr = bufs["ps_tr"]

    for s in range(2):
        emit_ln(p, cc, t["x"][s], t["xnh"][s], bufs, dst_hbm=True)

    with ExitStack() as sc_r:
        rw_sb = p.sb("rw_sb", [128, 8, 8], BF16, scope=sc_r)
        xnc = p.sb("xnc", [128, 8 * 512], BF16, scope=sc_r)
        sqr = p.sb("sqr", [128, 512], BF16, scope=sc_r)
        r_row = p.sb("r_row", [1, 512], F32, scope=sc_r)
        rcol = p.sb("rcol", [128, 1], F32, scope=sc_r)
        rcol_bf = p.sb("rcol_bf", [128, 1], BF16, scope=sc_r)
        rrow_bf = p.sb("rrow_bf", [1, 512], BF16, scope=sc_r)
        prow = p.sb("prow", [128, 8], F32, scope=sc_r)
        ptmp = p.sb("ptmp", [128, 8], F32, scope=sc_r)
        mcol = p.sb("mcol", [128, 4], F32, scope=sc_r)
        wdw = p.sb("wdw", [128, 8], F32, scope=sc_r)
        ind = p.sb("ind", [128, 8], F32, scope=sc_r)
        ind_bf = p.sb("ind_bf", [128, 8], BF16, scope=sc_r)
        base32 = p.sb("base32", [1, 8], F32, scope=sc_r)
        base_bf = p.sb("base_bf", [1, 8], BF16, scope=sc_r)
        vm = p.sb("vm", [128, 8], F32, scope=sc_r)
        rk_sb = p.sb("rk_sb", [128, 8], F32, scope=sc_r)
        cnt_row = p.sb("cnt_row", [1, 128], F32, scope=sc_r)
        pmrep = p.sb("pmrep", [1, 256], F32, scope=sc_r)
        pbase = p.sb("pbase", [1, 16], F32, scope=sc_r)
        pbase_bf = p.sb("pbase_bf", [1, 16], BF16, scope=sc_r)
        cnt2 = p.sb("cnt2", [1, 16], F32, scope=sc_r)
        junkr = p.sb("junkr", [1, 128], F32, scope=sc_r)

        p.dma("gpsimd", [(rw_sb[:], t["rw_t"].rearrange("(c p) e -> p c e", p=128))])
        p.dma("sync", [(pmrep[:], t["prefrep"])])
        eps32 = float(np.finfo(np.float32).eps)

        for sl in range(2):
            p.ph("vector", lambda eng: (eng.memset(base32[:], 0.0),
                                        eng.memset(base_bf[:], 0.0))[-1])
            for tc in range(4):
                t0c = tc * 512
                p.dma("sync", [(xnc[:, c * 512:(c + 1) * 512],
                                t["xnh"][sl][:, c * T + t0c:c * T + t0c + 512])
                               for c in range(8)])
                for c in range(8):
                    p.ph("scalar", lambda eng, c=c: eng.activation(
                        sqr[:, 0:512], xnc[:, c * 512:(c + 1) * 512], AF.Square))

                    def mmq(eng, c=c):
                        return eng.matmul(bufs["ps_misc"][0:1, 0:512],
                                          cc.ones_col[:], sqr[:, 0:512],
                                          start=(c == 0), stop=(c == 7))
                    p.ph("tensor", mmq)
                p.ph("vector", lambda eng: eng.tensor_scalar(
                    r_row[0:1, :], bufs["ps_misc"][0:1, 0:512], 1.0 / D, None,
                    ALU.mult))
                p.ph("vector", lambda eng: eng.tensor_scalar(
                    r_row[0:1, :], r_row[0:1, :], eps32, None, ALU.add),
                    hard=True)
                p.ph("scalar", lambda eng: eng.activation(
                    r_row[0:1, :], r_row[0:1, :], AF.Sqrt))
                p.ph("vector", lambda eng: eng.reciprocal(
                    r_row[0:1, :], r_row[0:1, :]))
                p.ph("vector", lambda eng: eng.tensor_copy(
                    rrow_bf[0:1, :], r_row[0:1, :]), hard=True)
                for tsl in range(4):
                    tt = tc * 4 + tsl
                    t0 = tt * 128
                    p.ph("tensor", lambda eng, tsl=tsl: eng.transpose(
                        ps_tr[0:128, 0:1],
                        rrow_bf[0:1, tsl * 128:(tsl + 1) * 128],
                        cc.ident[0:1, 0:1]))
                    p.ph("vector", lambda eng: eng.tensor_copy(
                        rcol[:, 0:1], ps_tr[0:128, 0:1]))

                    def mm_r(eng, tsl=tsl):
                        last = None
                        for c in range(8):
                            last = eng.matmul(
                                bufs["ps_misc"][:, 0:8],
                                xnc[:, c * 512 + tsl * 128:c * 512 + tsl * 128 + 128],
                                rw_sb[:, c, :], start=(c == 0), stop=(c == 7))
                        return last
                    p.ph("tensor", mm_r)
                    p.ph("vector", lambda eng: eng.tensor_scalar(
                        prow[:], bufs["ps_misc"][:, 0:8], rcol[:, 0:1], None,
                        ALU.mult), hard=True)
                    p.ph("vector", lambda eng: eng.reduce_max(
                        mcol[:, 0:1], prow[:], axis=AX.X), hard=True)
                    p.ph("vector", lambda eng: eng.tensor_scalar(
                        mcol[:, 0:1], mcol[:, 0:1], -1.0, None, ALU.mult),
                        hard=True)
                    p.ph("scalar", lambda eng: eng.activation(
                        prow[:], prow[:], AF.Exp, bias=mcol[:, 0:1],
                        accum_out=mcol[:, 1:2]))
                    p.ph("vector", lambda eng: eng.reciprocal(
                        mcol[:, 2:3], mcol[:, 1:2]))
                    p.ph("vector", lambda eng: eng.tensor_scalar(
                        prow[:], prow[:], mcol[:, 2:3], None, ALU.mult),
                        hard=True)
                    emit_top2(p, prow, ptmp, mcol, wdw[:, 0:8], 8, renorm=True,
                              ind_dst=ind[:, 0:8])
                    p.ph("vector", lambda eng: eng.tensor_copy(
                        ind_bf[:], ind[:]), hard=True)

                    def mm_rank(eng):
                        eng.matmul(bufs["ps_a"][:, 0:8], cc.m_utS[:], ind_bf[:],
                                   start=True, stop=False)
                        return eng.matmul(bufs["ps_a"][:, 0:8],
                                          cc.ones_row[0:1, :], base_bf[0:1, :],
                                          start=False, stop=True)
                    p.ph("tensor", mm_rank)
                    p.ph("vector", lambda eng: eng.tensor_copy(
                        rk_sb[:], bufs["ps_a"][:, 0:8]))
                    p.dma("sync",
                          [(t["wr_h"][sl, t0:t0 + 128, 0:8], wdw[:]),
                           (t["wr_h"][sl, t0:t0 + 128, 8:16], rk_sb[:])])
                    p.ph("tensor", lambda eng: eng.matmul(
                        bufs["ps_b"][0:1, 0:8], cc.ones_col[:], ind_bf[:],
                        start=True, stop=True))
                    p.ph("vector", lambda eng: eng.tensor_tensor(
                        base32[:], base32[:], bufs["ps_b"][0:1, 0:8], ALU.add),
                        hard=True)
                    p.ph("vector", lambda eng: eng.tensor_copy(
                        base_bf[:], base32[:]), hard=True)
            p.ph("vector", lambda eng, sl=sl: eng.tensor_copy(
                cnt2[0:1, sl * 8:(sl + 1) * 8], base32[0:1, :]), hard=True)
        ev_c = p.dma("sync", [(t["cnt_own"][:], cnt2[0:1, 0:16])])

        COLL = False
        if COLL:
            sname = "d_gpsimd"
            p.cnt[sname] += 16
            cval = p.cnt[sname]
            csem = p.sem[sname]
            tail_ev = p.tail

            def coll(eng, tail_ev=tail_ev, csem=csem):
                p._emit_wait(eng, "gpsimd", tail_ev)
                eng.collective_compute(
                    "AllGather", ALU.bypass, [list(range(8))],
                    [t["cnt_own"]], [t["cnt_all"]],
                ).then_inc(csem, 16)
            p.q["gpsimd"].append(coll)
            p.tail = ("dma", sname, cval)
        else:
            p.dma("gpsimd", [(t["cnt_all"][0:1, :], t["cnt_own"][:])])

        p.dma("sync", [(cnt_row[0:1, 0:128],
                        t["cnt_all"].rearrange("a b -> (a b)").rearrange(
                            "(a b) -> a b", a=1))])
        for sl in range(2):
            p.ph("vector", lambda eng, sl=sl: eng.tensor_tensor(
                junkr[0:1, 0:128], cnt_row[0:1, :],
                pmrep[0:1, sl * 128:(sl + 1) * 128], ALU.mult), hard=True)
            p.ph("vector", lambda eng, sl=sl: eng.reduce_sum(
                pbase[0:1, sl * 8:(sl + 1) * 8],
                junkr[0:1, 0:128].rearrange("p (g e) -> p e g", e=8),
                axis=AX.X), hard=True)
        p.ph("vector", lambda eng: eng.tensor_copy(pbase_bf[:], pbase[:]),
             hard=True)

        for sl in range(2):
            def bc(eng, sl=sl):
                return eng.matmul(bufs["ps_b"][:, 0:8], cc.ones_row[0:1, :],
                                  pbase_bf[0:1, sl * 8:(sl + 1) * 8],
                                  start=True, stop=True)
            p.ph("tensor", bc)
            for tt in range(16):
                t0 = tt * 128
                p.dma("sync", [(wdw[:], t["wr_h"][sl, t0:t0 + 128, 0:8]),
                               (rk_sb[:], t["wr_h"][sl, t0:t0 + 128, 8:16])])
                p.ph("vector", lambda eng: eng.tensor_tensor(
                    vm[:], rk_sb[:], bufs["ps_b"][:, 0:8], ALU.add))
                p.ph("vector", lambda eng: eng.tensor_scalar(
                    vm[:], vm[:], L3_CAP, None, ALU.is_lt), hard=True)
                p.ph("vector", lambda eng: eng.tensor_tensor(
                    vm[:], vm[:], wdw[:], ALU.mult), hard=True)
                p.dma("sync", [(t["wgt_h"][sl, t0:t0 + 128, :], vm[:])])

    slabs = []
    for q in range(4):
        slabs.append(("s", q, 8))
    for e in range(8):
        for hf in range(2):
            slabs.append(("r", (e, hf), e))

    with ExitStack() as sc_s:
        wg_sb = p.sb("wg_sb", [128, 8 * 2048], BF16, scope=sc_s)
        wu_sb = p.sb("wu_sb", [128, 8 * 2048], BF16, scope=sc_s)
        wd_sb = p.sb("wd_sb", [128, 16 * D], BF16, scope=sc_s)
        h_fm = p.sb("h_fm", [128, 16 * 512], BF16, scope=sc_s)
        ubuf = p.sb("ubuf", [128, 4 * 512], BF16, scope=sc_s)
        xch = p.sb("xch", [128, 8 * 512], BF16, scope=sc_s)
        wgt3 = p.sb("wgt3", [128, 16, 9], F32, scope=sc_s)
        maccbs = [p.sb("maccb", [128, 512], F32, scope=sc_s) for _ in range(2)]
        macc_evs = [None, None]
        dnct = [0]

        for si, (kind, idx, widx) in enumerate(slabs):
            if kind == "s":
                q = idx
                gsrc = t["sg"][:, q * 2048:(q + 1) * 2048]
                usrc = t["su"][:, q * 2048:(q + 1) * 2048]
                dsrc = t["sd"][q * 2048:(q + 1) * 2048, :]
            else:
                e, hf = idx
                gsrc = t["rg"][e][:, hf * 2048:(hf + 1) * 2048]
                usrc = t["ru"][e][:, hf * 2048:(hf + 1) * 2048]
                dsrc = t["rd"][e][hf * 2048:(hf + 1) * 2048, :]
            p.dma("gpsimd",
                  [(wg_sb[:, c * 2048:(c + 1) * 2048],
                    gsrc[c * 128:(c + 1) * 128, :]) for c in range(8)] +
                  [(wu_sb[:, c * 2048:(c + 1) * 2048],
                    usrc[c * 128:(c + 1) * 128, :]) for c in range(8)] +
                  [(wd_sb[:, ic * D:(ic + 1) * D],
                    dsrc[ic * 128:(ic + 1) * 128, :]) for ic in range(16)])
            for s in range(2):
                p.dma("sync", [(wgt3[:, :, 0:8],
                                t["wgt_h"][s].rearrange("(tt p) e -> p tt e",
                                                        p=128))])
                p.ph("vector", lambda eng: eng.memset(wgt3[:, :, 8:9], 1.0))
                for tc in range(4):
                    t0 = tc * 512
                    p.dma("sync", [(xch[:, c * 512:(c + 1) * 512],
                                    t["xnh"][s][:, c * T + t0:c * T + t0 + 512])
                                   for c in range(8)])
                    for ib in range(4):
                        def mm_g(eng, ib=ib):
                            last = None
                            for j in range(4):
                                ic = ib * 4 + j
                                for c in range(8):
                                    last = eng.matmul(
                                        ps_score[:, j * 512:(j + 1) * 512],
                                        wg_sb[:, c * 2048 + ic * 128:c * 2048 + ic * 128 + 128],
                                        xch[:, c * 512:(c + 1) * 512],
                                        start=(c == 0), stop=(c == 7))
                            return last
                        p.ph("tensor", mm_g)

                        def silu(eng, ib=ib):
                            last = None
                            for j in range(4):
                                ic = ib * 4 + j
                                last = eng.activation(
                                    h_fm[:, ic * 512:(ic + 1) * 512],
                                    ps_score[:, j * 512:(j + 1) * 512], AF.Silu)
                            return last
                        p.ph("scalar", silu)

                        def mm_u(eng, ib=ib):
                            last = None
                            for j in range(4):
                                ic = ib * 4 + j
                                for c in range(8):
                                    last = eng.matmul(
                                        ps_score[:, j * 512:(j + 1) * 512],
                                        wu_sb[:, c * 2048 + ic * 128:c * 2048 + ic * 128 + 128],
                                        xch[:, c * 512:(c + 1) * 512],
                                        start=(c == 0), stop=(c == 7))
                            return last
                        p.ph("tensor", mm_u)

                        def ev_u(eng):
                            last = None
                            for j in range(4):
                                last = eng.tensor_copy(
                                    ubuf[:, j * 512:(j + 1) * 512],
                                    ps_score[:, j * 512:(j + 1) * 512])
                            return last
                        p.ph("vector", ev_u)

                        def mul_h(eng, ib=ib):
                            last = None
                            for j in range(4):
                                ic = ib * 4 + j
                                ap = h_fm[:, ic * 512:(ic + 1) * 512]
                                last = eng.tensor_tensor(
                                    ap, ap, ubuf[:, j * 512:(j + 1) * 512],
                                    ALU.mult)
                            return last
                        p.ph("vector", mul_h)

                    for ts in range(4):
                        for fg in range(2):
                            def mm_d(eng, ts=ts, fg=fg):
                                last = None
                                for ic in range(16):
                                    last = eng.matmul(
                                        bufs["ps_a"][:, 0:512],
                                        h_fm[:, ic * 512 + ts * 128:ic * 512 + ts * 128 + 128],
                                        wd_sb[:, ic * D + fg * 512:ic * D + fg * 512 + 512],
                                        start=(ic == 0), stop=(ic == 15))
                                return last
                            p.ph("tensor", mm_d)
                            par = dnct[0] % 2
                            dnct[0] += 1
                            mb = maccbs[par]
                            p.ph("vector", lambda eng, tc=tc, ts=ts, widx=widx,
                                 mb=mb:
                                 eng.tensor_scalar(mb[:],
                                                   bufs["ps_a"][:, 0:512],
                                                   wgt3[:, tc * 4 + ts,
                                                        widx:widx + 1],
                                                   None, ALU.mult),
                                 waits=[p.tail, macc_evs[par]])
                            tglob = t0 + ts * 128
                            macc_evs[par] = p.dma(
                                "gpsimd",
                                [(t["macc"][s, tglob:tglob + 128,
                                            fg * 512:(fg + 1) * 512], mb[:])],
                                accum=(si > 0), chain=False, waits=[p.tail])

    with ExitStack() as sc_f:
        moefm = p.sb("moefm", [128, 8 * 512], BF16, scope=sc_f)
        m32 = p.sb("m32", [128, D], F32, scope=sc_f)
        mbf = p.sb("mbf", [128, D], BF16, scope=sc_f)
        first_f = [True]
        for s in range(2):
            for tc in range(4):
                t0 = tc * 512
                for tsl in range(4):
                    tg = t0 + tsl * 128
                    if first_f[0]:
                        first_f[0] = False
                        p.dma("sync", [(m32[:], t["macc"][s, tg:tg + 128, :])],
                              waits=[p.tail] + [e for e in macc_evs if e])
                    else:
                        p.dma("sync", [(m32[:], t["macc"][s, tg:tg + 128, :])])
                    p.ph("vector", lambda eng: eng.tensor_copy(mbf[:], m32[:]))
                    for half in range(2):
                        def trm(eng, half=half):
                            last = None
                            for j in range(4):
                                c = half * 4 + j
                                last = eng.transpose(
                                    ps_tr[:, j * 128:(j + 1) * 128],
                                    mbf[:, c * 128:(c + 1) * 128], cc.ident[:])
                            return last
                        p.ph("tensor", trm)

                        def evm(eng, half=half, tsl=tsl):
                            last = None
                            for j in range(4):
                                c = half * 4 + j
                                last = eng.tensor_copy(
                                    moefm[:, c * 512 + tsl * 128:c * 512 + tsl * 128 + 128],
                                    ps_tr[:, j * 128:(j + 1) * 128])
                            return last
                        p.ph("vector", evm)
                for f in range(8):
                    p.dma("sync",
                          [(bufs["x32"][:, 0:512],
                            t["xa"][s, f * 128:(f + 1) * 128, t0:t0 + 512])])

                    def fin(eng, f=f):
                        return eng.tensor_tensor(bufs["xev"][:, 0:512],
                                                 moefm[:, f * 512:(f + 1) * 512],
                                                 bufs["x32"][:, 0:512], ALU.add)
                    p.ph("vector", fin)
                    ev = p.dma("sync",
                               [(t["y"][s, f * 128:(f + 1) * 128, t0:t0 + 512],
                                 bufs["xev"][:, 0:512])])

    return ev




def build_all(nc):
    p = Prog(nc)
    cc = Consts(p)
    W = 512
    bufs = {
        "x32": p.sb("x32", [128, W], F32),
        "xev": p.sb("xev", [128, W], F32),
        "xb": p.sb("xb", [128, W], BF16),
        "sq": p.sb("sq", [128, W], BF16),
        "tmp32": p.sb("tmp32", [128, W], F32),
        "tmpo": p.sb("tmpo", [128, 256], F32),
        "mrow": p.sb("mrow", [1, W], F32),
        "vrow": p.sb("vrow", [1, W], F32),
        "mrow_bf": p.sb("mrow_bf", [1, W], BF16),
        "vrow_bf": p.sb("vrow_bf", [1, W], BF16),
        "ps_a": p.ps("ps_a", [128, 512], F32),
        "ps_b": p.ps("ps_b", [128, 512], F32),
        "ps_misc": p.ps("ps_misc", [128, 512], F32),
    }
    bufs["ps_score"] = p.ps("ps_score", [128, 2048], F32)
    bufs["ps_tr"] = p.ps("ps_tr", [128, 512], BF16)

    x1h = nc.dram_tensor("x1h", [2, D, T], F32, kind="Internal").ap()
    xa1h = nc.dram_tensor("xa1h", [2, D, T], F32, kind="Internal").ap()
    x2h = nc.dram_tensor("x2h", [2, D, T], F32, kind="Internal").ap()
    x3h = nc.dram_tensor("x3o", [2, D, T], F32, kind="ExternalOutput").ap()
    xa3h = nc.dram_tensor("xa3o", [2, D, T], F32, kind="ExternalOutput").ap()

    plan = [
        (decls_layer0, emit_layer0, "l0_", {"y": x1h}),
        (decls_layer1_attn, emit_layer1_attn, "l1a_", {"x": x1h, "y": xa1h}),
        (decls_layer1_ffn, emit_layer1_ffn, "l1f_",
         {"x": x1h, "xa": xa1h, "y": x2h}),
        (decls_layer2, emit_layer2, "l2_", {"x": x2h, "y": x3h}),
        (decls_layer3_attn, emit_layer3_attn, "l3a_", {"x": x3h, "y": xa3h}),
    ]
    outer = p.stack
    ev = None
    evs = []
    for decls, emit, pref, given in plan:
        t = decls(nc, pref, given)
        with ExitStack() as lsc:
            p.stack = lsc
            ev = emit(p, cc, bufs, t)
            evs.append(ev)
        p.stack = outer
    p.ph("sync", lambda eng: eng.nop(), waits=evs[-2:])
    p.finalize()
    p.close()
    return nc


def run_all(inputs):
    w = inputs
    nc = bass.Bass("TRN2", target_bir_lowering=False)
    build_all(nc)

    base = {}
    base["l0_wqkv_t"] = np.ascontiguousarray(np.asarray(w["L0_attn_Wqkv"]).T)
    base["l0_wproj_t"] = np.ascontiguousarray(np.asarray(w["L0_attn_Wproj"]).T)
    base["l0_bproj_col"] = np.ascontiguousarray(
        np.asarray(w["L0_attn_bproj"]).reshape(8, 128).T)
    base["l0_ropes"] = _rope_tables_l0()
    base["l0_wg_t"] = np.ascontiguousarray(np.asarray(w["L0_ffn_w_gate"]).T)
    base["l0_wu_t"] = np.ascontiguousarray(np.asarray(w["L0_ffn_w_up"]).T)
    base["l0_wd_t"] = np.ascontiguousarray(np.asarray(w["L0_ffn_w_down"]).T)
    base["l1a_wqkv_t"] = np.ascontiguousarray(np.asarray(w["L1_attn_qkv_w"]).T)
    base["l1a_bqkv_row"] = np.ascontiguousarray(
        np.asarray(w["L1_attn_qkv_b"]).reshape(1, -1).astype(np.float32))
    base["l1a_wout_t"] = np.ascontiguousarray(np.asarray(w["L1_attn_out_w"]).T)
    base["l1a_bout_col"] = np.ascontiguousarray(
        np.asarray(w["L1_attn_out_b"]).reshape(8, 128).T)
    sgT = np.asarray(w["L1_ffn_shared_gate"])[0].T[None]
    suT = np.asarray(w["L1_ffn_shared_up"])[0].T[None]
    sdT = np.asarray(w["L1_ffn_shared_down"])[0].T[None]
    rgT = np.asarray(w["L1_ffn_routed_gate"]).transpose(0, 2, 1)
    ruT = np.asarray(w["L1_ffn_routed_up"]).transpose(0, 2, 1)
    rdT = np.asarray(w["L1_ffn_routed_down"]).transpose(0, 2, 1)
    base["l1f_ln1_row"] = np.asarray(w["L1_ffn_ln1_w"]).reshape(1, -1).astype(np.float32)
    base["l1f_ln2_row"] = np.asarray(w["L1_ffn_ln2_w"]).reshape(1, -1).astype(np.float32)
    base["l1f_kvd_t"] = np.ascontiguousarray(np.asarray(w["L1_ffn_kv_d"]).T)
    base["l1f_qd_t"] = np.ascontiguousarray(np.asarray(w["L1_ffn_q_d"]).T)
    base["l1f_ku_t"] = np.ascontiguousarray(np.asarray(w["L1_ffn_k_u"]).T)
    base["l1f_qu_t"] = np.ascontiguousarray(np.asarray(w["L1_ffn_q_u"]).T)
    base["l1f_vu_t"] = np.ascontiguousarray(np.asarray(w["L1_ffn_v_u"]).T)
    base["l1f_rq_t"] = np.ascontiguousarray(np.asarray(w["L1_ffn_rope_q_w"]).T)
    base["l1f_rk_t"] = np.ascontiguousarray(np.asarray(w["L1_ffn_rope_k_w"]).T)
    base["l1f_ow_t"] = np.ascontiguousarray(np.asarray(w["L1_ffn_o_w"]).T)
    base["l1f_rw_t"] = np.ascontiguousarray(np.asarray(w["L1_ffn_router_w"]).T)
    base["l1f_rb_row"] = np.asarray(w["L1_ffn_routing_bias"]).reshape(1, -1).astype(np.float32)
    base["l1f_rg_all"] = np.ascontiguousarray(np.concatenate([sgT, rgT], axis=0))
    base["l1f_ru_all"] = np.ascontiguousarray(np.concatenate([suT, ruT], axis=0))
    base["l1f_rd_all"] = np.ascontiguousarray(np.concatenate([sdT, rdT], axis=0))
    base["l1f_rope2"] = _rope_tables_l1()
    base["l2_wv_t"] = np.ascontiguousarray(np.asarray(w["L2_attn_Wv"]).T)
    base["l2_bv_row"] = np.asarray(w["L2_attn_bv"]).reshape(1, -1).astype(np.float32)
    base["l2_wo_t"] = np.ascontiguousarray(np.asarray(w["L2_attn_Wo"]).T)
    base["l2_bo_col"] = np.ascontiguousarray(
        np.asarray(w["L2_attn_bo"]).reshape(8, 128).T)
    base["l2_wr_t"] = np.ascontiguousarray(np.asarray(w["L2_ffn_w_router"]).T)
    base["l2_cent"] = np.ascontiguousarray(np.asarray(w["L2_ffn_head_centroids"]))
    base["l2_temp"] = np.asarray(w["L2_ffn_temperature"]).reshape(1, 1).astype(np.float32)
    base["l2_ie"] = np.ascontiguousarray(np.asarray(w["L2_ffn_input_experts"]))
    base["l2_oe"] = np.ascontiguousarray(np.asarray(w["L2_ffn_output_experts"]))
    wbg = np.concatenate([np.asarray(w["L3_attn_Wb"]), np.asarray(w["L3_attn_Wg"])],
                         axis=0)
    bbg = np.concatenate([np.asarray(w["L3_attn_bb"]), np.asarray(w["L3_attn_bg"])])
    base["l3a_wq_t"] = np.ascontiguousarray(np.asarray(w["L3_attn_Wq"]).T)
    base["l3a_wk_t"] = np.ascontiguousarray(np.asarray(w["L3_attn_Wk"]).T)
    base["l3a_wv_t"] = np.ascontiguousarray(np.asarray(w["L3_attn_Wv"]).T)
    base["l3a_wa_t"] = np.ascontiguousarray(np.asarray(w["L3_attn_Wa"]).T)
    base["l3a_war_t"] = np.ascontiguousarray(np.asarray(w["L3_attn_WaR"]).T)
    base["l3a_wo_t"] = np.ascontiguousarray(np.asarray(w["L3_attn_Wo"]).T)
    base["l3a_wbg_t"] = np.ascontiguousarray(wbg.T)
    base["l3a_ba_row"] = np.asarray(w["L3_attn_ba"]).reshape(1, -1).astype(np.float32)
    base["l3a_bar_row"] = np.asarray(w["L3_attn_baR"]).reshape(1, -1).astype(np.float32)
    base["l3a_bbg_row"] = np.ascontiguousarray(bbg.reshape(1, -1).astype(np.float32))
    base["l3a_lnw_row"] = np.asarray(w["L3_attn_ln_w"]).reshape(1, -1).astype(np.float32)
    base["l3a_lnb_col"] = np.ascontiguousarray(
        np.asarray(w["L3_attn_ln_b"]).reshape(8, 128).T)
    base["l3f_rw_t"] = np.ascontiguousarray(np.asarray(w["L3_ffn_router_w"]).T)
    base["l3f_sg"] = np.ascontiguousarray(np.asarray(w["L3_ffn_sg"]))
    base["l3f_su"] = np.ascontiguousarray(np.asarray(w["L3_ffn_su"]))
    base["l3f_sd"] = np.ascontiguousarray(np.asarray(w["L3_ffn_sd"]))
    base["l3f_rg"] = np.ascontiguousarray(np.asarray(w["L3_ffn_rg"]))
    base["l3f_ru"] = np.ascontiguousarray(np.asarray(w["L3_ffn_ru"]))
    base["l3f_rd"] = np.ascontiguousarray(np.asarray(w["L3_ffn_rd"]))

    x0 = np.asarray(w["L0_attn_x"], dtype=np.float32)
    in_maps = []
    for c in range(NCORES):
        m = dict(base)
        m["l0_x"] = np.ascontiguousarray(
            x0[[c, 15 - c]].transpose(0, 2, 1).astype(np.float32))
        pm = np.zeros((2, 16, 8), dtype=np.float32)
        for sl, gown in ((0, c), (1, 15 - c)):
            for gp in range(16):
                r, slp = gp // 2, gp % 2
                seq = r if slp == 0 else 15 - r
                if seq < gown:
                    pm[sl, gp] = 1.0
        m["l3f_prefrep"] = np.ascontiguousarray(pm.reshape(1, 256))
        in_maps.append(m)
    res = run_bass_kernel_spmd(nc, in_maps, core_ids=list(range(NCORES)))
    x3 = [np.asarray(res.results[c]["x3o"]) for c in range(NCORES)]
    xa3 = [np.asarray(res.results[c]["xa3o"]) for c in range(NCORES)]
    x4 = run_layer3_ffn(x3, xa3, inputs)
    return fm8_gather(x4)
